# version 02
# baseline (speedup 1.0000x reference)
import numpy as np
import ml_dtypes
from contextlib import ExitStack
from collections import defaultdict

import concourse.bass as bass
import concourse.mybir as mybir
from concourse.bass_utils import run_bass_kernel_spmd

F32 = mybir.dt.float32
BF16 = mybir.dt.bfloat16
U8 = mybir.dt.uint8
AF = mybir.ActivationFunctionType
ALU = mybir.AluOpType
AX = mybir.AxisListType
BF = ml_dtypes.bfloat16

D = 1024
S = 2048
NT = S // 128
NCH = D // 128
NEG = -1e30

SLAB = 212736
PERSIST = 65792



class Ctx:

    def __init__(self, nc):
        self.nc = nc
        self.plan = []
        self.sems = {}
        self.cnt = defaultdict(int)

    def make_sems(self, stack, names):
        for n in names:
            self.sems[n] = stack.enter_context(self.nc.semaphore(n))

    def op(self, eng, emit, inc=None, n=1):
        if inc is None:
            self.plan.append((eng, emit))
            return None
        self.cnt[inc] += n
        s = self.sems[inc]
        self.plan.append((eng, lambda E, emit=emit, s=s, n=n: emit(E).then_inc(s, n)))
        return self.cnt[inc]

    def wait(self, eng, sem, val=None):
        v = self.cnt[sem] if val is None else val
        if v <= 0:
            return
        s = self.sems[sem]
        self.plan.append((eng, lambda E, s=s, v=v: E.wait_ge(s, v)))

    def dma(self, eng, sem, out, in_, **kw):
        return self.op(eng, lambda E: E.dma_start(out=out, in_=in_, **kw), inc=sem, n=16)

    def store(self, out, in_):
        return self.op("sp", lambda E: E.dma_start(out=out, in_=in_), inc="st", n=16)

    def barrier(self):
        for eng in ("sp", "pool", "pe", "dve", "act"):
            for s in list(self.cnt):
                self.wait(eng, s, self.cnt[s])

    def replay(self, block):
        plan = self.plan

        @block.sync
        def _(E):
            for g, fn in plan:
                if g == "sp":
                    fn(E)

        @block.gpsimd
        def _(E):
            for g, fn in plan:
                if g == "pool":
                    fn(E)

        @block.tensor
        def _(E):
            for g, fn in plan:
                if g == "pe":
                    fn(E)

        @block.vector
        def _(E):
            for g, fn in plan:
                if g == "dve":
                    fn(E)

        @block.scalar
        def _(E):
            for g, fn in plan:
                if g == "act":
                    fn(E)


class Carver:

    def __init__(self, nc, base, lo, hi, prefix):
        self.nc = nc
        self.base = base
        self.p = lo
        self.hi = hi
        self.prefix = prefix

    def t(self, name, shape, dtype):
        per = mybir.dt.size(dtype)
        for s in shape[1:]:
            per *= s
        addr = (self.p + 31) & ~31
        self.p = addr + per
        assert self.p <= self.hi, (self.prefix, name, self.p, self.hi)
        return self.nc.alloc_sbuf_tensor_at(self.prefix + name, shape, dtype,
                                            offset=self.base + addr)


def emit_ln_stats(ctx, bn6, sc1, src_ap, mcol, vcol, rscol, eps_ap):
    ctx.op("dve", lambda E: E.bn_stats(bn6[:, 0:6], src_ap[:, 0:512]))
    ctx.op("dve", lambda E: E.bn_stats(bn6[:, 6:12], src_ap[:, 512:1024]), inc="dve")
    ctx.wait("dve", "dve")
    c_mv = ctx.op("dve", lambda E: E.bn_aggr(mcol, bn6[:, 0:12]), inc="dve")
    ctx.wait("act", "dve", c_mv)
    ctx.op("act", lambda E: E.activation(sc1, vcol, AF.Ln, bias=eps_ap), inc="act")
    ctx.wait("act", "act")
    c_rs = ctx.op("act", lambda E: E.activation(rscol, sc1, AF.Exp, scale=-0.5),
                  inc="act")
    return c_mv, c_rs


def emit_load_ln_xp(ctx, nc, B, eps_val, load_x, xin=None):
    Xv, XPv = B["Xv"], B["XPv"]
    if load_x:
        c_x = ctx.dma("sp", "ld_x", Xv, xin.rearrange("(i p) d -> p i d", p=128))
    ctx.op("dve", lambda E: E.memset(B["EPS"][:, :], eps_val), inc="dve")
    c_eps = ctx.cnt["dve"]
    if load_x:
        ctx.wait("dve", "ld_x", c_x)
        ctx.wait("pool", "ld_x", c_x)
    ctx.wait("act", "dve", c_eps)
    pe_done = {}
    c_pt_evac = 0
    for i in range(NT):
        Xi = B["X"][:, i * D:(i + 1) * D]
        c_mv, c_rs = emit_ln_stats(ctx, B["BN6"], B["SC1"][:, :], Xi,
                                   B["MV"][:, 2 * i:2 * i + 2],
                                   B["MV"][:, 2 * i + 1:2 * i + 2],
                                   B["RS"][:, i:i + 1], B["EPS"][:, :])
        xt = B["XT"][:, (i % 2) * D:(i % 2 + 1) * D]
        if i >= 2:
            ctx.wait("pool", "pe", pe_done[i - 2])
        ctx.wait("pool", "dve", c_mv)
        ctx.wait("pool", "act", c_rs)
        c_ap = ctx.op("pool", lambda E, xt=xt, Xi=Xi, i=i:
                      E.tensor_scalar(xt, Xi, B["MV"][:, 2 * i:2 * i + 1],
                                      B["RS"][:, i:i + 1], ALU.subtract, ALU.mult),
                      inc="pool")
        ctx.wait("pe", "pool", c_ap)
        ctx.wait("pe", "ld_c", ctx.cnt["ld_c"])
        ctx.wait("pe", "dve", c_pt_evac)
        for c in range(NCH):
            r = ctx.op("pe", lambda E, c=c, xt=xt:
                       E.transpose(B["pt_ps"][:, c * 128:(c + 1) * 128],
                                   xt[:, c * 128:(c + 1) * 128], B["IDN"][:, :]),
                       inc="pe" if c == NCH - 1 else None)
        pe_done[i] = r
        ctx.wait("dve", "pe", pe_done[i])
        c_pt_evac = ctx.op(
            "dve", lambda E, i=i: E.tensor_copy(
                XPv[:, :, 128 * i:128 * (i + 1)],
                B["pt_ps"][:, :].rearrange("p (c t) -> p c t", t=128)),
            inc="dve")
    return c_pt_evac




def emit_attn0(nc, ctx, P, C):
    xin = P["x"]
    wq_d, wk_d, wv_d = P["a0_wq"], P["a0_wk"], P["a0_wv"]
    wo_d, gb_d, bb_d, mask_d = P["a0_wo"], P["a0_gb"], P["a0_bb"], P["a0_mask"]
    X, IDN = P["X"], P["IDN"]

    XP = C.t("XP", [128, NCH * S], BF16)
    OF = C.t("OF", [128, NCH * S], BF16)
    WO = C.t("WO", [128, NCH * D], BF16)
    WQ = C.t("WQ", [128, 2 * NCH * 128], BF16)
    WK = C.t("WK", [128, 2 * NCH * 128], BF16)
    WV = C.t("WV", [128, 2 * NCH * 128], BF16)
    QP = C.t("QP", [128, S], BF16)
    KP = C.t("KP", [128, S], BF16)
    VP = C.t("VP", [128, S], BF16)
    PT = C.t("PT", [128, S], BF16)
    PTT = C.t("PTT", [128, S], BF16)
    XT = C.t("XT", [128, 2 * D], BF16)
    TMP = C.t("TMP", [128, D], F32)
    Y = C.t("Y", [128, D], F32)
    YN = C.t("YN", [128, D], F32)
    GB = C.t("GB", [128, D], F32)
    BBt = C.t("BBt", [128, D], F32)
    MASK = C.t("MASK", [128, 4 * 512], BF16)
    MV = C.t("MV", [128, 2 * NT], F32)
    RS = C.t("RS", [128, NT], F32)
    MV2 = C.t("MV2", [128, 2], F32)
    RS2 = C.t("RS2", [128, 1], F32)
    BN6 = C.t("BN6", [128, 12], F32)
    SC1 = C.t("SC1", [128, 1], F32)
    NGM = C.t("NGM", [128, 1], F32)
    SM = C.t("SM", [128, 1], F32)
    REC = C.t("REC", [128, 1], F32)
    EPS5 = C.t("EPS5", [128, 1], F32)
    EPS3 = C.t("EPS3", [128, 1], F32)

    proj_ps = P["pair_ps"]
    sc_ps = P["big_ps"]
    pt_ps = P["pt_ps"]
    pv_ps = P["one_ps"]

    Xv = X[:, :].rearrange("p (i d) -> p i d", d=D)
    XPv = XP[:, :].rearrange("p (c t) -> p c t", t=S)
    OFv = OF[:, :].rearrange("p (c t) -> p c t", t=S)
    WOv = WO[:, :].rearrange("p (c d) -> p c d", d=D)

    c_x = ctx.dma("sp", "ld_x", Xv, xin.rearrange("(i p) d -> p i d", p=128))
    ctx.dma("sp", "ld_c", MASK[:, :], mask_d)
    ctx.dma("sp", "ld_c", GB[:, :], gb_d)
    c_const = ctx.dma("sp", "ld_c", BBt[:, :], bb_d)
    c_wo = ctx.dma("pool", "ld_wo", WOv, wo_d.rearrange("(c p) d -> p c d", p=128))
    cw = {}
    for nmd, dst, src in (("q", WQ, wq_d), ("k", WK, wk_d), ("v", WV, wv_d)):
        cw[(0, nmd)] = ctx.dma(
            "pool", "lw0", dst[:, 0:1024].rearrange("p (c n) -> p c n", n=128),
            src[0].rearrange("(c p) n -> p c n", p=128))
    ctx.op("dve", lambda E: E.memset(EPS5[:, :], 1e-5), inc="dve")
    c_eps = ctx.op("dve", lambda E: E.memset(EPS3[:, :], 1e-3), inc="dve")

    ctx.wait("dve", "ld_x", c_x)
    ctx.wait("dve", "ld_c", c_const)
    ctx.wait("pool", "ld_x", c_x)
    ctx.wait("pool", "ld_c", c_const)
    ctx.wait("act", "dve", c_eps)
    pe_tile_done = {}
    c_pt_evac = 0
    for i in range(NT):
        Xi = X[:, i * D:(i + 1) * D]
        c_mv, c_rs = emit_ln_stats(ctx, BN6, SC1[:, :], Xi,
                                   MV[:, 2 * i:2 * i + 2],
                                   MV[:, 2 * i + 1:2 * i + 2],
                                   RS[:, i:i + 1], EPS5[:, :])
        xt = XT[:, (i % 2) * D:(i % 2 + 1) * D]
        if i >= 2:
            ctx.wait("pool", "pe", pe_tile_done[i - 2])
        ctx.wait("pool", "dve", c_mv)
        ctx.wait("pool", "act", c_rs)
        c_ap = ctx.op("pool", lambda E, xt=xt, Xi=Xi, i=i:
                      E.tensor_scalar(xt, Xi, MV[:, 2 * i:2 * i + 1],
                                      RS[:, i:i + 1], ALU.subtract, ALU.mult),
                      inc="pool")
        ctx.wait("pe", "pool", c_ap)
        ctx.wait("pe", "ld_c", c_const)
        ctx.wait("pe", "dve", c_pt_evac)
        for c in range(NCH):
            r = ctx.op("pe", lambda E, c=c, xt=xt:
                       E.transpose(pt_ps[:, c * 128:(c + 1) * 128],
                                   xt[:, c * 128:(c + 1) * 128], IDN[:, :]),
                       inc="pe" if c == NCH - 1 else None)
        pe_tile_done[i] = r
        ctx.wait("dve", "pe", pe_tile_done[i])
        c_pt_evac = ctx.op(
            "dve", lambda E, i=i: E.tensor_copy(
                XPv[:, :, 128 * i:128 * (i + 1)],
                pt_ps[:, :].rearrange("p (c t) -> p c t", t=128)),
            inc="dve")

    pair_pe_done = {}
    for p in range(8):
        if p + 1 < 8:
            if p >= 1:
                ctx.wait("pool", "pe", pair_pe_done[p - 1])
            b = (p + 1) % 2
            for nmd, dstT, srcT in (("q", WQ, wq_d), ("k", WK, wk_d),
                                    ("v", WV, wv_d)):
                cw[(p + 1, nmd)] = ctx.dma(
                    "pool", f"lw{b}",
                    dstT[:, 1024 * b:1024 * (b + 1)]
                    .rearrange("p (c n) -> p c n", n=128),
                    srcT[p + 1].rearrange("(c p) n -> p c n", p=128))
        ctx.wait("pe", f"lw{p % 2}", cw[(p, "v")])
        ctx.wait("pe", "dve", c_pt_evac)
        bofs = (p % 2) * 1024
        for which, wsb, scale in (("q", WQ, 0.125), ("k", WK, None)):
            eng = "act" if which == "q" else "dve"
            for tb in range(4):
                ps = proj_ps[:, (tb % 2) * 512:(tb % 2) * 512 + 512]
                ctx.wait("pe", "act", ctx.cnt["act"])
                ctx.wait("pe", "dve", ctx.cnt["dve"])
                for c in range(NCH):
                    cg = ctx.op("pe", lambda E, c=c, tb=tb, wsb=wsb, ps=ps, bofs=bofs:
                                E.matmul(ps, wsb[:, bofs + c * 128:bofs + (c + 1) * 128],
                                         XPv[:, c, 512 * tb:512 * (tb + 1)],
                                         start=(c == 0), stop=(c == NCH - 1)),
                                inc="pe" if c == NCH - 1 else None)
                ctx.wait(eng, "pe", cg)
                if which == "q":
                    ctx.op("act", lambda E, tb=tb, ps=ps:
                           E.activation(QP[:, 512 * tb:512 * (tb + 1)], ps,
                                        AF.Copy, scale=0.125), inc="act")
                else:
                    ctx.op("dve", lambda E, tb=tb, ps=ps:
                           E.tensor_copy(KP[:, 512 * tb:512 * (tb + 1)], ps),
                           inc="dve")
        for tt in range(NT):
            ps = proj_ps[:, (tt % 2) * 512:(tt % 2) * 512 + 128]
            ctx.wait("pe", "act", ctx.cnt["act"])
            ctx.wait("pe", "dve", ctx.cnt["dve"])
            for c in range(NCH):
                cg = ctx.op("pe", lambda E, c=c, tt=tt, ps=ps, bofs=bofs:
                            E.matmul(ps, XPv[:, c, 128 * tt:128 * (tt + 1)],
                                     WV[:, bofs + c * 128:bofs + (c + 1) * 128],
                                     start=(c == 0), stop=(c == NCH - 1)),
                            inc="pe" if c == NCH - 1 else None)
            ctx.wait("dve", "pe", cg)
            ctx.op("dve", lambda E, tt=tt, ps=ps:
                   E.tensor_copy(VP[:, 128 * tt:128 * (tt + 1)], ps), inc="dve")
        c_qkv_dve = ctx.cnt["dve"]
        c_qkv_act = ctx.cnt["act"]

        for par in range(2):
            q_h = QP[64 * par:64 * (par + 1), :]
            k_h = KP[64 * par:64 * (par + 1), :]
            for i in range(NT):
                jmax = i // 4
                span = 512 * (jmax + 1)
                ctx.wait("pe", "act", ctx.cnt["act"])
                if par == 0 and i == 0:
                    ctx.wait("pe", "dve", c_qkv_dve)
                    ctx.wait("pe", "act", c_qkv_act)
                for j in range(jmax + 1):
                    c_sc = ctx.op("pe", lambda E, i=i, j=j, q_h=q_h, k_h=k_h:
                                  E.matmul(sc_ps[:, 512 * j:512 * (j + 1)],
                                           q_h[:, 128 * i:128 * (i + 1)],
                                           k_h[:, 512 * j:512 * (j + 1)],
                                           start=True, stop=True),
                                  inc="pe" if j == jmax else None)
                ctx.wait("dve", "pe", c_sc)
                ctx.op("dve", lambda E, i=i, jmax=jmax:
                       E.tensor_tensor(sc_ps[:, 512 * jmax:512 * (jmax + 1)],
                                       sc_ps[:, 512 * jmax:512 * (jmax + 1)],
                                       MASK[:, 512 * (i % 4):512 * (i % 4 + 1)],
                                       ALU.add), inc="dve")
                ctx.wait("dve", "dve")
                c_ngm = ctx.op("dve", lambda E, span=span:
                               E.reduce_max(NGM[:, :], sc_ps[:, 0:span],
                                            axis=AX.X, negate=True), inc="dve")
                ctx.wait("act", "dve", c_ngm)
                c_exp = ctx.op("act", lambda E, span=span:
                               E.activation(PT[:, 0:span], sc_ps[:, 0:span],
                                            AF.Exp, bias=NGM[:, :],
                                            accum_out=SM[:, :]), inc="act")
                ctx.wait("dve", "act", c_exp)
                c_rec = ctx.op("dve", lambda E: E.reciprocal(REC[:, :], SM[:, :]),
                               inc="dve")
                ctx.wait("dve", "dve", c_rec)
                c_nrm = ctx.op("dve", lambda E, span=span:
                               E.tensor_scalar(PT[:, 0:span], PT[:, 0:span],
                                               REC[:, :], None, ALU.mult),
                               inc="dve")
                nblk = i + 1
                for w in range((nblk + 7) // 8):
                    lo, hi = 8 * w, min(8 * w + 8, nblk)
                    ctx.wait("pe", "dve", c_nrm if w == 0 else c_ptt)
                    for c in range(lo, hi):
                        c_tp = ctx.op("pe", lambda E, c=c, lo=lo:
                                      E.transpose(pt_ps[:, 128 * (c - lo):128 * (c - lo + 1)],
                                                  PT[:, 128 * c:128 * (c + 1)],
                                                  IDN[:, :]),
                                      inc="pe" if c == hi - 1 else None)
                    ctx.wait("dve", "pe", c_tp)
                    c_ptt = ctx.op("dve", lambda E, lo=lo, hi=hi: E.tensor_copy(
                        PTT[:, 128 * lo:128 * hi],
                        pt_ps[:, 0:128 * (hi - lo)]), inc="dve")
                eng = "act" if (par + i) % 2 else "dve"
                ctx.wait("pe", "dve", c_ptt)
                ctx.wait("pe", eng, ctx.cnt[eng])
                for c in range(nblk):
                    c_pv = ctx.op("pe", lambda E, c=c, par=par, i=i:
                                  E.matmul(pv_ps[0:64, 0:128],
                                           VP[:, 128 * c + 64 * par:128 * c + 64 * par + 64],
                                           PTT[:, 128 * c:128 * (c + 1)],
                                           start=(c == 0), stop=(c == i)),
                                  inc="pe" if c == i else None)
                ctx.wait(eng, "pe", c_pv)
                dst = OF[64 * par:64 * (par + 1),
                         S * p + 128 * i:S * p + 128 * (i + 1)]
                if eng == "act":
                    ctx.op("act", lambda E, dst=dst:
                           E.copy(dst, pv_ps[0:64, 0:128]), inc="act")
                else:
                    ctx.op("dve", lambda E, dst=dst:
                           E.tensor_copy(dst, pv_ps[0:64, 0:128]), inc="dve")
        pair_pe_done[p] = ctx.cnt["pe"]
    c_of_dve = ctx.cnt["dve"]
    c_of_act = ctx.cnt["act"]

    ctx.wait("pe", "ld_wo", c_wo)
    ctx.wait("pe", "dve", c_of_dve)
    ctx.wait("pe", "act", c_of_act)
    c_y = 0
    c_xu = 0
    for i in range(NT):
        Xi = X[:, i * D:(i + 1) * D]
        ctx.wait("pe", "dve", c_y)
        for hh in range(2):
            for c in range(NCH):
                c_d = ctx.op("pe", lambda E, c=c, hh=hh, i=i:
                             E.matmul(proj_ps[:, 512 * hh:512 * (hh + 1)],
                                      OFv[:, c, 128 * i:128 * (i + 1)],
                                      WOv[:, c, 512 * hh:512 * (hh + 1)],
                                      start=(c == 0), stop=(c == NCH - 1)),
                             inc="pe" if (hh == 1 and c == NCH - 1) else None)
        ctx.wait("pool", "dve", c_y)
        c_tmp = ctx.op("pool", lambda E, Xi=Xi, i=i:
                       E.tensor_scalar(TMP[:, :], Xi, MV[:, 2 * i:2 * i + 1],
                                       RS[:, i:i + 1], ALU.subtract, ALU.mult),
                       inc="pool")
        ctx.wait("dve", "pe", c_d)
        ctx.wait("dve", "pool", c_tmp)
        c_y = ctx.op("dve", lambda E: E.tensor_add(Y[:, :], TMP[:, :],
                                                   proj_ps[:, 0:1024]), inc="dve")
        ctx.wait("dve", "dve")
        ctx.op("dve", lambda E: E.bn_stats(BN6[:, 0:6], Y[:, 0:512]))
        ctx.op("dve", lambda E: E.bn_stats(BN6[:, 6:12], Y[:, 512:1024]),
               inc="dve")
        ctx.wait("dve", "dve")
        c_mv2 = ctx.op("dve", lambda E: E.bn_aggr(MV2[:, :], BN6[:, 0:12]),
                       inc="dve")
        ctx.wait("act", "dve", c_mv2)
        ctx.op("act", lambda E: E.activation(SC1[:, :], MV2[:, 1:2],
                                             AF.Ln, bias=EPS3[:, :]), inc="act")
        ctx.wait("act", "act")
        c_rs2 = ctx.op("act", lambda E: E.activation(RS2[:, :], SC1[:, :],
                                                     AF.Exp, scale=-0.5),
                       inc="act")
        ctx.wait("dve", "act", c_rs2)
        ctx.wait("dve", "pool", c_xu)
        ctx.op("dve", lambda E: E.tensor_scalar(YN[:, :], Y[:, :], MV2[:, 0:1],
                                                RS2[:, :], ALU.subtract,
                                                ALU.mult), inc="dve")
        ctx.wait("pool", "dve", ctx.cnt["dve"])
        ctx.op("pool", lambda E: E.tensor_mul(YN[:, :], YN[:, :], GB[:, :]),
               inc="pool")
        ctx.wait("pool", "pool")
        ctx.op("pool", lambda E: E.tensor_add(YN[:, :], YN[:, :], BBt[:, :]),
               inc="pool")
        ctx.wait("pool", "pool")
        c_xu = ctx.op("pool", lambda E, Xi=Xi: E.tensor_add(Xi, Xi, YN[:, :]),
                      inc="pool")




def emit_ffn0(nc, ctx, P, C):
    rw_d, gup_d, dwn_d = P["f0_rw"], P["f0_gup"], P["f0_dwn"]
    shgu_d, shdw_d = P["f0_shgu"], P["f0_shdw"]
    X, IDN = P["X"], P["IDN"]

    XP = C.t("XP", [128, NCH * S], BF16)
    GU = C.t("GU", [128, NCH * 2048], BF16)
    DW = C.t("DW", [128, 2 * NCH * D], BF16)
    Hb = C.t("Hb", [128, 8 * 512], BF16)
    SG = C.t("SG", [128, 512], BF16)
    XT = C.t("XT", [128, 2 * D], BF16)
    RW = C.t("RW", [128, NCH * 16], BF16)
    LG = C.t("LG", [128, NT * 16], F32)
    WF = C.t("WF", [128, NT * 16], F32)
    M1 = C.t("M1", [128, 1], F32)
    M2 = C.t("M2", [128, 1], F32)
    MSK1 = C.t("MSK1", [128, 16], F32)
    LG2 = C.t("LG2", [128, 16], F32)
    GTS = C.t("GTS", [128, 16], F32)
    MV = C.t("MV", [128, 2 * NT], F32)
    RS = C.t("RS", [128, NT], F32)
    BN6 = C.t("BN6", [128, 12], F32)
    SC1 = C.t("SC1", [128, 1], F32)
    EPS = C.t("EPS", [128, 1], F32)

    pt_ps = P["pt_ps"]
    gps = P["pair_ps"][:, 0:512]
    ups = P["pair_ps"][:, 512:1024]
    d_ps = P["big_ps"]
    m_ps = P["one_ps"]

    B = {"X": X, "XP": XP, "XT": XT, "pt_ps": pt_ps, "BN6": BN6,
         "SC1": SC1, "MV": MV, "RS": RS, "EPS": EPS, "IDN": IDN,
         "Xv": X[:, :].rearrange("p (i d) -> p i d", d=D),
         "XPv": XP[:, :].rearrange("p (c t) -> p c t", t=S)}
    XPv = B["XPv"]
    GUv = GU[:, :].rearrange("p (c h) -> p c h", h=2048)

    ctx.dma("pool", "ld_rw", RW[:, :].rearrange("p (c n) -> p c n", n=16),
            rw_d.rearrange("(c p) n -> p c n", p=128))

    def gu_src(ei):
        return shgu_d[ei] if ei < 2 else gup_d[ei - 2]

    def dw_src(ei):
        return shdw_d[ei] if ei < 2 else dwn_d[ei - 2]

    c_gu = {0: ctx.dma("pool", "lgu",
                       GUv, gu_src(0).rearrange("(c p) h -> p c h", p=128))}
    c_dw = {0: ctx.dma("pool", "ldw0",
                       DW[:, 0:NCH * D].rearrange("p (c h) -> p c h", h=D),
                       dw_src(0).rearrange("(c p) h -> p c h", h=D, p=128))}

    c_xp = emit_load_ln_xp(ctx, nc, B, 1e-5, load_x=False)

    ctx.wait("pe", "ld_c", ctx.cnt["ld_c"])
    ctx.wait("pe", "ld_rw", ctx.cnt["ld_rw"])
    for i in range(NT):
        ctx.wait("pe", "dve", ctx.cnt["dve"])
        for c in range(NCH):
            r = ctx.op("pe", lambda E, c=c, i=i:
                       E.matmul(m_ps[:, 0:16], XPv[:, c, 128 * i:128 * (i + 1)],
                                RW[:, 16 * c:16 * (c + 1)],
                                start=(c == 0), stop=(c == NCH - 1)),
                       inc="pe" if c == NCH - 1 else None)
        ctx.wait("dve", "pe", r)
        ctx.op("dve", lambda E, i=i:
               E.tensor_copy(LG[:, 16 * i:16 * (i + 1)], m_ps[:, 0:16]),
               inc="dve")
    ctx.wait("act", "dve", ctx.cnt["dve"])
    c_wf = 0
    for i in range(NT):
        lg = LG[:, 16 * i:16 * (i + 1)]
        ctx.wait("act", "dve", c_wf)
        ctx.wait("dve", "dve")
        c_m1 = ctx.op("dve", lambda E, lg=lg:
                      E.reduce_max(M1[:, :], lg, axis=AX.X), inc="dve")
        ctx.wait("dve", "dve", c_m1)
        ctx.op("dve", lambda E, lg=lg:
               E.tensor_scalar(MSK1[:, :], lg, M1[:, :], None, ALU.is_ge),
               inc="dve")
        ctx.wait("dve", "dve")
        ctx.op("dve", lambda E, lg=lg:
               E.scalar_tensor_tensor(LG2[:, :], MSK1[:, :], NEG, lg,
                                      ALU.mult, ALU.add), inc="dve")
        ctx.wait("dve", "dve")
        c_m2 = ctx.op("dve", lambda E:
                      E.reduce_max(M2[:, :], LG2[:, :], axis=AX.X), inc="dve")
        ctx.wait("dve", "dve", c_m2)
        c_msk = ctx.op("dve", lambda E, lg=lg:
                       E.tensor_scalar(MSK1[:, :], lg, M2[:, :], None,
                                       ALU.is_ge), inc="dve")
        c_sig = ctx.op("act", lambda E, lg=lg:
                       E.activation(GTS[:, :], lg, AF.Sigmoid), inc="act")
        ctx.wait("dve", "act", c_sig)
        ctx.wait("dve", "dve", c_msk)
        c_wf = ctx.op("dve", lambda E, i=i:
                      E.tensor_mul(WF[:, 16 * i:16 * (i + 1)], MSK1[:, :],
                                   GTS[:, :]), inc="dve")
    c_wf_all = c_wf

    c_dph = 0
    for ei in range(18):
        real = ei >= 2
        dwbuf = ei % 2
        ctx.wait("pe", "lgu", c_gu[ei])
        ctx.wait("pe", f"ldw{dwbuf}", c_dw[ei])
        ctx.wait("pe", "dve", c_xp)
        dwv = DW[:, dwbuf * NCH * D:(dwbuf + 1) * NCH * D] \
            .rearrange("p (c h) -> p c h", h=D)
        for tq in range(4):
            for f in range(NCH):
                ctx.wait("pe", "act", ctx.cnt["act"])
                ctx.wait("pe", "dve", ctx.cnt["dve"])
                for c in range(NCH):
                    ctx.op("pe", lambda E, c=c, f=f, tq=tq:
                           E.matmul(gps, GUv[:, c, 128 * f:128 * (f + 1)],
                                    XPv[:, c, 512 * tq:512 * (tq + 1)],
                                    start=(c == 0), stop=(c == NCH - 1)))
                for c in range(NCH):
                    r_u = ctx.op("pe", lambda E, c=c, f=f, tq=tq:
                                 E.matmul(ups,
                                          GUv[:, c, 1024 + 128 * f:1024 + 128 * (f + 1)],
                                          XPv[:, c, 512 * tq:512 * (tq + 1)],
                                          start=(c == 0), stop=(c == NCH - 1)),
                                 inc="pe" if c == NCH - 1 else None)
                ctx.wait("act", "pe", r_u)
                c_sg = ctx.op("act", lambda E:
                              E.activation(SG[:, :], gps, AF.Silu),
                              inc="act")
                ctx.wait("dve", "pe", r_u)
                ctx.wait("dve", "act", c_sg)
                c_h = ctx.op("dve", lambda E, f=f:
                             E.tensor_mul(Hb[:, 512 * f:512 * (f + 1)],
                                          ups, SG[:, :]), inc="dve")
            if tq == 3 and ei + 1 < 18:
                ctx.wait("pool", "pe", ctx.cnt["pe"])
                c_gu[ei + 1] = ctx.dma(
                    "pool", "lgu", GUv,
                    gu_src(ei + 1).rearrange("(c p) h -> p c h", p=128))
            ctx.wait("pe", "dve", c_h)
            if real:
                ctx.wait("dve", "dve", c_wf_all)
            for ts in range(4):
                it = 4 * tq + ts
                dps = d_ps[:, (ts % 2) * 1024:(ts % 2) * 1024 + 1024]
                ctx.wait("pe", "dve", c_dph)
                for dh in range(2):
                    for f in range(NCH):
                        r_d = ctx.op(
                            "pe", lambda E, f=f, ts=ts, dh=dh, dps=dps, dwv=dwv:
                            E.matmul(dps[:, 512 * dh:512 * (dh + 1)],
                                     Hb[:, 512 * f + 128 * ts:512 * f + 128 * (ts + 1)],
                                     dwv[:, f, 512 * dh:512 * (dh + 1)],
                                     start=(f == 0), stop=(f == NCH - 1)),
                            inc="pe" if (dh == 1 and f == NCH - 1) else None)
                ctx.wait("dve", "pe", r_d)
                Xi = X[:, 1024 * it:1024 * (it + 1)]
                if real:
                    wcol = WF[:, 16 * it + (ei - 2):16 * it + (ei - 2) + 1]
                    c_dph = ctx.op(
                        "dve", lambda E, Xi=Xi, dps=dps, wcol=wcol:
                        E.scalar_tensor_tensor(Xi, dps, wcol, Xi,
                                               ALU.mult, ALU.add), inc="dve")
                else:
                    c_dph = ctx.op("dve", lambda E, Xi=Xi, dps=dps:
                                   E.tensor_add(Xi, Xi, dps), inc="dve")
            ctx.wait("pe", "dve", c_dph)
        if ei + 1 < 18:
            ctx.wait("pool", "pe", ctx.cnt["pe"])
            c_dw[ei + 1] = ctx.dma(
                "pool", f"ldw{(ei + 1) % 2}",
                DW[:, ((ei + 1) % 2) * NCH * D:((ei + 1) % 2 + 1) * NCH * D]
                .rearrange("p (c h) -> p c h", h=D),
                dw_src(ei + 1).rearrange("(c p) h -> p c h", p=128))




def emit_attn1(nc, ctx, P, C):
    w3_d, wo_d = P["a1_w3"], P["a1_wo"]
    X, IDN = P["X"], P["IDN"]

    XQ = C.t("XQ", [128, NCH * S], BF16)
    KF = C.t("KF", [128, NT * D], BF16)
    VT = C.t("VT", [128, NT * D], BF16)
    WO = C.t("WO", [128, NCH * D], BF16)
    W3 = C.t("W3", [128, 3 * 512], BF16)
    KVZ = C.t("KVZ", [128, 16 * 65], BF16)
    ONES = C.t("ONES", [128, 1], BF16)
    MN = C.t("MN", [128, D], F32)
    EXs = C.t("EXs", [128, D], BF16)
    RL = C.t("RL", [128, D], BF16)
    QT = C.t("QT", [128, D], BF16)
    XT = C.t("XT", [128, 2 * D], BF16)
    DN = C.t("DN", [128, 1], F32)
    RC = C.t("RC", [128, 1], F32)
    EPS6 = C.t("EPS6", [128, 1], F32)
    MV = C.t("MV", [128, 2 * NT], F32)
    RS = C.t("RS", [128, NT], F32)
    BN6 = C.t("BN6", [128, 12], F32)
    SC1 = C.t("SC1", [128, 1], F32)
    EPS = C.t("EPS", [128, 1], F32)

    pt_ps = P["pt_ps"]
    p_ps = P["pair_ps"]
    d_ps = P["big_ps"]
    nd_ps = P["one_ps"]
    KZ = P["big_ps"]

    B = {"X": X, "XP": XQ, "XT": XT, "pt_ps": pt_ps, "BN6": BN6,
         "SC1": SC1, "MV": MV, "RS": RS, "EPS": EPS, "IDN": IDN,
         "Xv": X[:, :].rearrange("p (i d) -> p i d", d=D),
         "XPv": XQ[:, :].rearrange("p (c t) -> p c t", t=S)}
    XPv = B["XPv"]
    WOv = WO[:, :].rearrange("p (c d) -> p c d", d=D)

    c_w3 = ctx.dma("pool", "ld_w", W3[:, :], w3_d)
    c_wo = ctx.dma("pool", "ld_wo", WOv, wo_d.rearrange("(c p) d -> p c d", p=128))
    ctx.op("dve", lambda E: E.memset(ONES[:, :], 1.0), inc="dve")
    ctx.op("dve", lambda E: E.memset(EPS6[:, :], 1e-6), inc="dve")
    c_ones = ctx.cnt["dve"]

    c_xp = emit_load_ln_xp(ctx, nc, B, 1e-5, load_x=False)

    ctx.wait("pe", "ld_w", c_w3)
    c_evac = 0
    c_pt = ctx.cnt["dve"]
    for i in range(NT):
        for m, nm in ((0, "q"), (1, "k"), (2, "v")):
            ctx.wait("pe", "dve", c_evac)
            ctx.wait("pe", "act", ctx.cnt["act"])
            for par in range(2):
                for j in range(8):
                    h = 2 * j + par
                    pbp = 64 * par
                    pc = 512 * par + 64 * j
                    r = ctx.op("pe", lambda E, h=h, pbp=pbp, pc=pc, m=m, i=i:
                               E.matmul(p_ps[:, pc:pc + 64],
                                        XPv[pbp:pbp + 64, h // 2, 128 * i:128 * (i + 1)],
                                        W3[pbp:pbp + 64, 512 * m + 64 * (h // 2):
                                           512 * m + 64 * (h // 2) + 64],
                                        start=True, stop=True),
                               inc="pe" if (par == 1 and j == 7) else None)
            if m < 2:
                ctx.wait("dve", "pe", r)
                c_mn = ctx.op("dve", lambda E:
                              E.tensor_scalar_min(MN[:, :], p_ps[:, :], 0.0),
                              inc="dve")
                c_rl = ctx.op("dve", lambda E:
                              E.tensor_scalar_max(RL[:, :], p_ps[:, :], 0.0),
                              inc="dve")
                ctx.wait("act", "dve", c_mn)
                c_ex = ctx.op("act", lambda E:
                              E.activation(EXs[:, :], MN[:, :], AF.Exp),
                              inc="act")
                ctx.wait("dve", "act", c_ex)
                ctx.wait("dve", "dve", c_rl)
                dst = QT[:, :] if m == 0 else KF[:, D * i:D * (i + 1)]
                c_evac = ctx.op("dve", lambda E, dst=dst:
                                E.tensor_add(dst, EXs[:, :], RL[:, :]),
                                inc="dve")
            else:
                ctx.wait("dve", "pe", r)
                c_evac = ctx.op("dve", lambda E, i=i:
                                E.tensor_copy(VT[:, D * i:D * (i + 1)],
                                              p_ps[:, :]), inc="dve")
        for par in range(2):
            ctx.wait("pe", "dve", c_evac if par == 0 else c_pt)
            ctx.wait("pe", "dve", c_pt)
            for j in range(8):
                h = 2 * j + par
                pc = 512 * (h % 2) + 64 * (h // 2)
                r = ctx.op("pe", lambda E, j=j, pc=pc:
                           E.transpose(pt_ps[0:64, 128 * j:128 * (j + 1)],
                                       QT[:, pc:pc + 64], IDN[:, :]),
                           inc="pe" if j == 7 else None)
            ctx.wait("dve", "pe", r)
            c_pt = ctx.op("dve", lambda E, par=par, i=i: E.tensor_copy(
                XPv[64 * par:64 * par + 64, :, 128 * i:128 * (i + 1)],
                pt_ps[0:64, :].rearrange("p (c t) -> p c t", t=128)),
                inc="dve")
    c_proj = ctx.cnt["dve"]

    ctx.wait("pe", "dve", c_proj)
    ctx.wait("pe", "dve", c_ones)
    c_kvz_evac = 0
    for h in range(16):
        ko = (h % 2) * 65
        ctx.wait("pe", "dve", c_kvz_evac)
        pc = 512 * (h % 2) + 64 * (h // 2)
        for tt in range(NT):
            ctx.op("pe", lambda E, tt=tt, ko=ko, pc=pc:
                   E.matmul(KZ[0:64, 1024 + ko:1024 + ko + 64],
                            KF[:, 1024 * tt + pc:1024 * tt + pc + 64],
                            VT[:, 1024 * tt + pc:1024 * tt + pc + 64],
                            start=(tt == 0), stop=(tt == NT - 1)))
        for tt in range(NT):
            r = ctx.op("pe", lambda E, tt=tt, ko=ko, pc=pc:
                       E.matmul(KZ[0:64, 1024 + ko + 64:1024 + ko + 65],
                                KF[:, 1024 * tt + pc:1024 * tt + pc + 64],
                                ONES[:, :],
                                start=(tt == 0), stop=(tt == NT - 1)),
                       inc="pe" if tt == NT - 1 else None)
        ctx.wait("dve", "pe", r)
        ctx.op("dve", lambda E, h=h, ko=ko:
               E.tensor_copy(KVZ[0:64, 65 * h:65 * (h + 1)],
                             KZ[0:64, 1024 + ko:1024 + ko + 65]), inc="dve")
        c_kvz_evac = ctx.op("dve", lambda E, h=h, ko=ko:
                            E.tensor_copy(KVZ[64:128, 65 * h:65 * (h + 1)],
                                          KZ[0:64, 1024 + ko:1024 + ko + 65]), inc="dve")
    c_kvz = c_kvz_evac

    c_ct = 0
    for i in range(NT):
        for h in range(16):
            pbp = 64 * (h % 2)
            no = ((i * 16 + h) % 2) * 65
            ctx.wait("pe", "dve", c_ct)
            ctx.wait("pe", "dve", c_kvz)
            r = ctx.op("pe", lambda E, h=h, i=i, pbp=pbp, no=no:
                       E.matmul(nd_ps[:, no:no + 65],
                                XPv[pbp:pbp + 64, h // 2, 128 * i:128 * (i + 1)],
                                KVZ[pbp:pbp + 64, 65 * h:65 * (h + 1)],
                                start=True, stop=True), inc="pe")
            ctx.wait("dve", "pe", r)
            ctx.op("dve", lambda E, no=no:
                   E.tensor_scalar(DN[:, :], nd_ps[:, no + 64:no + 65],
                                   EPS6[:, :], None, ALU.add), inc="dve")
            ctx.wait("dve", "dve")
            ctx.op("dve", lambda E: E.reciprocal(RC[:, :], DN[:, :]), inc="dve")
            ctx.wait("dve", "dve")
            c_ct = ctx.op("dve", lambda E, i=i, h=h, no=no:
                          E.tensor_scalar(KF[:, 1024 * i + 64 * h:
                                             1024 * i + 64 * (h + 1)],
                                          nd_ps[:, no:no + 64],
                                          RC[:, :], None, ALU.mult), inc="dve")

    VTv = VT[:, :].rearrange("p (c t) -> p c t", t=S)
    c_pt2 = c_pt
    for i in range(NT):
        ctx.wait("pe", "dve", c_ct)
        ctx.wait("pe", "dve", c_pt2)
        for c in range(NCH):
            r = ctx.op("pe", lambda E, c=c, i=i:
                       E.transpose(pt_ps[:, c * 128:(c + 1) * 128],
                                   KF[:, 1024 * i + 128 * c:1024 * i + 128 * (c + 1)],
                                   IDN[:, :]),
                       inc="pe" if c == NCH - 1 else None)
        ctx.wait("dve", "pe", r)
        c_pt2 = ctx.op("dve", lambda E, i=i: E.tensor_copy(
            VTv[:, :, 128 * i:128 * (i + 1)],
            pt_ps[:, :].rearrange("p (c t) -> p c t", t=128)), inc="dve")
    c_ctf = c_pt2

    ctx.wait("pe", "ld_wo", c_wo)
    ctx.wait("pe", "dve", c_ctf)
    c_y = 0
    for i in range(NT):
        ctx.wait("pe", "dve", c_y)
        for hh in range(2):
            for c in range(NCH):
                r = ctx.op("pe", lambda E, c=c, hh=hh, i=i:
                           E.matmul(d_ps[:, 512 * hh:512 * (hh + 1)],
                                    VTv[:, c, 128 * i:128 * (i + 1)],
                                    WOv[:, c, 512 * hh:512 * (hh + 1)],
                                    start=(c == 0), stop=(c == NCH - 1)),
                           inc="pe" if (hh == 1 and c == NCH - 1) else None)
        ctx.wait("dve", "pe", r)
        c_y = ctx.op("dve", lambda E, i=i:
                     E.tensor_add(X[:, 1024 * i:1024 * (i + 1)],
                                  X[:, 1024 * i:1024 * (i + 1)],
                                  d_ps[:, 0:1024]), inc="dve")




def emit_ffn1(nc, ctx, P, C):
    rw_d, br_d = P["f1_rw"], P["f1_br"]
    w1_d, b1_d, w2_d, b2_d = P["f1_w1"], P["f1_b1"], P["f1_w2"], P["f1_b2"]
    X, IDN = P["X"], P["IDN"]

    XP = C.t("XP", [128, NCH * S], BF16)
    W1 = C.t("W1", [128, NCH * 2048], BF16)
    W2 = C.t("W2", [128, 16 * D], BF16)
    Hb = C.t("Hb", [128, 16 * 512], BF16)
    XT = C.t("XT", [128, 2 * D], BF16)
    RW = C.t("RW", [128, NCH * 8], BF16)
    BR = C.t("BR", [128, 8], F32)
    B1 = C.t("B1", [128, 256], F32)
    B2 = C.t("B2", [8, D], BF16)
    SCO = C.t("SCO", [128, NT * 8], F32)
    OH = C.t("OH", [128, NT * 8], F32)
    OHB = C.t("OHB", [128, 8], BF16)
    OHT = C.t("OHT", [8, S], BF16)
    M1 = C.t("M1", [128, 1], F32)
    MV = C.t("MV", [128, 2 * NT], F32)
    RS = C.t("RS", [128, NT], F32)
    BN6 = C.t("BN6", [128, 12], F32)
    SC1 = C.t("SC1", [128, 1], F32)
    EPS = C.t("EPS", [128, 1], F32)

    pt_ps = P["pt_ps"]
    h_ps = P["pair_ps"]
    d_ps = P["big_ps"]
    m_ps = P["one_ps"]

    B = {"X": X, "XP": XP, "XT": XT, "pt_ps": pt_ps, "BN6": BN6,
         "SC1": SC1, "MV": MV, "RS": RS, "EPS": EPS, "IDN": IDN,
         "Xv": X[:, :].rearrange("p (i d) -> p i d", d=D),
         "XPv": XP[:, :].rearrange("p (c t) -> p c t", t=S)}
    XPv = B["XPv"]
    W1v = W1[:, :].rearrange("p (c h) -> p c h", h=2048)
    W2v = W2[:, :].rearrange("p (c h) -> p c h", h=D)

    ctx.dma("sp", "ld_c", BR[:, :], br_d)
    ctx.dma("sp", "ld_c", B1[:, :], b1_d)
    ctx.dma("pool", "ld_rw", RW[:, :].rearrange("p (c n) -> p c n", n=8),
            rw_d.rearrange("(c p) n -> p c n", p=128))
    ctx.dma("pool", "ld_rw", B2[:, :], b2_d)

    def w1_src(se):
        return w1_d[se // 2][:, 2048 * (se % 2):2048 * (se % 2 + 1)]

    def w2_src(se):
        return w2_d[se // 2][2048 * (se % 2):2048 * (se % 2 + 1), :]

    c_w1 = {0: ctx.dma("pool", "lw1", W1v,
                       w1_src(0).rearrange("(c p) h -> p c h", p=128))}
    c_w2 = {0: ctx.dma("pool", "lw2", W2v,
                       w2_src(0).rearrange("(c p) h -> p c h", p=128))}

    c_xp = emit_load_ln_xp(ctx, nc, B, 1e-5, load_x=False)

    ctx.wait("pe", "ld_c", ctx.cnt["ld_c"])
    ctx.wait("pe", "ld_rw", ctx.cnt["ld_rw"])
    for i in range(NT):
        ctx.wait("pe", "dve", ctx.cnt["dve"])
        for c in range(NCH):
            r = ctx.op("pe", lambda E, c=c, i=i:
                       E.matmul(m_ps[:, 0:8], XPv[:, c, 128 * i:128 * (i + 1)],
                                RW[:, 8 * c:8 * (c + 1)],
                                start=(c == 0), stop=(c == NCH - 1)),
                       inc="pe" if c == NCH - 1 else None)
        ctx.wait("dve", "pe", r)
        ctx.op("dve", lambda E, i=i:
               E.scalar_tensor_tensor(SCO[:, 8 * i:8 * (i + 1)], m_ps[:, 0:8],
                                      1.0, BR[:, :], ALU.mult, ALU.add),
               inc="dve")
    c_pt = ctx.cnt["dve"]
    for i in range(NT):
        sco = SCO[:, 8 * i:8 * (i + 1)]
        ctx.wait("dve", "dve")
        c_m1 = ctx.op("dve", lambda E, sco=sco:
                      E.reduce_max(M1[:, :], sco, axis=AX.X), inc="dve")
        ctx.wait("dve", "dve", c_m1)
        ctx.op("dve", lambda E, sco=sco, i=i:
               E.tensor_scalar(OH[:, 8 * i:8 * (i + 1)], sco, M1[:, :], None,
                               ALU.is_ge), inc="dve")
        ctx.wait("dve", "dve")
        c_ohb = ctx.op("dve", lambda E, i=i:
                       E.tensor_copy(OHB[:, :], OH[:, 8 * i:8 * (i + 1)]),
                       inc="dve")
        ctx.wait("pe", "dve", c_ohb)
        ctx.wait("pe", "dve", c_pt)
        r = ctx.op("pe", lambda E:
                   E.transpose(pt_ps[0:8, 0:128], OHB[:, :], IDN[:, :]),
                   inc="pe")
        ctx.wait("dve", "pe", r)
        c_pt = ctx.op("dve", lambda E, i=i:
                      E.tensor_copy(OHT[0:8, 128 * i:128 * (i + 1)],
                                    pt_ps[0:8, 0:128]), inc="dve")
    c_oh_all = ctx.cnt["dve"]

    c_dph = 0
    for se in range(16):
        ex = se // 2
        ctx.wait("pe", "lw1", c_w1[se])
        ctx.wait("pe", "lw2", c_w2[se])
        ctx.wait("pe", "dve", c_xp)
        for tq in range(4):
            for f in range(16):
                hp = h_ps[:, (f % 2) * 512:(f % 2) * 512 + 512]
                ctx.wait("pe", "act", ctx.cnt["act"])
                for c in range(NCH):
                    r_h = ctx.op("pe", lambda E, c=c, f=f, tq=tq, hp=hp:
                                 E.matmul(hp, W1v[:, c, 128 * f:128 * (f + 1)],
                                          XPv[:, c, 512 * tq:512 * (tq + 1)],
                                          start=(c == 0), stop=(c == NCH - 1)),
                                 inc="pe" if c == NCH - 1 else None)
                ctx.wait("act", "pe", r_h)
                bcol = 32 * ex + 16 * (se % 2) + f
                ctx.op("act", lambda E, f=f, hp=hp, bcol=bcol:
                       E.activation(Hb[:, 512 * f:512 * (f + 1)], hp, AF.Gelu,
                                    bias=B1[:, bcol:bcol + 1]), inc="act")
            if tq == 3 and se + 1 < 16:
                ctx.wait("pool", "pe", ctx.cnt["pe"])
                c_w1[se + 1] = ctx.dma(
                    "pool", "lw1", W1v,
                    w1_src(se + 1).rearrange("(c p) h -> p c h", p=128))
            ctx.wait("pe", "act", ctx.cnt["act"])
            if se == 0 and tq == 0:
                ctx.wait("dve", "dve", c_oh_all)
            for ts in range(4):
                it = 4 * tq + ts
                dps = d_ps[:, (ts % 2) * 1024:(ts % 2) * 1024 + 1024]
                ctx.wait("pe", "dve", c_dph)
                for dh in range(2):
                    for f in range(16):
                        r_d = ctx.op(
                            "pe", lambda E, f=f, ts=ts, dh=dh, dps=dps:
                            E.matmul(dps[:, 512 * dh:512 * (dh + 1)],
                                     Hb[:, 512 * f + 128 * ts:512 * f + 128 * (ts + 1)],
                                     W2v[:, f, 512 * dh:512 * (dh + 1)],
                                     start=(f == 0), stop=(f == 15)),
                            inc="pe" if (dh == 1 and f == 15) else None)
                ctx.wait("dve", "pe", r_d)
                Xi = X[:, 1024 * it:1024 * (it + 1)]
                ocol = OH[:, 8 * it + ex:8 * it + ex + 1]
                c_dph = ctx.op(
                    "dve", lambda E, Xi=Xi, dps=dps, ocol=ocol:
                    E.scalar_tensor_tensor(Xi, dps, ocol, Xi,
                                           ALU.mult, ALU.add), inc="dve")
            ctx.wait("pe", "dve", c_dph)
        if se + 1 < 16:
            ctx.wait("pool", "pe", ctx.cnt["pe"])
            c_w2[se + 1] = ctx.dma(
                "pool", "lw2", W2v,
                w2_src(se + 1).rearrange("(c p) h -> p c h", p=128))

    ctx.wait("pe", "ld_rw", ctx.cnt["ld_rw"])
    c_y = c_dph
    for i in range(NT):
        ctx.wait("pe", "dve", c_y)
        for dh in range(2):
            r = ctx.op("pe", lambda E, i=i, dh=dh:
                       E.matmul(d_ps[:, 512 * dh:512 * (dh + 1)],
                                OHT[0:8, 128 * i:128 * (i + 1)],
                                B2[0:8, 512 * dh:512 * (dh + 1)],
                                start=True, stop=True),
                       inc="pe" if dh == 1 else None)
        ctx.wait("dve", "pe", r)
        c_y = ctx.op("dve", lambda E, i=i:
                     E.tensor_add(X[:, 1024 * i:1024 * (i + 1)],
                                  X[:, 1024 * i:1024 * (i + 1)],
                                  d_ps[:, 0:1024]), inc="dve")




def build_full():
    nc = bass.Bass("TRN2", target_bir_lowering=False)
    P = {}
    P["x"] = nc.dram_tensor("x", [S, D], F32, kind="ExternalInput").ap()
    P["idn"] = nc.dram_tensor("idn", [128, 128], BF16, kind="ExternalInput").ap()
    P["a0_wq"] = nc.dram_tensor("a0_wq", [8, D, 128], F32, kind="ExternalInput").ap()
    P["a0_wk"] = nc.dram_tensor("a0_wk", [8, D, 128], F32, kind="ExternalInput").ap()
    P["a0_wv"] = nc.dram_tensor("a0_wv", [8, D, 128], F32, kind="ExternalInput").ap()
    P["a0_wo"] = nc.dram_tensor("a0_wo", [D, D], F32, kind="ExternalInput").ap()
    P["a0_gb"] = nc.dram_tensor("a0_gb", [128, D], F32, kind="ExternalInput").ap()
    P["a0_bb"] = nc.dram_tensor("a0_bb", [128, D], F32, kind="ExternalInput").ap()
    P["a0_mask"] = nc.dram_tensor("a0_mask", [128, 4 * 512], BF16,
                                  kind="ExternalInput").ap()
    P["f0_rw"] = nc.dram_tensor("f0_rw", [D, 16], F32, kind="ExternalInput").ap()
    P["f0_gup"] = nc.dram_tensor("f0_gup", [16, D, 2048], F32,
                                 kind="ExternalInput").ap()
    P["f0_dwn"] = nc.dram_tensor("f0_dwn", [16, D, D], F32,
                                 kind="ExternalInput").ap()
    P["f0_shgu"] = nc.dram_tensor("f0_shgu", [2, D, 2048], F32,
                                  kind="ExternalInput").ap()
    P["f0_shdw"] = nc.dram_tensor("f0_shdw", [2, D, D], F32,
                                  kind="ExternalInput").ap()
    P["a1_w3"] = nc.dram_tensor("a1_w3", [128, 3 * 512], F32,
                                kind="ExternalInput").ap()
    P["a1_wo"] = nc.dram_tensor("a1_wo", [D, D], F32, kind="ExternalInput").ap()
    P["f1_rw"] = nc.dram_tensor("f1_rw", [D, 8], F32, kind="ExternalInput").ap()
    P["f1_br"] = nc.dram_tensor("f1_br", [128, 8], F32, kind="ExternalInput").ap()
    P["f1_w1"] = nc.dram_tensor("f1_w1", [8, D, 4096], F32,
                                kind="ExternalInput").ap()
    P["f1_b1"] = nc.dram_tensor("f1_b1", [128, 256], F32, kind="ExternalInput").ap()
    P["f1_w2"] = nc.dram_tensor("f1_w2", [8, 4096, D], F32,
                                kind="ExternalInput").ap()
    P["f1_b2"] = nc.dram_tensor("f1_b2", [8, D], F32, kind="ExternalInput").ap()
    yout = nc.dram_tensor("y", [S, D], F32, kind="ExternalOutput").ap()

    with ExitStack() as stk:
        slab = nc.alloc_sbuf_tensor("slab", [128, SLAB], U8)
        base = nc.lookup_mloc(slab).addr
        X = nc.alloc_sbuf_tensor_at("X", [128, NT * D], F32, offset=base)
        IDN = nc.alloc_sbuf_tensor_at("IDN", [128, 128], BF16, offset=base + 65536)
        P["X"] = X
        P["IDN"] = IDN
        P["pt_ps"] = nc.alloc_psum_tensor("pt_ps", [128, 1024], BF16)
        P["big_ps"] = nc.alloc_psum_tensor("big_ps", [128, 2048], F32)
        P["pair_ps"] = nc.alloc_psum_tensor("pair_ps", [128, 1024], F32)
        P["one_ps"] = nc.alloc_psum_tensor("one_ps", [128, 512], F32)
        Xv = X[:, :].rearrange("p (i d) -> p i d", d=D)

        ctx = Ctx(nc)
        ctx.make_sems(stk, ["ld_x", "ld_c", "ld_wo", "ld_rw", "ld_w",
                            "lw0", "lw1", "lw2", "lgu", "ldw0", "ldw1", "st",
                            "pe", "dve", "act", "pool"])

        ctx.dma("sp", "ld_c", IDN[:, :], P["idn"])

        emit_attn0(nc, ctx, P, Carver(nc, base, PERSIST, SLAB, "a0"))
        ctx.barrier()
        emit_ffn0(nc, ctx, P, Carver(nc, base, PERSIST, SLAB, "f0"))
        ctx.barrier()
        emit_attn1(nc, ctx, P, Carver(nc, base, PERSIST, SLAB, "a1"))
        ctx.barrier()
        emit_ffn1(nc, ctx, P, Carver(nc, base, PERSIST, SLAB, "f1"))

        ctx.wait("sp", "dve", ctx.cnt["dve"])
        ctx.store(yout.rearrange("(i p) d -> p i d", p=128), Xv)
        ctx.wait("sp", "st", ctx.cnt["st"])

        with nc.Block() as block:
            ctx.replay(block)
    return nc




def _pairify(w):
    w = np.asarray(w, np.float32)
    return np.ascontiguousarray(
        np.transpose(w, (1, 0, 2)).reshape(D, D).reshape(D, 8, 128)
        .transpose(1, 0, 2))


def _prep_all(inp):
    p = np.arange(128)[:, None]
    f = np.arange(512)[None, :]
    mask = np.stack([np.where(f <= p + 128 * k, 0.0, NEG) for k in range(4)],
                    axis=1).reshape(128, 4 * 512).astype(BF)
    d = {
        "idn": np.eye(128, dtype=BF),
        "a0_wq": _pairify(inp["L0_attn_Wq"]),
        "a0_wk": _pairify(inp["L0_attn_Wk"]),
        "a0_wv": _pairify(inp["L0_attn_Wv"]),
        "a0_wo": np.ascontiguousarray(np.asarray(inp["L0_attn_Wo"], np.float32)),
        "a0_gb": np.ascontiguousarray(np.broadcast_to(
            np.asarray(inp["L0_attn_gamma"], np.float32), (128, D))),
        "a0_bb": np.ascontiguousarray(np.broadcast_to(
            np.asarray(inp["L0_attn_beta"], np.float32), (128, D))),
        "a0_mask": mask,
    }
    sgw = np.asarray(inp["L0_ffn_shared_gate_w"], np.float32).T
    suw = np.asarray(inp["L0_ffn_shared_up_w"], np.float32).T
    sdw = np.asarray(inp["L0_ffn_shared_down_w"], np.float32).T
    shgu = np.stack([
        np.concatenate([sgw[:, 0:1024], suw[:, 0:1024]], axis=1),
        np.concatenate([sgw[:, 1024:2048], suw[:, 1024:2048]], axis=1)])
    d.update({
        "f0_rw": np.ascontiguousarray(
            np.asarray(inp["L0_ffn_router_w"], np.float32).T),
        "f0_gup": np.ascontiguousarray(np.asarray(inp["L0_ffn_gate_up_w"],
                                                  np.float32)),
        "f0_dwn": np.ascontiguousarray(np.asarray(inp["L0_ffn_down_w"],
                                                  np.float32)),
        "f0_shgu": np.ascontiguousarray(shgu),
        "f0_shdw": np.ascontiguousarray(np.stack([sdw[0:1024], sdw[1024:2048]])),
    })
    w3 = np.zeros((128, 3 * 512), np.float32)
    for m, nm in ((0, "L1_attn_Wq"), (1, "L1_attn_Wk"), (2, "L1_attn_Wv")):
        w = np.asarray(inp[nm], np.float32)
        for h in range(16):
            pbp = 64 * (h % 2)
            w3[pbp:pbp + 64, 512 * m + 64 * (h // 2):512 * m + 64 * (h // 2) + 64] = w[h]
    d.update({
        "a1_w3": w3,
        "a1_wo": np.ascontiguousarray(np.asarray(inp["L1_attn_Wo"], np.float32)),
    })
    b1 = np.asarray(inp["L1_ffn_b1"], np.float32)
    b1sb = np.zeros((128, 256), np.float32)
    for ex in range(8):
        for col in range(32):
            b1sb[:, 32 * ex + col] = b1[ex, 128 * col:128 * (col + 1)]
    d.update({
        "f1_rw": np.ascontiguousarray(np.asarray(inp["L1_ffn_wr"], np.float32)),
        "f1_br": np.ascontiguousarray(np.broadcast_to(
            np.asarray(inp["L1_ffn_br"], np.float32), (128, 8))),
        "f1_w1": np.ascontiguousarray(np.asarray(inp["L1_ffn_w1"], np.float32)),
        "f1_b1": b1sb,
        "f1_w2": np.ascontiguousarray(np.asarray(inp["L1_ffn_w2"], np.float32)),
        "f1_b2": np.ascontiguousarray(np.asarray(inp["L1_ffn_b2"], np.float32)),
    })
    return d


def kernel(**inp):
    x = np.asarray(inp["L0_attn_x"], dtype=np.float32)
    nc = build_full()
    shared = _prep_all(inp)
    in_maps = [dict(shared, x=np.ascontiguousarray(x[b])) for b in range(8)]
    res = run_bass_kernel_spmd(nc, in_maps, core_ids=list(range(8)))
    return np.ascontiguousarray(
        np.stack([res.results[b]["y"] for b in range(8)], axis=0))


# version 03
# speedup vs baseline: 1.2600x; 1.2600x over previous
import numpy as np
import ml_dtypes
from contextlib import ExitStack
from collections import defaultdict

import concourse.bass as bass
import concourse.mybir as mybir
from concourse.bass_utils import run_bass_kernel_spmd

F32 = mybir.dt.float32
BF16 = mybir.dt.bfloat16
U8 = mybir.dt.uint8
AF = mybir.ActivationFunctionType
ALU = mybir.AluOpType
AX = mybir.AxisListType
BF = ml_dtypes.bfloat16

D = 1024
S = 2048
NT = S // 128
NCH = D // 128
NEG = -1e30

SLAB = 212736
PERSIST = 65792



class Ctx:

    def __init__(self, nc):
        self.nc = nc
        self.plan = []
        self.sems = {}
        self.cnt = defaultdict(int)

    def make_sems(self, stack, names):
        for n in names:
            self.sems[n] = stack.enter_context(self.nc.semaphore(n))

    def op(self, eng, emit, inc=None, n=1):
        if inc is None:
            self.plan.append((eng, emit))
            return None
        self.cnt[inc] += n
        s = self.sems[inc]
        self.plan.append((eng, lambda E, emit=emit, s=s, n=n: emit(E).then_inc(s, n)))
        return self.cnt[inc]

    def wait(self, eng, sem, val=None):
        v = self.cnt[sem] if val is None else val
        if v <= 0:
            return
        s = self.sems[sem]
        self.plan.append((eng, lambda E, s=s, v=v: E.wait_ge(s, v)))

    def dma(self, eng, sem, out, in_, **kw):
        return self.op(eng, lambda E: E.dma_start(out=out, in_=in_, **kw), inc=sem, n=16)

    def store(self, out, in_):
        return self.op("sp", lambda E: E.dma_start(out=out, in_=in_), inc="st", n=16)

    def barrier(self):
        for eng in ("sp", "pool", "pe", "dve", "act"):
            for s in list(self.cnt):
                self.wait(eng, s, self.cnt[s])

    def replay(self, block):
        plan = self.plan

        @block.sync
        def _(E):
            for g, fn in plan:
                if g == "sp":
                    fn(E)

        @block.gpsimd
        def _(E):
            for g, fn in plan:
                if g == "pool":
                    fn(E)

        @block.tensor
        def _(E):
            for g, fn in plan:
                if g == "pe":
                    fn(E)

        @block.vector
        def _(E):
            for g, fn in plan:
                if g == "dve":
                    fn(E)

        @block.scalar
        def _(E):
            for g, fn in plan:
                if g == "act":
                    fn(E)


class Carver:

    def __init__(self, nc, base, lo, hi, prefix):
        self.nc = nc
        self.base = base
        self.p = lo
        self.hi = hi
        self.prefix = prefix

    def t(self, name, shape, dtype):
        per = mybir.dt.size(dtype)
        for s in shape[1:]:
            per *= s
        addr = (self.p + 31) & ~31
        self.p = addr + per
        assert self.p <= self.hi, (self.prefix, name, self.p, self.hi)
        return self.nc.alloc_sbuf_tensor_at(self.prefix + name, shape, dtype,
                                            offset=self.base + addr)


def emit_ln_stats(ctx, bn6, sc1, src_ap, mcol, vcol, rscol, eps_ap):
    ctx.op("dve", lambda E: E.bn_stats(bn6[:, 0:6], src_ap[:, 0:512]))
    ctx.op("dve", lambda E: E.bn_stats(bn6[:, 6:12], src_ap[:, 512:1024]), inc="dve")
    ctx.wait("dve", "dve")
    c_mv = ctx.op("dve", lambda E: E.bn_aggr(mcol, bn6[:, 0:12]), inc="dve")
    ctx.wait("act", "dve", c_mv)
    ctx.op("act", lambda E: E.activation(sc1, vcol, AF.Ln, bias=eps_ap), inc="act")
    ctx.wait("act", "act")
    c_rs = ctx.op("act", lambda E: E.activation(rscol, sc1, AF.Exp, scale=-0.5),
                  inc="act")
    return c_mv, c_rs


def emit_load_ln_xp(ctx, nc, B, eps_val, load_x, xin=None):
    Xv, XPv = B["Xv"], B["XPv"]
    if load_x:
        c_x = ctx.dma("sp", "ld_x", Xv, xin.rearrange("(i p) d -> p i d", p=128))
    ctx.op("dve", lambda E: E.memset(B["EPS"][:, :], eps_val), inc="dve")
    c_eps = ctx.cnt["dve"]
    if load_x:
        ctx.wait("dve", "ld_x", c_x)
        ctx.wait("pool", "ld_x", c_x)
    ctx.wait("act", "dve", c_eps)
    pe_done = {}
    c_pt_evac = 0
    for i in range(NT):
        Xi = B["X"][:, i * D:(i + 1) * D]
        c_mv, c_rs = emit_ln_stats(ctx, B["BN6"], B["SC1"][:, :], Xi,
                                   B["MV"][:, 2 * i:2 * i + 2],
                                   B["MV"][:, 2 * i + 1:2 * i + 2],
                                   B["RS"][:, i:i + 1], B["EPS"][:, :])
        xt = B["XT"][:, (i % 2) * D:(i % 2 + 1) * D]
        if i >= 2:
            ctx.wait("pool", "pe", pe_done[i - 2])
        ctx.wait("pool", "dve", c_mv)
        ctx.wait("pool", "act", c_rs)
        c_ap = ctx.op("pool", lambda E, xt=xt, Xi=Xi, i=i:
                      E.tensor_scalar(xt, Xi, B["MV"][:, 2 * i:2 * i + 1],
                                      B["RS"][:, i:i + 1], ALU.subtract, ALU.mult),
                      inc="pool")
        ctx.wait("pe", "pool", c_ap)
        ctx.wait("pe", "ld_c", ctx.cnt["ld_c"])
        ctx.wait("pe", "dve", c_pt_evac)
        for c in range(NCH):
            r = ctx.op("pe", lambda E, c=c, xt=xt:
                       E.transpose(B["pt_ps"][:, c * 128:(c + 1) * 128],
                                   xt[:, c * 128:(c + 1) * 128], B["IDN"][:, :]),
                       inc="pe" if c == NCH - 1 else None)
        pe_done[i] = r
        ctx.wait("dve", "pe", pe_done[i])
        c_pt_evac = ctx.op(
            "dve", lambda E, i=i: E.tensor_copy(
                XPv[:, :, 128 * i:128 * (i + 1)],
                B["pt_ps"][:, :].rearrange("p (c t) -> p c t", t=128)),
            inc="dve")
    return c_pt_evac




def emit_attn0(nc, ctx, P, C):
    xin = P["x"]
    wq_d, wk_d, wv_d = P["a0_wq"], P["a0_wk"], P["a0_wv"]
    wo_d, gb_d, bb_d, mask_d = P["a0_wo"], P["a0_gb"], P["a0_bb"], P["a0_mask"]
    X, IDN = P["X"], P["IDN"]

    XP = C.t("XP", [128, NCH * S], BF16)
    OF = C.t("OF", [128, NCH * S], BF16)
    WO = C.t("WO", [128, NCH * D], BF16)
    WQ = C.t("WQ", [128, 2 * NCH * 128], BF16)
    WK = C.t("WK", [128, 2 * NCH * 128], BF16)
    WV = C.t("WV", [128, 2 * NCH * 128], BF16)
    QP = C.t("QP", [128, S], BF16)
    KP = C.t("KP", [128, S], BF16)
    VP = C.t("VP", [128, S], BF16)
    PT = C.t("PT", [128, S], BF16)
    PTT = C.t("PTT", [128, S], BF16)
    XT = C.t("XT", [128, 2 * D], BF16)
    TMP = C.t("TMP", [128, D], F32)
    Y = C.t("Y", [128, D], F32)
    YN = C.t("YN", [128, D], F32)
    GB = C.t("GB", [128, D], F32)
    BBt = C.t("BBt", [128, D], F32)
    MASK = C.t("MASK", [128, 4 * 512], BF16)
    MV = C.t("MV", [128, 2 * NT], F32)
    RS = C.t("RS", [128, NT], F32)
    MV2 = C.t("MV2", [128, 2], F32)
    RS2 = C.t("RS2", [128, 1], F32)
    BN6 = C.t("BN6", [128, 12], F32)
    SC1 = C.t("SC1", [128, 1], F32)
    NGM = C.t("NGM", [128, 1], F32)
    SM = C.t("SM", [128, 1], F32)
    REC = C.t("REC", [128, 1], F32)
    EPS5 = C.t("EPS5", [128, 1], F32)
    EPS3 = C.t("EPS3", [128, 1], F32)

    proj_ps = P["bigB"]
    sc_ps = P["bigA"]
    pt_ps = P["ptv"]
    pv_ps = P["bigB"]

    Xv = X[:, :].rearrange("p (i d) -> p i d", d=D)
    XPv = XP[:, :].rearrange("p (c t) -> p c t", t=S)
    OFv = OF[:, :].rearrange("p (c t) -> p c t", t=S)
    WOv = WO[:, :].rearrange("p (c d) -> p c d", d=D)

    c_x = ctx.dma("sp", "ld_x", Xv, xin.rearrange("(i p) d -> p i d", p=128))
    ctx.dma("sp", "ld_c", MASK[:, :], mask_d)
    ctx.dma("sp", "ld_c", GB[:, :], gb_d)
    c_const = ctx.dma("sp", "ld_c", BBt[:, :], bb_d)
    c_wo = ctx.dma("pool", "ld_wo", WOv, wo_d.rearrange("(c p) d -> p c d", p=128))
    cw = {}
    for nmd, dst, src in (("q", WQ, wq_d), ("k", WK, wk_d), ("v", WV, wv_d)):
        cw[(0, nmd)] = ctx.dma(
            "pool", "lw0", dst[:, 0:1024].rearrange("p (c n) -> p c n", n=128),
            src[0].rearrange("(c p) n -> p c n", p=128))
    ctx.op("dve", lambda E: E.memset(EPS5[:, :], 1e-5), inc="dve")
    c_eps = ctx.op("dve", lambda E: E.memset(EPS3[:, :], 1e-3), inc="dve")

    ctx.wait("dve", "ld_x", c_x)
    ctx.wait("dve", "ld_c", c_const)
    ctx.wait("pool", "ld_x", c_x)
    ctx.wait("pool", "ld_c", c_const)
    ctx.wait("act", "dve", c_eps)
    pe_tile_done = {}
    c_pt_evac = 0
    for i in range(NT):
        Xi = X[:, i * D:(i + 1) * D]
        c_mv, c_rs = emit_ln_stats(ctx, BN6, SC1[:, :], Xi,
                                   MV[:, 2 * i:2 * i + 2],
                                   MV[:, 2 * i + 1:2 * i + 2],
                                   RS[:, i:i + 1], EPS5[:, :])
        xt = XT[:, (i % 2) * D:(i % 2 + 1) * D]
        if i >= 2:
            ctx.wait("pool", "pe", pe_tile_done[i - 2])
        ctx.wait("pool", "dve", c_mv)
        ctx.wait("pool", "act", c_rs)
        c_ap = ctx.op("pool", lambda E, xt=xt, Xi=Xi, i=i:
                      E.tensor_scalar(xt, Xi, MV[:, 2 * i:2 * i + 1],
                                      RS[:, i:i + 1], ALU.subtract, ALU.mult),
                      inc="pool")
        ctx.wait("pe", "pool", c_ap)
        ctx.wait("pe", "ld_c", c_const)
        ctx.wait("pe", "dve", c_pt_evac)
        for c in range(NCH):
            r = ctx.op("pe", lambda E, c=c, xt=xt:
                       E.transpose(pt_ps[:, c * 128:(c + 1) * 128],
                                   xt[:, c * 128:(c + 1) * 128], IDN[:, :]),
                       inc="pe" if c == NCH - 1 else None)
        pe_tile_done[i] = r
        ctx.wait("dve", "pe", pe_tile_done[i])
        c_pt_evac = ctx.op(
            "dve", lambda E, i=i: E.tensor_copy(
                XPv[:, :, 128 * i:128 * (i + 1)],
                pt_ps[:, :].rearrange("p (c t) -> p c t", t=128)),
            inc="dve")

    pair_pe_done = {}
    for p in range(8):
        if p + 1 < 8:
            if p >= 1:
                ctx.wait("pool", "pe", pair_pe_done[p - 1])
            b = (p + 1) % 2
            for nmd, dstT, srcT in (("q", WQ, wq_d), ("k", WK, wk_d),
                                    ("v", WV, wv_d)):
                cw[(p + 1, nmd)] = ctx.dma(
                    "pool", f"lw{b}",
                    dstT[:, 1024 * b:1024 * (b + 1)]
                    .rearrange("p (c n) -> p c n", n=128),
                    srcT[p + 1].rearrange("(c p) n -> p c n", p=128))
        ctx.wait("pe", f"lw{p % 2}", cw[(p, "v")])
        ctx.wait("pe", "dve", c_pt_evac)
        bofs = (p % 2) * 1024
        for which, wsb, scale in (("q", WQ, 0.125), ("k", WK, None)):
            eng = "act" if which == "q" else "dve"
            for tb in range(4):
                ps = proj_ps[:, (tb % 2) * 512:(tb % 2) * 512 + 512]
                ctx.wait("pe", "act", ctx.cnt["act"])
                ctx.wait("pe", "dve", ctx.cnt["dve"])
                for c in range(NCH):
                    cg = ctx.op("pe", lambda E, c=c, tb=tb, wsb=wsb, ps=ps, bofs=bofs:
                                E.matmul(ps, wsb[:, bofs + c * 128:bofs + (c + 1) * 128],
                                         XPv[:, c, 512 * tb:512 * (tb + 1)],
                                         start=(c == 0), stop=(c == NCH - 1)),
                                inc="pe" if c == NCH - 1 else None)
                ctx.wait(eng, "pe", cg)
                if which == "q":
                    ctx.op("act", lambda E, tb=tb, ps=ps:
                           E.activation(QP[:, 512 * tb:512 * (tb + 1)], ps,
                                        AF.Copy, scale=0.125), inc="act")
                else:
                    ctx.op("dve", lambda E, tb=tb, ps=ps:
                           E.tensor_copy(KP[:, 512 * tb:512 * (tb + 1)], ps),
                           inc="dve")
        for tt in range(NT):
            ps = proj_ps[:, (tt % 2) * 512:(tt % 2) * 512 + 128]
            ctx.wait("pe", "act", ctx.cnt["act"])
            ctx.wait("pe", "dve", ctx.cnt["dve"])
            for c in range(NCH):
                cg = ctx.op("pe", lambda E, c=c, tt=tt, ps=ps, bofs=bofs:
                            E.matmul(ps, XPv[:, c, 128 * tt:128 * (tt + 1)],
                                     WV[:, bofs + c * 128:bofs + (c + 1) * 128],
                                     start=(c == 0), stop=(c == NCH - 1)),
                            inc="pe" if c == NCH - 1 else None)
            ctx.wait("dve", "pe", cg)
            ctx.op("dve", lambda E, tt=tt, ps=ps:
                   E.tensor_copy(VP[:, 128 * tt:128 * (tt + 1)], ps), inc="dve")
        c_qkv_dve = ctx.cnt["dve"]
        c_qkv_act = ctx.cnt["act"]

        for par in range(2):
            q_h = QP[64 * par:64 * (par + 1), :]
            k_h = KP[64 * par:64 * (par + 1), :]
            for i in range(NT):
                jmax = i // 4
                span = 512 * (jmax + 1)
                ctx.wait("pe", "act", ctx.cnt["act"])
                if par == 0 and i == 0:
                    ctx.wait("pe", "dve", c_qkv_dve)
                    ctx.wait("pe", "act", c_qkv_act)
                for j in range(jmax + 1):
                    c_sc = ctx.op("pe", lambda E, i=i, j=j, q_h=q_h, k_h=k_h:
                                  E.matmul(sc_ps[:, 512 * j:512 * (j + 1)],
                                           q_h[:, 128 * i:128 * (i + 1)],
                                           k_h[:, 512 * j:512 * (j + 1)],
                                           start=True, stop=True),
                                  inc="pe" if j == jmax else None)
                ctx.wait("dve", "pe", c_sc)
                ctx.op("dve", lambda E, i=i, jmax=jmax:
                       E.tensor_tensor(sc_ps[:, 512 * jmax:512 * (jmax + 1)],
                                       sc_ps[:, 512 * jmax:512 * (jmax + 1)],
                                       MASK[:, 512 * (i % 4):512 * (i % 4 + 1)],
                                       ALU.add), inc="dve")
                ctx.wait("dve", "dve")
                c_ngm = ctx.op("dve", lambda E, span=span:
                               E.reduce_max(NGM[:, :], sc_ps[:, 0:span],
                                            axis=AX.X, negate=True), inc="dve")
                ctx.wait("act", "dve", c_ngm)
                c_exp = ctx.op("act", lambda E, span=span:
                               E.activation(PT[:, 0:span], sc_ps[:, 0:span],
                                            AF.Exp, bias=NGM[:, :],
                                            accum_out=SM[:, :]), inc="act")
                ctx.wait("dve", "act", c_exp)
                c_rec = ctx.op("dve", lambda E: E.reciprocal(REC[:, :], SM[:, :]),
                               inc="dve")
                ctx.wait("dve", "dve", c_rec)
                c_nrm = ctx.op("dve", lambda E, span=span:
                               E.tensor_scalar(PT[:, 0:span], PT[:, 0:span],
                                               REC[:, :], None, ALU.mult),
                               inc="dve")
                nblk = i + 1
                for w in range((nblk + 7) // 8):
                    lo, hi = 8 * w, min(8 * w + 8, nblk)
                    ctx.wait("pe", "dve", c_nrm if w == 0 else c_ptt)
                    for c in range(lo, hi):
                        c_tp = ctx.op("pe", lambda E, c=c, lo=lo:
                                      E.transpose(pt_ps[:, 128 * (c - lo):128 * (c - lo + 1)],
                                                  PT[:, 128 * c:128 * (c + 1)],
                                                  IDN[:, :]),
                                      inc="pe" if c == hi - 1 else None)
                    ctx.wait("dve", "pe", c_tp)
                    c_ptt = ctx.op("dve", lambda E, lo=lo, hi=hi: E.tensor_copy(
                        PTT[:, 128 * lo:128 * hi],
                        pt_ps[:, 0:128 * (hi - lo)]), inc="dve")
                eng = "act" if (par + i) % 2 else "dve"
                ctx.wait("pe", "dve", c_ptt)
                ctx.wait("pe", eng, ctx.cnt[eng])
                for c in range(nblk):
                    c_pv = ctx.op("pe", lambda E, c=c, par=par, i=i:
                                  E.matmul(pv_ps[0:64, 1024:1152],
                                           VP[:, 128 * c + 64 * par:128 * c + 64 * par + 64],
                                           PTT[:, 128 * c:128 * (c + 1)],
                                           start=(c == 0), stop=(c == i)),
                                  inc="pe" if c == i else None)
                ctx.wait(eng, "pe", c_pv)
                dst = OF[64 * par:64 * (par + 1),
                         S * p + 128 * i:S * p + 128 * (i + 1)]
                if eng == "act":
                    ctx.op("act", lambda E, dst=dst:
                           E.copy(dst, pv_ps[0:64, 1024:1152]), inc="act")
                else:
                    ctx.op("dve", lambda E, dst=dst:
                           E.tensor_copy(dst, pv_ps[0:64, 1024:1152]), inc="dve")
        pair_pe_done[p] = ctx.cnt["pe"]
    c_of_dve = ctx.cnt["dve"]
    c_of_act = ctx.cnt["act"]

    ctx.wait("pe", "ld_wo", c_wo)
    ctx.wait("pe", "dve", c_of_dve)
    ctx.wait("pe", "act", c_of_act)
    c_y = 0
    c_xu = 0
    for i in range(NT):
        Xi = X[:, i * D:(i + 1) * D]
        ctx.wait("pe", "dve", c_y)
        for hh in range(2):
            for c in range(NCH):
                c_d = ctx.op("pe", lambda E, c=c, hh=hh, i=i:
                             E.matmul(proj_ps[:, 512 * hh:512 * (hh + 1)],
                                      OFv[:, c, 128 * i:128 * (i + 1)],
                                      WOv[:, c, 512 * hh:512 * (hh + 1)],
                                      start=(c == 0), stop=(c == NCH - 1)),
                             inc="pe" if (hh == 1 and c == NCH - 1) else None)
        ctx.wait("pool", "dve", c_y)
        c_tmp = ctx.op("pool", lambda E, Xi=Xi, i=i:
                       E.tensor_scalar(TMP[:, :], Xi, MV[:, 2 * i:2 * i + 1],
                                       RS[:, i:i + 1], ALU.subtract, ALU.mult),
                       inc="pool")
        ctx.wait("dve", "pe", c_d)
        ctx.wait("dve", "pool", c_tmp)
        c_y = ctx.op("dve", lambda E: E.tensor_add(Y[:, :], TMP[:, :],
                                                   proj_ps[:, 0:1024]), inc="dve")
        ctx.wait("dve", "dve")
        ctx.op("dve", lambda E: E.bn_stats(BN6[:, 0:6], Y[:, 0:512]))
        ctx.op("dve", lambda E: E.bn_stats(BN6[:, 6:12], Y[:, 512:1024]),
               inc="dve")
        ctx.wait("dve", "dve")
        c_mv2 = ctx.op("dve", lambda E: E.bn_aggr(MV2[:, :], BN6[:, 0:12]),
                       inc="dve")
        ctx.wait("act", "dve", c_mv2)
        ctx.op("act", lambda E: E.activation(SC1[:, :], MV2[:, 1:2],
                                             AF.Ln, bias=EPS3[:, :]), inc="act")
        ctx.wait("act", "act")
        c_rs2 = ctx.op("act", lambda E: E.activation(RS2[:, :], SC1[:, :],
                                                     AF.Exp, scale=-0.5),
                       inc="act")
        ctx.wait("dve", "act", c_rs2)
        ctx.wait("dve", "pool", c_xu)
        ctx.op("dve", lambda E: E.tensor_scalar(YN[:, :], Y[:, :], MV2[:, 0:1],
                                                RS2[:, :], ALU.subtract,
                                                ALU.mult), inc="dve")
        ctx.wait("pool", "dve", ctx.cnt["dve"])
        ctx.op("pool", lambda E: E.tensor_mul(YN[:, :], YN[:, :], GB[:, :]),
               inc="pool")
        ctx.wait("pool", "pool")
        ctx.op("pool", lambda E: E.tensor_add(YN[:, :], YN[:, :], BBt[:, :]),
               inc="pool")
        ctx.wait("pool", "pool")
        c_xu = ctx.op("pool", lambda E, Xi=Xi: E.tensor_add(Xi, Xi, YN[:, :]),
                      inc="pool")




def emit_ffn0(nc, ctx, P, C):
    rw_d, gup_d, dwn_d = P["f0_rw"], P["f0_gup"], P["f0_dwn"]
    shgu_d, shdw_d = P["f0_shgu"], P["f0_shdw"]
    X, IDN = P["X"], P["IDN"]

    XP = C.t("XP", [128, NCH * S], BF16)
    GU = C.t("GU", [128, NCH * 2048], BF16)
    DW = C.t("DW", [128, 2 * NCH * D], BF16)
    Hb = C.t("Hb", [128, 8 * 2048], BF16)
    SG = C.t("SG", [128, 2048], BF16)
    XT = C.t("XT", [128, 2 * D], BF16)
    RW = C.t("RW", [128, NCH * 16], BF16)
    LG = C.t("LG", [128, NT * 16], F32)
    WF = C.t("WF", [128, NT * 16], F32)
    M1 = C.t("M1", [128, 1], F32)
    M2 = C.t("M2", [128, 1], F32)
    MSK1 = C.t("MSK1", [128, 16], F32)
    LG2 = C.t("LG2", [128, 16], F32)
    GTS = C.t("GTS", [128, 16], F32)
    MV = C.t("MV", [128, 2 * NT], F32)
    RS = C.t("RS", [128, NT], F32)
    BN6 = C.t("BN6", [128, 12], F32)
    SC1 = C.t("SC1", [128, 1], F32)
    EPS = C.t("EPS", [128, 1], F32)

    pt_ps = P["ptv"]
    GA = P["bigA"]
    GB = P["bigB"]
    m_ps = P["bigA"]

    B = {"X": X, "XP": XP, "XT": XT, "pt_ps": pt_ps, "BN6": BN6,
         "SC1": SC1, "MV": MV, "RS": RS, "EPS": EPS, "IDN": IDN,
         "Xv": X[:, :].rearrange("p (i d) -> p i d", d=D),
         "XPv": XP[:, :].rearrange("p (c t) -> p c t", t=S)}
    XPv = B["XPv"]
    GUv = GU[:, :].rearrange("p (c h) -> p c h", h=2048)

    ctx.dma("pool", "ld_rw", RW[:, :].rearrange("p (c n) -> p c n", n=16),
            rw_d.rearrange("(c p) n -> p c n", p=128))

    def gu_src(ei):
        return shgu_d[ei] if ei < 2 else gup_d[ei - 2]

    def dw_src(ei):
        return shdw_d[ei] if ei < 2 else dwn_d[ei - 2]

    c_gu = {0: ctx.dma("pool", "lgu",
                       GUv, gu_src(0).rearrange("(c p) h -> p c h", p=128))}
    c_dw = {0: ctx.dma("pool", "ldw0",
                       DW[:, 0:NCH * D].rearrange("p (c h) -> p c h", h=D),
                       dw_src(0).rearrange("(c p) h -> p c h", h=D, p=128))}

    c_xp = emit_load_ln_xp(ctx, nc, B, 1e-5, load_x=False)

    ctx.wait("pe", "ld_c", ctx.cnt["ld_c"])
    ctx.wait("pe", "ld_rw", ctx.cnt["ld_rw"])
    for i in range(NT):
        ctx.wait("pe", "dve", ctx.cnt["dve"])
        for c in range(NCH):
            r = ctx.op("pe", lambda E, c=c, i=i:
                       E.matmul(m_ps[:, 0:16], XPv[:, c, 128 * i:128 * (i + 1)],
                                RW[:, 16 * c:16 * (c + 1)],
                                start=(c == 0), stop=(c == NCH - 1)),
                       inc="pe" if c == NCH - 1 else None)
        ctx.wait("dve", "pe", r)
        ctx.op("dve", lambda E, i=i:
               E.tensor_copy(LG[:, 16 * i:16 * (i + 1)], m_ps[:, 0:16]),
               inc="dve")
    ctx.wait("act", "dve", ctx.cnt["dve"])
    c_wf = 0
    for i in range(NT):
        lg = LG[:, 16 * i:16 * (i + 1)]
        ctx.wait("act", "dve", c_wf)
        ctx.wait("dve", "dve")
        c_m1 = ctx.op("dve", lambda E, lg=lg:
                      E.reduce_max(M1[:, :], lg, axis=AX.X), inc="dve")
        ctx.wait("dve", "dve", c_m1)
        ctx.op("dve", lambda E, lg=lg:
               E.tensor_scalar(MSK1[:, :], lg, M1[:, :], None, ALU.is_ge),
               inc="dve")
        ctx.wait("dve", "dve")
        ctx.op("dve", lambda E, lg=lg:
               E.scalar_tensor_tensor(LG2[:, :], MSK1[:, :], NEG, lg,
                                      ALU.mult, ALU.add), inc="dve")
        ctx.wait("dve", "dve")
        c_m2 = ctx.op("dve", lambda E:
                      E.reduce_max(M2[:, :], LG2[:, :], axis=AX.X), inc="dve")
        ctx.wait("dve", "dve", c_m2)
        c_msk = ctx.op("dve", lambda E, lg=lg:
                       E.tensor_scalar(MSK1[:, :], lg, M2[:, :], None,
                                       ALU.is_ge), inc="dve")
        c_sig = ctx.op("act", lambda E, lg=lg:
                       E.activation(GTS[:, :], lg, AF.Sigmoid), inc="act")
        ctx.wait("dve", "act", c_sig)
        ctx.wait("dve", "dve", c_msk)
        c_wf = ctx.op("dve", lambda E, i=i:
                      E.tensor_mul(WF[:, 16 * i:16 * (i + 1)], MSK1[:, :],
                                   GTS[:, :]), inc="dve")
    c_wf_all = c_wf

    c_pre = ctx.cnt["dve"]
    c_dph = {0: c_pre, 1: c_pre}
    r_d_last = 0
    prev_mul7 = 0
    for ei in range(18):
        real = ei >= 2
        dwbuf = ei % 2
        ctx.wait("pe", "lgu", c_gu[ei])
        ctx.wait("pe", f"ldw{dwbuf}", c_dw[ei])
        if ei == 0:
            ctx.wait("pe", "dve", c_xp)
        dwv = DW[:, dwbuf * NCH * D:(dwbuf + 1) * NCH * D] \
            .rearrange("p (c h) -> p c h", h=D)
        if ei + 1 < 18:
            ctx.wait("pool", "pe", ctx.cnt["pe"])
            c_dw[ei + 1] = ctx.dma(
                "pool", f"ldw{(ei + 1) % 2}",
                DW[:, ((ei + 1) % 2) * NCH * D:((ei + 1) % 2 + 1) * NCH * D]
                .rearrange("p (c h) -> p c h", h=D),
                dw_src(ei + 1).rearrange("(c p) h -> p c h", p=128))
        c_silu = {}
        c_mul = {}
        for f in range(NCH):
            if f == 0:
                ctx.wait("pe", "dve", max(c_dph.values()))
            else:
                ctx.wait("pe", "act", c_silu[f - 1])
            for c in range(NCH):
                for tq in range(4):
                    ctx.op("pe", lambda E, c=c, f=f, tq=tq:
                           E.matmul(GA[:, 512 * tq:512 * (tq + 1)],
                                    GUv[:, c, 128 * f:128 * (f + 1)],
                                    XPv[:, c, 512 * tq:512 * (tq + 1)],
                                    start=(c == 0), stop=(c == NCH - 1)),
                           inc="pe" if (c == NCH - 1 and tq == 3) else None)
            r_g = ctx.cnt["pe"]
            if f >= 1:
                ctx.wait("pe", "dve", c_mul[f - 1])
            for c in range(NCH):
                for tq in range(4):
                    ctx.op("pe", lambda E, c=c, f=f, tq=tq:
                           E.matmul(GB[:, 512 * tq:512 * (tq + 1)],
                                    GUv[:, c, 1024 + 128 * f:1024 + 128 * (f + 1)],
                                    XPv[:, c, 512 * tq:512 * (tq + 1)],
                                    start=(c == 0), stop=(c == NCH - 1)),
                           inc="pe" if (c == NCH - 1 and tq == 3) else None)
            r_u = ctx.cnt["pe"]
            ctx.wait("act", "pe", r_g)
            if f == 0:
                ctx.wait("act", "dve", prev_mul7)
            c_silu[f] = ctx.op("act", lambda E:
                               E.activation(SG[:, :], GA[:, 0:2048], AF.Silu),
                               inc="act")
            ctx.wait("dve", "pe", r_u)
            if f == 0:
                ctx.wait("dve", "pe", r_d_last)
            ctx.wait("dve", "act", c_silu[f])
            c_mul[f] = ctx.op("dve", lambda E, f=f:
                              E.tensor_mul(Hb[:, 2048 * f:2048 * (f + 1)],
                                           GB[:, 0:2048], SG[:, :]), inc="dve")
        prev_mul7 = c_mul[NCH - 1]
        if ei + 1 < 18:
            ctx.wait("pool", "pe", r_u)
            c_gu[ei + 1] = ctx.dma(
                "pool", "lgu", GUv,
                gu_src(ei + 1).rearrange("(c p) h -> p c h", p=128))
        ctx.wait("pe", "act", c_silu[NCH - 1])
        ctx.wait("pe", "dve", c_mul[NCH - 1])
        for it in range(NT):
            par = it % 2
            base = par * 1024
            ctx.wait("pe", "dve", c_dph[par])
            for f in range(NCH):
                for dh in range(2):
                    ctx.op("pe", lambda E, f=f, dh=dh, it=it, base=base, dwv=dwv:
                           E.matmul(GA[:, base + 512 * dh:base + 512 * (dh + 1)],
                                    Hb[:, 2048 * f + 128 * it:2048 * f + 128 * (it + 1)],
                                    dwv[:, f, 512 * dh:512 * (dh + 1)],
                                    start=(f == 0), stop=(f == NCH - 1)),
                           inc="pe" if (f == NCH - 1 and dh == 1) else None)
            r_d_last = ctx.cnt["pe"]
            ctx.wait("dve", "pe", r_d_last)
            Xi = X[:, 1024 * it:1024 * (it + 1)]
            if real:
                wcol = WF[:, 16 * it + (ei - 2):16 * it + (ei - 2) + 1]
                c_dph[par] = ctx.op(
                    "dve", lambda E, Xi=Xi, base=base, wcol=wcol:
                    E.scalar_tensor_tensor(Xi, GA[:, base:base + 1024], wcol, Xi,
                                           ALU.mult, ALU.add), inc="dve")
            else:
                c_dph[par] = ctx.op(
                    "dve", lambda E, Xi=Xi, base=base:
                    E.tensor_add(Xi, Xi, GA[:, base:base + 1024]), inc="dve")




def emit_attn1(nc, ctx, P, C):
    w3_d, wo_d = P["a1_w3"], P["a1_wo"]
    X, IDN = P["X"], P["IDN"]

    XQ = C.t("XQ", [128, NCH * S], BF16)
    KF = C.t("KF", [128, NT * D], BF16)
    VT = C.t("VT", [128, NT * D], BF16)
    WO = C.t("WO", [128, NCH * D], BF16)
    W3 = C.t("W3", [128, 3 * 512], BF16)
    KVZ = C.t("KVZ", [128, 16 * 65], BF16)
    ONES = C.t("ONES", [128, 1], BF16)
    MN = C.t("MN", [128, D], F32)
    EXs = C.t("EXs", [128, D], BF16)
    RL = C.t("RL", [128, D], BF16)
    QT = C.t("QT", [128, D], BF16)
    XT = C.t("XT", [128, 2 * D], BF16)
    DN = C.t("DN", [128, 1], F32)
    RC = C.t("RC", [128, 1], F32)
    EPS6 = C.t("EPS6", [128, 1], F32)
    MV = C.t("MV", [128, 2 * NT], F32)
    RS = C.t("RS", [128, NT], F32)
    BN6 = C.t("BN6", [128, 12], F32)
    SC1 = C.t("SC1", [128, 1], F32)
    EPS = C.t("EPS", [128, 1], F32)

    pt_ps = P["ptv"]
    p_ps = P["bigB"]
    d_ps = P["bigB"]
    nd_ps = P["bigA"]
    KZ = P["bigA"]

    B = {"X": X, "XP": XQ, "XT": XT, "pt_ps": pt_ps, "BN6": BN6,
         "SC1": SC1, "MV": MV, "RS": RS, "EPS": EPS, "IDN": IDN,
         "Xv": X[:, :].rearrange("p (i d) -> p i d", d=D),
         "XPv": XQ[:, :].rearrange("p (c t) -> p c t", t=S)}
    XPv = B["XPv"]
    WOv = WO[:, :].rearrange("p (c d) -> p c d", d=D)

    c_w3 = ctx.dma("pool", "ld_w", W3[:, :], w3_d)
    c_wo = ctx.dma("pool", "ld_wo", WOv, wo_d.rearrange("(c p) d -> p c d", p=128))
    ctx.op("dve", lambda E: E.memset(ONES[:, :], 1.0), inc="dve")
    ctx.op("dve", lambda E: E.memset(EPS6[:, :], 1e-6), inc="dve")
    c_ones = ctx.cnt["dve"]

    c_xp = emit_load_ln_xp(ctx, nc, B, 1e-5, load_x=False)

    ctx.wait("pe", "ld_w", c_w3)
    c_evac = 0
    c_pt = ctx.cnt["dve"]
    for i in range(NT):
        for m, nm in ((0, "q"), (1, "k"), (2, "v")):
            ctx.wait("pe", "dve", c_evac)
            ctx.wait("pe", "act", ctx.cnt["act"])
            for par in range(2):
                for j in range(8):
                    h = 2 * j + par
                    pbp = 64 * par
                    pc = 512 * par + 64 * j
                    r = ctx.op("pe", lambda E, h=h, pbp=pbp, pc=pc, m=m, i=i:
                               E.matmul(p_ps[:, pc:pc + 64],
                                        XPv[pbp:pbp + 64, h // 2, 128 * i:128 * (i + 1)],
                                        W3[pbp:pbp + 64, 512 * m + 64 * (h // 2):
                                           512 * m + 64 * (h // 2) + 64],
                                        start=True, stop=True),
                               inc="pe" if (par == 1 and j == 7) else None)
            if m < 2:
                ctx.wait("dve", "pe", r)
                c_mn = ctx.op("dve", lambda E:
                              E.tensor_scalar_min(MN[:, :], p_ps[:, 0:1024], 0.0),
                              inc="dve")
                c_rl = ctx.op("dve", lambda E:
                              E.tensor_scalar_max(RL[:, :], p_ps[:, 0:1024], 0.0),
                              inc="dve")
                ctx.wait("act", "dve", c_mn)
                c_ex = ctx.op("act", lambda E:
                              E.activation(EXs[:, :], MN[:, :], AF.Exp),
                              inc="act")
                ctx.wait("dve", "act", c_ex)
                ctx.wait("dve", "dve", c_rl)
                dst = QT[:, :] if m == 0 else KF[:, D * i:D * (i + 1)]
                c_evac = ctx.op("dve", lambda E, dst=dst:
                                E.tensor_add(dst, EXs[:, :], RL[:, :]),
                                inc="dve")
            else:
                ctx.wait("dve", "pe", r)
                c_evac = ctx.op("dve", lambda E, i=i:
                                E.tensor_copy(VT[:, D * i:D * (i + 1)],
                                              p_ps[:, 0:1024]), inc="dve")
        for par in range(2):
            ctx.wait("pe", "dve", c_evac if par == 0 else c_pt)
            ctx.wait("pe", "dve", c_pt)
            for j in range(8):
                h = 2 * j + par
                pc = 512 * (h % 2) + 64 * (h // 2)
                r = ctx.op("pe", lambda E, j=j, pc=pc:
                           E.transpose(pt_ps[0:64, 128 * j:128 * (j + 1)],
                                       QT[:, pc:pc + 64], IDN[:, :]),
                           inc="pe" if j == 7 else None)
            ctx.wait("dve", "pe", r)
            c_pt = ctx.op("dve", lambda E, par=par, i=i: E.tensor_copy(
                XPv[64 * par:64 * par + 64, :, 128 * i:128 * (i + 1)],
                pt_ps[0:64, :].rearrange("p (c t) -> p c t", t=128)),
                inc="dve")
    c_proj = ctx.cnt["dve"]

    ctx.wait("pe", "dve", c_proj)
    ctx.wait("pe", "dve", c_ones)
    c_kvz_evac = 0
    for h in range(16):
        ko = (h % 2) * 65
        ctx.wait("pe", "dve", c_kvz_evac)
        pc = 512 * (h % 2) + 64 * (h // 2)
        for tt in range(NT):
            ctx.op("pe", lambda E, tt=tt, ko=ko, pc=pc:
                   E.matmul(KZ[0:64, ko:ko + 64],
                            KF[:, 1024 * tt + pc:1024 * tt + pc + 64],
                            VT[:, 1024 * tt + pc:1024 * tt + pc + 64],
                            start=(tt == 0), stop=(tt == NT - 1)))
        for tt in range(NT):
            r = ctx.op("pe", lambda E, tt=tt, ko=ko, pc=pc:
                       E.matmul(KZ[0:64, ko + 64:ko + 65],
                                KF[:, 1024 * tt + pc:1024 * tt + pc + 64],
                                ONES[:, :],
                                start=(tt == 0), stop=(tt == NT - 1)),
                       inc="pe" if tt == NT - 1 else None)
        ctx.wait("dve", "pe", r)
        ctx.op("dve", lambda E, h=h, ko=ko:
               E.tensor_copy(KVZ[0:64, 65 * h:65 * (h + 1)],
                             KZ[0:64, ko:ko + 65]), inc="dve")
        c_kvz_evac = ctx.op("dve", lambda E, h=h, ko=ko:
                            E.tensor_copy(KVZ[64:128, 65 * h:65 * (h + 1)],
                                          KZ[0:64, ko:ko + 65]), inc="dve")
    c_kvz = c_kvz_evac

    c_ct = 0
    for i in range(NT):
        for h in range(16):
            pbp = 64 * (h % 2)
            no = ((i * 16 + h) % 2) * 65
            ctx.wait("pe", "dve", c_ct)
            ctx.wait("pe", "dve", c_kvz)
            r = ctx.op("pe", lambda E, h=h, i=i, pbp=pbp, no=no:
                       E.matmul(nd_ps[:, 512 + no:512 + no + 65],
                                XPv[pbp:pbp + 64, h // 2, 128 * i:128 * (i + 1)],
                                KVZ[pbp:pbp + 64, 65 * h:65 * (h + 1)],
                                start=True, stop=True), inc="pe")
            ctx.wait("dve", "pe", r)
            ctx.op("dve", lambda E, no=no:
                   E.tensor_scalar(DN[:, :], nd_ps[:, 512 + no + 64:512 + no + 65],
                                   EPS6[:, :], None, ALU.add), inc="dve")
            ctx.wait("dve", "dve")
            ctx.op("dve", lambda E: E.reciprocal(RC[:, :], DN[:, :]), inc="dve")
            ctx.wait("dve", "dve")
            c_ct = ctx.op("dve", lambda E, i=i, h=h, no=no:
                          E.tensor_scalar(KF[:, 1024 * i + 64 * h:
                                             1024 * i + 64 * (h + 1)],
                                          nd_ps[:, 512 + no:512 + no + 64],
                                          RC[:, :], None, ALU.mult), inc="dve")

    VTv = VT[:, :].rearrange("p (c t) -> p c t", t=S)
    c_pt2 = c_pt
    for i in range(NT):
        ctx.wait("pe", "dve", c_ct)
        ctx.wait("pe", "dve", c_pt2)
        for c in range(NCH):
            r = ctx.op("pe", lambda E, c=c, i=i:
                       E.transpose(pt_ps[:, c * 128:(c + 1) * 128],
                                   KF[:, 1024 * i + 128 * c:1024 * i + 128 * (c + 1)],
                                   IDN[:, :]),
                       inc="pe" if c == NCH - 1 else None)
        ctx.wait("dve", "pe", r)
        c_pt2 = ctx.op("dve", lambda E, i=i: E.tensor_copy(
            VTv[:, :, 128 * i:128 * (i + 1)],
            pt_ps[:, :].rearrange("p (c t) -> p c t", t=128)), inc="dve")
    c_ctf = c_pt2

    ctx.wait("pe", "ld_wo", c_wo)
    ctx.wait("pe", "dve", c_ctf)
    c_y = 0
    for i in range(NT):
        ctx.wait("pe", "dve", c_y)
        for hh in range(2):
            for c in range(NCH):
                r = ctx.op("pe", lambda E, c=c, hh=hh, i=i:
                           E.matmul(d_ps[:, 512 * hh:512 * (hh + 1)],
                                    VTv[:, c, 128 * i:128 * (i + 1)],
                                    WOv[:, c, 512 * hh:512 * (hh + 1)],
                                    start=(c == 0), stop=(c == NCH - 1)),
                           inc="pe" if (hh == 1 and c == NCH - 1) else None)
        ctx.wait("dve", "pe", r)
        c_y = ctx.op("dve", lambda E, i=i:
                     E.tensor_add(X[:, 1024 * i:1024 * (i + 1)],
                                  X[:, 1024 * i:1024 * (i + 1)],
                                  d_ps[:, 0:1024]), inc="dve")




def emit_ffn1(nc, ctx, P, C):
    rw_d, br_d = P["f1_rw"], P["f1_br"]
    w1_d, b1_d, w2_d, b2_d = P["f1_w1"], P["f1_b1"], P["f1_w2"], P["f1_b2"]
    X, IDN = P["X"], P["IDN"]

    XP = C.t("XP", [128, NCH * S], BF16)
    W1 = C.t("W1", [128, NCH * 2048], BF16)
    W2 = C.t("W2", [128, 16 * D], BF16)
    Hb = C.t("Hb", [128, 2 * 16 * 512], BF16)
    XT = C.t("XT", [128, 2 * D], BF16)
    RW = C.t("RW", [128, NCH * 8], BF16)
    BR = C.t("BR", [128, 8], F32)
    B1 = C.t("B1", [128, 256], F32)
    B2 = C.t("B2", [8, D], BF16)
    SCO = C.t("SCO", [128, NT * 8], F32)
    OH = C.t("OH", [128, NT * 8], F32)
    OHB = C.t("OHB", [128, 8], BF16)
    OHT = C.t("OHT", [8, S], BF16)
    M1 = C.t("M1", [128, 1], F32)
    MV = C.t("MV", [128, 2 * NT], F32)
    RS = C.t("RS", [128, NT], F32)
    BN6 = C.t("BN6", [128, 12], F32)
    SC1 = C.t("SC1", [128, 1], F32)
    EPS = C.t("EPS", [128, 1], F32)

    pt_ps = P["ptv"]
    GA = P["bigA"]
    GB = P["bigB"]
    d_ps = P["bigB"]
    m_ps = P["bigA"]

    B = {"X": X, "XP": XP, "XT": XT, "pt_ps": pt_ps, "BN6": BN6,
         "SC1": SC1, "MV": MV, "RS": RS, "EPS": EPS, "IDN": IDN,
         "Xv": X[:, :].rearrange("p (i d) -> p i d", d=D),
         "XPv": XP[:, :].rearrange("p (c t) -> p c t", t=S)}
    XPv = B["XPv"]
    W1v = W1[:, :].rearrange("p (c h) -> p c h", h=2048)
    W2v = W2[:, :].rearrange("p (c h) -> p c h", h=D)

    ctx.dma("sp", "ld_c", BR[:, :], br_d)
    ctx.dma("sp", "ld_c", B1[:, :], b1_d)
    ctx.dma("pool", "ld_rw", RW[:, :].rearrange("p (c n) -> p c n", n=8),
            rw_d.rearrange("(c p) n -> p c n", p=128))
    ctx.dma("pool", "ld_rw", B2[:, :], b2_d)

    def w1_src(se):
        return w1_d[se // 2][:, 2048 * (se % 2):2048 * (se % 2 + 1)]

    def w2_src(se):
        return w2_d[se // 2][2048 * (se % 2):2048 * (se % 2 + 1), :]

    c_w1 = {0: ctx.dma("pool", "lw1", W1v,
                       w1_src(0).rearrange("(c p) h -> p c h", p=128))}
    c_w2 = {0: ctx.dma("pool", "lw2", W2v,
                       w2_src(0).rearrange("(c p) h -> p c h", p=128))}

    c_xp = emit_load_ln_xp(ctx, nc, B, 1e-5, load_x=False)

    ctx.wait("pe", "ld_c", ctx.cnt["ld_c"])
    ctx.wait("pe", "ld_rw", ctx.cnt["ld_rw"])
    for i in range(NT):
        ctx.wait("pe", "dve", ctx.cnt["dve"])
        for c in range(NCH):
            r = ctx.op("pe", lambda E, c=c, i=i:
                       E.matmul(m_ps[:, 0:8], XPv[:, c, 128 * i:128 * (i + 1)],
                                RW[:, 8 * c:8 * (c + 1)],
                                start=(c == 0), stop=(c == NCH - 1)),
                       inc="pe" if c == NCH - 1 else None)
        ctx.wait("dve", "pe", r)
        ctx.op("dve", lambda E, i=i:
               E.scalar_tensor_tensor(SCO[:, 8 * i:8 * (i + 1)], m_ps[:, 0:8],
                                      1.0, BR[:, :], ALU.mult, ALU.add),
               inc="dve")
    c_pt = ctx.cnt["dve"]
    for i in range(NT):
        sco = SCO[:, 8 * i:8 * (i + 1)]
        ctx.wait("dve", "dve")
        c_m1 = ctx.op("dve", lambda E, sco=sco:
                      E.reduce_max(M1[:, :], sco, axis=AX.X), inc="dve")
        ctx.wait("dve", "dve", c_m1)
        ctx.op("dve", lambda E, sco=sco, i=i:
               E.tensor_scalar(OH[:, 8 * i:8 * (i + 1)], sco, M1[:, :], None,
                               ALU.is_ge), inc="dve")
        ctx.wait("dve", "dve")
        c_ohb = ctx.op("dve", lambda E, i=i:
                       E.tensor_copy(OHB[:, :], OH[:, 8 * i:8 * (i + 1)]),
                       inc="dve")
        ctx.wait("pe", "dve", c_ohb)
        ctx.wait("pe", "dve", c_pt)
        r = ctx.op("pe", lambda E:
                   E.transpose(pt_ps[0:8, 0:128], OHB[:, :], IDN[:, :]),
                   inc="pe")
        ctx.wait("dve", "pe", r)
        c_pt = ctx.op("dve", lambda E, i=i:
                      E.tensor_copy(OHT[0:8, 128 * i:128 * (i + 1)],
                                    pt_ps[0:8, 0:128]), inc="dve")
    c_oh_all = ctx.cnt["dve"]

    c_pre = ctx.cnt["dve"]
    c_dph = {0: c_pre, 1: c_pre}
    gelu_hist = [0, 0]
    c_hb_done = {0: 0, 1: 0}
    state = {"r_d": 0}

    def emit_down(se, tq):
        ex = se // 2
        hbo = (tq % 2) * 8192
        ctx.wait("pe", "act", c_hb_done[tq % 2])
        for ts in range(4):
            it = 4 * tq + ts
            par = it % 2
            base = par * 1024
            ctx.wait("pe", "dve", c_dph[par])
            for f in range(16):
                for dh in range(2):
                    ctx.op("pe", lambda E, f=f, dh=dh, ts=ts, base=base, hbo=hbo:
                           E.matmul(GB[:, base + 512 * dh:base + 512 * (dh + 1)],
                                    Hb[:, hbo + 512 * f + 128 * ts:
                                       hbo + 512 * f + 128 * (ts + 1)],
                                    W2v[:, f, 512 * dh:512 * (dh + 1)],
                                    start=(f == 0), stop=(f == 15)),
                           inc="pe" if (f == 15 and dh == 1) else None)
            state["r_d"] = ctx.cnt["pe"]
            ctx.wait("dve", "pe", state["r_d"])
            Xi = X[:, 1024 * it:1024 * (it + 1)]
            ocol = OH[:, 8 * it + ex:8 * it + ex + 1]
            c_dph[par] = ctx.op(
                "dve", lambda E, Xi=Xi, base=base, ocol=ocol:
                E.scalar_tensor_tensor(Xi, GB[:, base:base + 1024], ocol, Xi,
                                       ALU.mult, ALU.add), inc="dve")

    for se in range(16):
        ctx.wait("pe", "lw1", c_w1[se])
        ctx.wait("pe", "lw2", c_w2[se])
        if se == 0:
            ctx.wait("pe", "dve", c_pre)
        for tq in range(4):
            hbo = (tq % 2) * 8192
            for f in range(16):
                hbase = (f % 2) * 512
                ctx.wait("pe", "act", gelu_hist[-2])
                for c in range(NCH):
                    ctx.op("pe", lambda E, c=c, f=f, tq=tq, hbase=hbase:
                           E.matmul(GA[:, hbase:hbase + 512],
                                    W1v[:, c, 128 * f:128 * (f + 1)],
                                    XPv[:, c, 512 * tq:512 * (tq + 1)],
                                    start=(c == 0), stop=(c == NCH - 1)),
                           inc="pe" if c == NCH - 1 else None)
                r_h = ctx.cnt["pe"]
                ctx.wait("act", "pe", r_h)
                if f == 0:
                    ctx.wait("act", "pe", state["r_d"])
                bcol = 32 * (se // 2) + 16 * (se % 2) + f
                cg = ctx.op("act", lambda E, f=f, hbase=hbase, bcol=bcol, hbo=hbo:
                            E.activation(Hb[:, hbo + 512 * f:hbo + 512 * (f + 1)],
                                         GA[:, hbase:hbase + 512], AF.Gelu,
                                         bias=B1[:, bcol:bcol + 1]), inc="act")
                gelu_hist.append(cg)
            c_hb_done[tq % 2] = gelu_hist[-1]
            if tq == 3 and se + 1 < 16:
                ctx.wait("pool", "pe", ctx.cnt["pe"])
                c_w1[se + 1] = ctx.dma(
                    "pool", "lw1", W1v,
                    w1_src(se + 1).rearrange("(c p) h -> p c h", p=128))
            if tq >= 1:
                emit_down(se, tq - 1)
        emit_down(se, 3)
        if se + 1 < 16:
            ctx.wait("pool", "pe", ctx.cnt["pe"])
            c_w2[se + 1] = ctx.dma(
                "pool", "lw2", W2v,
                w2_src(se + 1).rearrange("(c p) h -> p c h", p=128))

    ctx.wait("pe", "ld_rw", ctx.cnt["ld_rw"])
    c_y = max(c_dph.values())
    for i in range(NT):
        ctx.wait("pe", "dve", c_y)
        for dh in range(2):
            r = ctx.op("pe", lambda E, i=i, dh=dh:
                       E.matmul(d_ps[:, 512 * dh:512 * (dh + 1)],
                                OHT[0:8, 128 * i:128 * (i + 1)],
                                B2[0:8, 512 * dh:512 * (dh + 1)],
                                start=True, stop=True),
                       inc="pe" if dh == 1 else None)
        ctx.wait("dve", "pe", r)
        c_y = ctx.op("dve", lambda E, i=i:
                     E.tensor_add(X[:, 1024 * i:1024 * (i + 1)],
                                  X[:, 1024 * i:1024 * (i + 1)],
                                  d_ps[:, 0:1024]), inc="dve")




def build_full():
    nc = bass.Bass("TRN2", target_bir_lowering=False)
    P = {}
    P["x"] = nc.dram_tensor("x", [S, D], F32, kind="ExternalInput").ap()
    P["idn"] = nc.dram_tensor("idn", [128, 128], BF16, kind="ExternalInput").ap()
    P["a0_wq"] = nc.dram_tensor("a0_wq", [8, D, 128], F32, kind="ExternalInput").ap()
    P["a0_wk"] = nc.dram_tensor("a0_wk", [8, D, 128], F32, kind="ExternalInput").ap()
    P["a0_wv"] = nc.dram_tensor("a0_wv", [8, D, 128], F32, kind="ExternalInput").ap()
    P["a0_wo"] = nc.dram_tensor("a0_wo", [D, D], F32, kind="ExternalInput").ap()
    P["a0_gb"] = nc.dram_tensor("a0_gb", [128, D], F32, kind="ExternalInput").ap()
    P["a0_bb"] = nc.dram_tensor("a0_bb", [128, D], F32, kind="ExternalInput").ap()
    P["a0_mask"] = nc.dram_tensor("a0_mask", [128, 4 * 512], BF16,
                                  kind="ExternalInput").ap()
    P["f0_rw"] = nc.dram_tensor("f0_rw", [D, 16], F32, kind="ExternalInput").ap()
    P["f0_gup"] = nc.dram_tensor("f0_gup", [16, D, 2048], F32,
                                 kind="ExternalInput").ap()
    P["f0_dwn"] = nc.dram_tensor("f0_dwn", [16, D, D], F32,
                                 kind="ExternalInput").ap()
    P["f0_shgu"] = nc.dram_tensor("f0_shgu", [2, D, 2048], F32,
                                  kind="ExternalInput").ap()
    P["f0_shdw"] = nc.dram_tensor("f0_shdw", [2, D, D], F32,
                                  kind="ExternalInput").ap()
    P["a1_w3"] = nc.dram_tensor("a1_w3", [128, 3 * 512], F32,
                                kind="ExternalInput").ap()
    P["a1_wo"] = nc.dram_tensor("a1_wo", [D, D], F32, kind="ExternalInput").ap()
    P["f1_rw"] = nc.dram_tensor("f1_rw", [D, 8], F32, kind="ExternalInput").ap()
    P["f1_br"] = nc.dram_tensor("f1_br", [128, 8], F32, kind="ExternalInput").ap()
    P["f1_w1"] = nc.dram_tensor("f1_w1", [8, D, 4096], F32,
                                kind="ExternalInput").ap()
    P["f1_b1"] = nc.dram_tensor("f1_b1", [128, 256], F32, kind="ExternalInput").ap()
    P["f1_w2"] = nc.dram_tensor("f1_w2", [8, 4096, D], F32,
                                kind="ExternalInput").ap()
    P["f1_b2"] = nc.dram_tensor("f1_b2", [8, D], F32, kind="ExternalInput").ap()
    yout = nc.dram_tensor("y", [S, D], F32, kind="ExternalOutput").ap()

    with ExitStack() as stk:
        slab = nc.alloc_sbuf_tensor("slab", [128, SLAB], U8)
        base = nc.lookup_mloc(slab).addr
        X = nc.alloc_sbuf_tensor_at("X", [128, NT * D], F32, offset=base)
        IDN = nc.alloc_sbuf_tensor_at("IDN", [128, 128], BF16, offset=base + 65536)
        P["X"] = X
        P["IDN"] = IDN
        P["bigA"] = nc.alloc_psum_tensor("bigA", [128, 2048], F32)
        P["bigB"] = nc.alloc_psum_tensor("bigB", [128, 2048], F32)
        ptv = nc.virtual_psum_tensor("ptv", [128, 1024], BF16)
        nc._tensor(ptv.name, ptv.shape, ptv.dtype, type="PSUM")
        mloc = nc.lookup_mloc(ptv)
        mloc.allocated = True
        mloc.addr = nc.lookup_mloc(P["bigB"]).addr + 6144
        P["ptv"] = ptv
        Xv = X[:, :].rearrange("p (i d) -> p i d", d=D)

        ctx = Ctx(nc)
        ctx.make_sems(stk, ["ld_x", "ld_c", "ld_wo", "ld_rw", "ld_w",
                            "lw0", "lw1", "lw2", "lgu", "ldw0", "ldw1", "st",
                            "pe", "dve", "act", "pool"])

        ctx.dma("sp", "ld_c", IDN[:, :], P["idn"])

        emit_attn0(nc, ctx, P, Carver(nc, base, PERSIST, SLAB, "a0"))
        ctx.barrier()
        emit_ffn0(nc, ctx, P, Carver(nc, base, PERSIST, SLAB, "f0"))
        ctx.barrier()
        emit_attn1(nc, ctx, P, Carver(nc, base, PERSIST, SLAB, "a1"))
        ctx.barrier()
        emit_ffn1(nc, ctx, P, Carver(nc, base, PERSIST, SLAB, "f1"))

        ctx.wait("sp", "dve", ctx.cnt["dve"])
        ctx.store(yout.rearrange("(i p) d -> p i d", p=128), Xv)
        ctx.wait("sp", "st", ctx.cnt["st"])

        with nc.Block() as block:
            ctx.replay(block)
    return nc




def _pairify(w):
    w = np.asarray(w, np.float32)
    return np.ascontiguousarray(
        np.transpose(w, (1, 0, 2)).reshape(D, D).reshape(D, 8, 128)
        .transpose(1, 0, 2))


def _prep_all(inp):
    p = np.arange(128)[:, None]
    f = np.arange(512)[None, :]
    mask = np.stack([np.where(f <= p + 128 * k, 0.0, NEG) for k in range(4)],
                    axis=1).reshape(128, 4 * 512).astype(BF)
    d = {
        "idn": np.eye(128, dtype=BF),
        "a0_wq": _pairify(inp["L0_attn_Wq"]),
        "a0_wk": _pairify(inp["L0_attn_Wk"]),
        "a0_wv": _pairify(inp["L0_attn_Wv"]),
        "a0_wo": np.ascontiguousarray(np.asarray(inp["L0_attn_Wo"], np.float32)),
        "a0_gb": np.ascontiguousarray(np.broadcast_to(
            np.asarray(inp["L0_attn_gamma"], np.float32), (128, D))),
        "a0_bb": np.ascontiguousarray(np.broadcast_to(
            np.asarray(inp["L0_attn_beta"], np.float32), (128, D))),
        "a0_mask": mask,
    }
    sgw = np.asarray(inp["L0_ffn_shared_gate_w"], np.float32).T
    suw = np.asarray(inp["L0_ffn_shared_up_w"], np.float32).T
    sdw = np.asarray(inp["L0_ffn_shared_down_w"], np.float32).T
    shgu = np.stack([
        np.concatenate([sgw[:, 0:1024], suw[:, 0:1024]], axis=1),
        np.concatenate([sgw[:, 1024:2048], suw[:, 1024:2048]], axis=1)])
    d.update({
        "f0_rw": np.ascontiguousarray(
            np.asarray(inp["L0_ffn_router_w"], np.float32).T),
        "f0_gup": np.ascontiguousarray(np.asarray(inp["L0_ffn_gate_up_w"],
                                                  np.float32)),
        "f0_dwn": np.ascontiguousarray(np.asarray(inp["L0_ffn_down_w"],
                                                  np.float32)),
        "f0_shgu": np.ascontiguousarray(shgu),
        "f0_shdw": np.ascontiguousarray(np.stack([sdw[0:1024], sdw[1024:2048]])),
    })
    w3 = np.zeros((128, 3 * 512), np.float32)
    for m, nm in ((0, "L1_attn_Wq"), (1, "L1_attn_Wk"), (2, "L1_attn_Wv")):
        w = np.asarray(inp[nm], np.float32)
        for h in range(16):
            pbp = 64 * (h % 2)
            w3[pbp:pbp + 64, 512 * m + 64 * (h // 2):512 * m + 64 * (h // 2) + 64] = w[h]
    d.update({
        "a1_w3": w3,
        "a1_wo": np.ascontiguousarray(np.asarray(inp["L1_attn_Wo"], np.float32)),
    })
    b1 = np.asarray(inp["L1_ffn_b1"], np.float32)
    b1sb = np.zeros((128, 256), np.float32)
    for ex in range(8):
        for col in range(32):
            b1sb[:, 32 * ex + col] = b1[ex, 128 * col:128 * (col + 1)]
    d.update({
        "f1_rw": np.ascontiguousarray(np.asarray(inp["L1_ffn_wr"], np.float32)),
        "f1_br": np.ascontiguousarray(np.broadcast_to(
            np.asarray(inp["L1_ffn_br"], np.float32), (128, 8))),
        "f1_w1": np.ascontiguousarray(np.asarray(inp["L1_ffn_w1"], np.float32)),
        "f1_b1": b1sb,
        "f1_w2": np.ascontiguousarray(np.asarray(inp["L1_ffn_w2"], np.float32)),
        "f1_b2": np.ascontiguousarray(np.asarray(inp["L1_ffn_b2"], np.float32)),
    })
    return d


def kernel(**inp):
    x = np.asarray(inp["L0_attn_x"], dtype=np.float32)
    nc = build_full()
    shared = _prep_all(inp)
    in_maps = [dict(shared, x=np.ascontiguousarray(x[b])) for b in range(8)]
    res = run_bass_kernel_spmd(nc, in_maps, core_ids=list(range(8)))
    return np.ascontiguousarray(
        np.stack([res.results[b]["y"] for b in range(8)], axis=0))


# version 04
# speedup vs baseline: 1.2991x; 1.0310x over previous
import numpy as np
import ml_dtypes
from contextlib import ExitStack
from collections import defaultdict

import concourse.bass as bass
import concourse.mybir as mybir
from concourse.bass_utils import run_bass_kernel_spmd

F32 = mybir.dt.float32
BF16 = mybir.dt.bfloat16
U8 = mybir.dt.uint8
AF = mybir.ActivationFunctionType
ALU = mybir.AluOpType
AX = mybir.AxisListType
BF = ml_dtypes.bfloat16

D = 1024
S = 2048
NT = S // 128
NCH = D // 128
NEG = -1e30

SLAB = 212736
PERSIST = 65792



class Ctx:

    def __init__(self, nc):
        self.nc = nc
        self.plan = []
        self.sems = {}
        self.cnt = defaultdict(int)

    def make_sems(self, stack, names):
        for n in names:
            self.sems[n] = stack.enter_context(self.nc.semaphore(n))

    def op(self, eng, emit, inc=None, n=1):
        if inc is None:
            self.plan.append((eng, emit))
            return None
        self.cnt[inc] += n
        s = self.sems[inc]
        self.plan.append((eng, lambda E, emit=emit, s=s, n=n: emit(E).then_inc(s, n)))
        return self.cnt[inc]

    def wait(self, eng, sem, val=None):
        v = self.cnt[sem] if val is None else val
        if v <= 0:
            return
        s = self.sems[sem]
        self.plan.append((eng, lambda E, s=s, v=v: E.wait_ge(s, v)))

    def dma(self, eng, sem, out, in_, **kw):
        return self.op(eng, lambda E: E.dma_start(out=out, in_=in_, **kw), inc=sem, n=16)

    def store(self, out, in_):
        return self.op("sp", lambda E: E.dma_start(out=out, in_=in_), inc="st", n=16)

    def barrier(self):
        for eng in ("sp", "pool", "pe", "dve", "act"):
            for s in list(self.cnt):
                self.wait(eng, s, self.cnt[s])

    def replay(self, block):
        plan = self.plan

        @block.sync
        def _(E):
            for g, fn in plan:
                if g == "sp":
                    fn(E)

        @block.gpsimd
        def _(E):
            for g, fn in plan:
                if g == "pool":
                    fn(E)

        @block.tensor
        def _(E):
            for g, fn in plan:
                if g == "pe":
                    fn(E)

        @block.vector
        def _(E):
            for g, fn in plan:
                if g == "dve":
                    fn(E)

        @block.scalar
        def _(E):
            for g, fn in plan:
                if g == "act":
                    fn(E)


class Carver:

    def __init__(self, nc, base, lo, hi, prefix):
        self.nc = nc
        self.base = base
        self.p = lo
        self.hi = hi
        self.prefix = prefix

    def t(self, name, shape, dtype):
        per = mybir.dt.size(dtype)
        for s in shape[1:]:
            per *= s
        addr = (self.p + 31) & ~31
        self.p = addr + per
        assert self.p <= self.hi, (self.prefix, name, self.p, self.hi)
        return self.nc.alloc_sbuf_tensor_at(self.prefix + name, shape, dtype,
                                            offset=self.base + addr)


def emit_ln_stats(ctx, bn6, sc1, src_ap, mcol, vcol, rscol, eps_ap):
    ctx.op("dve", lambda E: E.bn_stats(bn6[:, 0:6], src_ap[:, 0:512]))
    ctx.op("dve", lambda E: E.bn_stats(bn6[:, 6:12], src_ap[:, 512:1024]), inc="dve")
    ctx.wait("dve", "dve")
    c_mv = ctx.op("dve", lambda E: E.bn_aggr(mcol, bn6[:, 0:12]), inc="dve")
    ctx.wait("act", "dve", c_mv)
    ctx.op("act", lambda E: E.activation(sc1, vcol, AF.Ln, bias=eps_ap), inc="act")
    ctx.wait("act", "act")
    c_rs = ctx.op("act", lambda E: E.activation(rscol, sc1, AF.Exp, scale=-0.5),
                  inc="act")
    return c_mv, c_rs


def emit_load_ln_xp(ctx, nc, B, eps_val, load_x, xin=None):
    Xv, XPv = B["Xv"], B["XPv"]
    BN6, MV, RS, SC16 = B["BN6"], B["MV"], B["RS"], B["SC16"]
    if load_x:
        c_x = ctx.dma("sp", "ld_x", Xv, xin.rearrange("(i p) d -> p i d", p=128))
    ctx.op("dve", lambda E: E.memset(B["EPS"][:, :], eps_val), inc="dve")
    c_eps = ctx.cnt["dve"]
    if load_x:
        ctx.wait("dve", "ld_x", c_x)
        ctx.wait("pool", "ld_x", c_x)
    for i in range(NT):
        Xi = B["X"][:, i * D:(i + 1) * D]
        ctx.op("dve", lambda E, i=i, Xi=Xi:
               E.bn_stats(BN6[:, 12 * i:12 * i + 6], Xi[:, 0:512]))
        ctx.op("dve", lambda E, i=i, Xi=Xi:
               E.bn_stats(BN6[:, 12 * i + 6:12 * i + 12], Xi[:, 512:1024]),
               inc="dve")
    ctx.wait("dve", "dve")
    for i in range(NT):
        ctx.op("dve", lambda E, i=i:
               E.bn_aggr(MV[:, 2 * i:2 * i + 2], BN6[:, 12 * i:12 * i + 12]),
               inc="dve" if i == NT - 1 else None)
    c_mv = ctx.cnt["dve"]
    vview = MV[:, :].rearrange("p (i two) -> p two i", two=2)[:, 1, :]
    ctx.wait("act", "dve", c_mv)
    ctx.wait("act", "dve", c_eps)
    ctx.op("act", lambda E: E.activation(SC16[:, :], vview, AF.Ln,
                                         bias=B["EPS"][:, :]), inc="act")
    ctx.wait("act", "act")
    c_rs = ctx.op("act", lambda E: E.activation(RS[:, 0:NT], SC16[:, :],
                                                AF.Exp, scale=-0.5), inc="act")
    ctx.wait("pool", "act", c_rs)
    pe_done = {}
    evac = {}
    for i in range(NT):
        Xi = B["X"][:, i * D:(i + 1) * D]
        xt = B["XT"][:, (i % 2) * D:(i % 2 + 1) * D]
        if i >= 2:
            ctx.wait("pool", "pe", pe_done[i - 2])
        c_ap = ctx.op("pool", lambda E, xt=xt, Xi=Xi, i=i:
                      E.tensor_scalar(xt, Xi, MV[:, 2 * i:2 * i + 1],
                                      RS[:, i:i + 1], ALU.subtract, ALU.mult),
                      inc="pool")
        ctx.wait("pe", "pool", c_ap)
        if i == 0:
            ctx.wait("pe", "ld_c", ctx.cnt["ld_c"])
        if i >= 1:
            ctx.wait("pe", "dve", evac[i - 1])
        for c in range(NCH):
            ctx.op("pe", lambda E, c=c, xt=xt:
                   E.transpose(B["pt_ps"][:, c * 128:(c + 1) * 128],
                               xt[:, c * 128:(c + 1) * 128], B["IDN"][:, :]),
                   inc="pe" if c == NCH - 1 else None)
        pe_done[i] = ctx.cnt["pe"]
        ctx.wait("dve", "pe", pe_done[i])
        evac[i] = ctx.op(
            "dve", lambda E, i=i: E.tensor_copy(
                XPv[:, :, 128 * i:128 * (i + 1)],
                B["pt_ps"][:, :].rearrange("p (c t) -> p c t", t=128)),
            inc="dve")
    return evac[NT - 1]




def emit_attn0(nc, ctx, P, C):
    xin = P["x"]
    wq_d, wk_d, wv_d = P["a0_wq"], P["a0_wk"], P["a0_wv"]
    wo_d, gb_d, bb_d, mask_d = P["a0_wo"], P["a0_gb"], P["a0_bb"], P["a0_mask"]
    X, IDN = P["X"], P["IDN"]

    XP = C.t("XP", [128, NCH * S], BF16)
    OF = C.t("OF", [128, NCH * S], BF16)
    WO = C.t("WO", [128, NCH * D], BF16)
    WQ = C.t("WQ", [128, 2 * NCH * 128], BF16)
    WK = C.t("WK", [128, 2 * NCH * 128], BF16)
    WV = C.t("WV", [128, 2 * NCH * 128], BF16)
    QP = C.t("QP", [128, S], BF16)
    KP = C.t("KP", [128, S], BF16)
    VP = C.t("VP", [128, S], BF16)
    PT = C.t("PT", [128, S], BF16)
    PTT = C.t("PTT", [128, S], BF16)
    XT = C.t("XT", [128, 2 * D], BF16)
    TMP = C.t("TMP", [128, D], F32)
    Y = C.t("Y", [128, D], F32)
    YN = C.t("YN", [128, D], F32)
    GB = C.t("GB", [128, D], F32)
    BBt = C.t("BBt", [128, D], F32)
    MASK = C.t("MASK", [128, 4 * 512], BF16)
    MV = C.t("MV", [128, 2 * NT], F32)
    RS = C.t("RS", [128, NT], F32)
    MV2 = C.t("MV2", [128, 2], F32)
    RS2 = C.t("RS2", [128, 1], F32)
    BN6 = C.t("BN6", [128, 192], F32)
    SC16 = C.t("SC16", [128, 16], F32)
    SC1 = C.t("SC1", [128, 1], F32)
    NGM = C.t("NGM", [128, 1], F32)
    SM = C.t("SM", [128, 1], F32)
    REC = C.t("REC", [128, 1], F32)
    EPS5 = C.t("EPS5", [128, 1], F32)
    EPS3 = C.t("EPS3", [128, 1], F32)

    proj_ps = P["bigB"]
    sc_ps = P["bigA"]
    pt_ps = P["ptv"]
    pv_ps = P["bigB"]

    Xv = X[:, :].rearrange("p (i d) -> p i d", d=D)
    XPv = XP[:, :].rearrange("p (c t) -> p c t", t=S)
    OFv = OF[:, :].rearrange("p (c t) -> p c t", t=S)
    WOv = WO[:, :].rearrange("p (c d) -> p c d", d=D)

    c_x = ctx.dma("sp", "ld_x", Xv, xin.rearrange("(i p) d -> p i d", p=128))
    ctx.dma("sp", "ld_c", MASK[:, :], mask_d)
    ctx.dma("sp", "ld_c", GB[:, :], gb_d)
    c_const = ctx.dma("sp", "ld_c", BBt[:, :], bb_d)
    c_wo = ctx.dma("pool", "ld_wo", WOv, wo_d.rearrange("(c p) d -> p c d", p=128))
    cw = {}
    for nmd, dst, src in (("q", WQ, wq_d), ("k", WK, wk_d), ("v", WV, wv_d)):
        cw[(0, nmd)] = ctx.dma(
            "pool", "lw0", dst[:, 0:1024].rearrange("p (c n) -> p c n", n=128),
            src[0].rearrange("(c p) n -> p c n", p=128))
    c_eps3 = ctx.op("dve", lambda E: E.memset(EPS3[:, :], 1e-3), inc="dve")

    B = {"X": X, "XP": XP, "XT": XT, "pt_ps": pt_ps, "BN6": BN6,
         "SC16": SC16, "MV": MV, "RS": RS, "EPS": EPS5, "IDN": IDN,
         "Xv": Xv, "XPv": XPv}
    c_pt_evac = emit_load_ln_xp(ctx, nc, B, 1e-5, load_x=True, xin=xin)
    ctx.wait("dve", "ld_c", c_const)
    ctx.wait("pool", "ld_c", c_const)
    ctx.wait("act", "dve", c_eps3)

    pair_pe_done = {}
    for p in range(8):
        if p + 1 < 8:
            if p >= 1:
                ctx.wait("pool", "pe", pair_pe_done[p - 1])
            b = (p + 1) % 2
            for nmd, dstT, srcT in (("q", WQ, wq_d), ("k", WK, wk_d),
                                    ("v", WV, wv_d)):
                cw[(p + 1, nmd)] = ctx.dma(
                    "pool", f"lw{b}",
                    dstT[:, 1024 * b:1024 * (b + 1)]
                    .rearrange("p (c n) -> p c n", n=128),
                    srcT[p + 1].rearrange("(c p) n -> p c n", p=128))
        ctx.wait("pe", f"lw{p % 2}", cw[(p, "v")])
        ctx.wait("pe", "dve", c_pt_evac)
        bofs = (p % 2) * 1024
        for which, wsb, scale in (("q", WQ, 0.125), ("k", WK, None)):
            eng = "act" if which == "q" else "dve"
            for tb in range(4):
                ps = proj_ps[:, (tb % 2) * 512:(tb % 2) * 512 + 512]
                ctx.wait("pe", "act", ctx.cnt["act"])
                ctx.wait("pe", "dve", ctx.cnt["dve"])
                for c in range(NCH):
                    cg = ctx.op("pe", lambda E, c=c, tb=tb, wsb=wsb, ps=ps, bofs=bofs:
                                E.matmul(ps, wsb[:, bofs + c * 128:bofs + (c + 1) * 128],
                                         XPv[:, c, 512 * tb:512 * (tb + 1)],
                                         start=(c == 0), stop=(c == NCH - 1)),
                                inc="pe" if c == NCH - 1 else None)
                ctx.wait(eng, "pe", cg)
                if which == "q":
                    ctx.op("act", lambda E, tb=tb, ps=ps:
                           E.activation(QP[:, 512 * tb:512 * (tb + 1)], ps,
                                        AF.Copy, scale=0.125), inc="act")
                else:
                    ctx.op("dve", lambda E, tb=tb, ps=ps:
                           E.tensor_copy(KP[:, 512 * tb:512 * (tb + 1)], ps),
                           inc="dve")
        for tt in range(NT):
            ps = proj_ps[:, (tt % 2) * 512:(tt % 2) * 512 + 128]
            ctx.wait("pe", "act", ctx.cnt["act"])
            ctx.wait("pe", "dve", ctx.cnt["dve"])
            for c in range(NCH):
                cg = ctx.op("pe", lambda E, c=c, tt=tt, ps=ps, bofs=bofs:
                            E.matmul(ps, XPv[:, c, 128 * tt:128 * (tt + 1)],
                                     WV[:, bofs + c * 128:bofs + (c + 1) * 128],
                                     start=(c == 0), stop=(c == NCH - 1)),
                            inc="pe" if c == NCH - 1 else None)
            ctx.wait("dve", "pe", cg)
            ctx.op("dve", lambda E, tt=tt, ps=ps:
                   E.tensor_copy(VP[:, 128 * tt:128 * (tt + 1)], ps), inc="dve")
        c_qkv_dve = ctx.cnt["dve"]
        c_qkv_act = ctx.cnt["act"]

        for par in range(2):
            q_h = QP[64 * par:64 * (par + 1), :]
            k_h = KP[64 * par:64 * (par + 1), :]
            for i in range(NT):
                jmax = i // 4
                span = 512 * (jmax + 1)
                ctx.wait("pe", "act", ctx.cnt["act"])
                if par == 0 and i == 0:
                    ctx.wait("pe", "dve", c_qkv_dve)
                    ctx.wait("pe", "act", c_qkv_act)
                for j in range(jmax + 1):
                    c_sc = ctx.op("pe", lambda E, i=i, j=j, q_h=q_h, k_h=k_h:
                                  E.matmul(sc_ps[:, 512 * j:512 * (j + 1)],
                                           q_h[:, 128 * i:128 * (i + 1)],
                                           k_h[:, 512 * j:512 * (j + 1)],
                                           start=True, stop=True),
                                  inc="pe" if j == jmax else None)
                ctx.wait("dve", "pe", c_sc)
                ctx.op("dve", lambda E, i=i, jmax=jmax:
                       E.tensor_tensor(sc_ps[:, 512 * jmax:512 * (jmax + 1)],
                                       sc_ps[:, 512 * jmax:512 * (jmax + 1)],
                                       MASK[:, 512 * (i % 4):512 * (i % 4 + 1)],
                                       ALU.add), inc="dve")
                ctx.wait("dve", "dve")
                c_ngm = ctx.op("dve", lambda E, span=span:
                               E.reduce_max(NGM[:, :], sc_ps[:, 0:span],
                                            axis=AX.X, negate=True), inc="dve")
                ctx.wait("act", "dve", c_ngm)
                c_exp = ctx.op("act", lambda E, span=span:
                               E.activation(PT[:, 0:span], sc_ps[:, 0:span],
                                            AF.Exp, bias=NGM[:, :],
                                            accum_out=SM[:, :]), inc="act")
                ctx.wait("dve", "act", c_exp)
                c_rec = ctx.op("dve", lambda E: E.reciprocal(REC[:, :], SM[:, :]),
                               inc="dve")
                ctx.wait("dve", "dve", c_rec)
                c_nrm = ctx.op("dve", lambda E, span=span:
                               E.tensor_scalar(PT[:, 0:span], PT[:, 0:span],
                                               REC[:, :], None, ALU.mult),
                               inc="dve")
                nblk = i + 1
                for w in range((nblk + 7) // 8):
                    lo, hi = 8 * w, min(8 * w + 8, nblk)
                    ctx.wait("pe", "dve", c_nrm if w == 0 else c_ptt)
                    for c in range(lo, hi):
                        c_tp = ctx.op("pe", lambda E, c=c, lo=lo:
                                      E.transpose(pt_ps[:, 128 * (c - lo):128 * (c - lo + 1)],
                                                  PT[:, 128 * c:128 * (c + 1)],
                                                  IDN[:, :]),
                                      inc="pe" if c == hi - 1 else None)
                    ctx.wait("dve", "pe", c_tp)
                    c_ptt = ctx.op("dve", lambda E, lo=lo, hi=hi: E.tensor_copy(
                        PTT[:, 128 * lo:128 * hi],
                        pt_ps[:, 0:128 * (hi - lo)]), inc="dve")
                eng = "act" if (par + i) % 2 else "dve"
                ctx.wait("pe", "dve", c_ptt)
                ctx.wait("pe", eng, ctx.cnt[eng])
                for c in range(nblk):
                    c_pv = ctx.op("pe", lambda E, c=c, par=par, i=i:
                                  E.matmul(pv_ps[0:64, 1024:1152],
                                           VP[:, 128 * c + 64 * par:128 * c + 64 * par + 64],
                                           PTT[:, 128 * c:128 * (c + 1)],
                                           start=(c == 0), stop=(c == i)),
                                  inc="pe" if c == i else None)
                ctx.wait(eng, "pe", c_pv)
                dst = OF[64 * par:64 * (par + 1),
                         S * p + 128 * i:S * p + 128 * (i + 1)]
                if eng == "act":
                    ctx.op("act", lambda E, dst=dst:
                           E.copy(dst, pv_ps[0:64, 1024:1152]), inc="act")
                else:
                    ctx.op("dve", lambda E, dst=dst:
                           E.tensor_copy(dst, pv_ps[0:64, 1024:1152]), inc="dve")
        pair_pe_done[p] = ctx.cnt["pe"]
    c_of_dve = ctx.cnt["dve"]
    c_of_act = ctx.cnt["act"]

    ctx.wait("pe", "ld_wo", c_wo)
    ctx.wait("pe", "dve", c_of_dve)
    ctx.wait("pe", "act", c_of_act)
    c_y = 0
    c_xu = 0
    for i in range(NT):
        Xi = X[:, i * D:(i + 1) * D]
        ctx.wait("pe", "dve", c_y)
        for hh in range(2):
            for c in range(NCH):
                c_d = ctx.op("pe", lambda E, c=c, hh=hh, i=i:
                             E.matmul(proj_ps[:, 512 * hh:512 * (hh + 1)],
                                      OFv[:, c, 128 * i:128 * (i + 1)],
                                      WOv[:, c, 512 * hh:512 * (hh + 1)],
                                      start=(c == 0), stop=(c == NCH - 1)),
                             inc="pe" if (hh == 1 and c == NCH - 1) else None)
        ctx.wait("pool", "dve", c_y)
        c_tmp = ctx.op("pool", lambda E, Xi=Xi, i=i:
                       E.tensor_scalar(TMP[:, :], Xi, MV[:, 2 * i:2 * i + 1],
                                       RS[:, i:i + 1], ALU.subtract, ALU.mult),
                       inc="pool")
        ctx.wait("dve", "pe", c_d)
        ctx.wait("dve", "pool", c_tmp)
        c_y = ctx.op("dve", lambda E: E.tensor_add(Y[:, :], TMP[:, :],
                                                   proj_ps[:, 0:1024]), inc="dve")
        ctx.wait("dve", "dve")
        ctx.op("dve", lambda E: E.bn_stats(BN6[:, 0:6], Y[:, 0:512]))
        ctx.op("dve", lambda E: E.bn_stats(BN6[:, 6:12], Y[:, 512:1024]),
               inc="dve")
        ctx.wait("dve", "dve")
        c_mv2 = ctx.op("dve", lambda E: E.bn_aggr(MV2[:, :], BN6[:, 0:12]),
                       inc="dve")
        ctx.wait("act", "dve", c_mv2)
        ctx.op("act", lambda E: E.activation(SC1[:, :], MV2[:, 1:2],
                                             AF.Ln, bias=EPS3[:, :]), inc="act")
        ctx.wait("act", "act")
        c_rs2 = ctx.op("act", lambda E: E.activation(RS2[:, :], SC1[:, :],
                                                     AF.Exp, scale=-0.5),
                       inc="act")
        ctx.wait("dve", "act", c_rs2)
        ctx.wait("dve", "pool", c_xu)
        ctx.op("dve", lambda E: E.tensor_scalar(YN[:, :], Y[:, :], MV2[:, 0:1],
                                                RS2[:, :], ALU.subtract,
                                                ALU.mult), inc="dve")
        ctx.wait("pool", "dve", ctx.cnt["dve"])
        ctx.op("pool", lambda E: E.tensor_mul(YN[:, :], YN[:, :], GB[:, :]),
               inc="pool")
        ctx.wait("pool", "pool")
        ctx.op("pool", lambda E: E.tensor_add(YN[:, :], YN[:, :], BBt[:, :]),
               inc="pool")
        ctx.wait("pool", "pool")
        c_xu = ctx.op("pool", lambda E, Xi=Xi: E.tensor_add(Xi, Xi, YN[:, :]),
                      inc="pool")




def emit_ffn0(nc, ctx, P, C):
    rw_d, gup_d, dwn_d = P["f0_rw"], P["f0_gup"], P["f0_dwn"]
    shgu_d, shdw_d = P["f0_shgu"], P["f0_shdw"]
    X, IDN = P["X"], P["IDN"]

    XP = C.t("XP", [128, NCH * S], BF16)
    GU = C.t("GU", [128, NCH * 2048], BF16)
    DW = C.t("DW", [128, 2 * NCH * D], BF16)
    Hb = C.t("Hb", [128, 8 * 2048], BF16)
    SG = C.t("SG", [128, 2048], BF16)
    XT = C.t("XT", [128, 2 * D], BF16)
    RW = C.t("RW", [128, NCH * 16], BF16)
    LG = C.t("LG", [128, NT * 16], F32)
    WF = C.t("WF", [128, NT * 16], F32)
    M1 = C.t("M1", [128, 1], F32)
    M2 = C.t("M2", [128, 1], F32)
    MSK1 = C.t("MSK1", [128, 16], F32)
    LG2 = C.t("LG2", [128, 16], F32)
    GTS = C.t("GTS", [128, 16], F32)
    MV = C.t("MV", [128, 2 * NT], F32)
    RS = C.t("RS", [128, NT], F32)
    BN6 = C.t("BN6", [128, 192], F32)
    SC16 = C.t("SC16", [128, 16], F32)
    EPS = C.t("EPS", [128, 1], F32)

    pt_ps = P["ptv"]
    GA = P["bigA"]
    GB = P["bigB"]
    m_ps = P["bigA"]

    B = {"X": X, "XP": XP, "XT": XT, "pt_ps": pt_ps, "BN6": BN6,
         "SC16": SC16, "MV": MV, "RS": RS, "EPS": EPS, "IDN": IDN,
         "Xv": X[:, :].rearrange("p (i d) -> p i d", d=D),
         "XPv": XP[:, :].rearrange("p (c t) -> p c t", t=S)}
    XPv = B["XPv"]
    GUv = GU[:, :].rearrange("p (c h) -> p c h", h=2048)

    ctx.dma("pool", "ld_rw", RW[:, :].rearrange("p (c n) -> p c n", n=16),
            rw_d.rearrange("(c p) n -> p c n", p=128))

    def gu_src(ei):
        return shgu_d[ei] if ei < 2 else gup_d[ei - 2]

    def dw_src(ei):
        return shdw_d[ei] if ei < 2 else dwn_d[ei - 2]

    c_gu = {0: ctx.dma("pool", "lgu",
                       GUv, gu_src(0).rearrange("(c p) h -> p c h", p=128))}
    c_dw = {0: ctx.dma("pool", "ldw0",
                       DW[:, 0:NCH * D].rearrange("p (c h) -> p c h", h=D),
                       dw_src(0).rearrange("(c p) h -> p c h", h=D, p=128))}

    c_xp = emit_load_ln_xp(ctx, nc, B, 1e-5, load_x=False)

    ctx.wait("pe", "ld_c", ctx.cnt["ld_c"])
    ctx.wait("pe", "ld_rw", ctx.cnt["ld_rw"])
    for i in range(NT):
        ctx.wait("pe", "dve", ctx.cnt["dve"])
        for c in range(NCH):
            r = ctx.op("pe", lambda E, c=c, i=i:
                       E.matmul(m_ps[:, 0:16], XPv[:, c, 128 * i:128 * (i + 1)],
                                RW[:, 16 * c:16 * (c + 1)],
                                start=(c == 0), stop=(c == NCH - 1)),
                       inc="pe" if c == NCH - 1 else None)
        ctx.wait("dve", "pe", r)
        ctx.op("dve", lambda E, i=i:
               E.tensor_copy(LG[:, 16 * i:16 * (i + 1)], m_ps[:, 0:16]),
               inc="dve")
    ctx.wait("act", "dve", ctx.cnt["dve"])
    c_wf = 0
    for i in range(NT):
        lg = LG[:, 16 * i:16 * (i + 1)]
        ctx.wait("act", "dve", c_wf)
        ctx.wait("dve", "dve")
        c_m1 = ctx.op("dve", lambda E, lg=lg:
                      E.reduce_max(M1[:, :], lg, axis=AX.X), inc="dve")
        ctx.wait("dve", "dve", c_m1)
        ctx.op("dve", lambda E, lg=lg:
               E.tensor_scalar(MSK1[:, :], lg, M1[:, :], None, ALU.is_ge),
               inc="dve")
        ctx.wait("dve", "dve")
        ctx.op("dve", lambda E, lg=lg:
               E.scalar_tensor_tensor(LG2[:, :], MSK1[:, :], NEG, lg,
                                      ALU.mult, ALU.add), inc="dve")
        ctx.wait("dve", "dve")
        c_m2 = ctx.op("dve", lambda E:
                      E.reduce_max(M2[:, :], LG2[:, :], axis=AX.X), inc="dve")
        ctx.wait("dve", "dve", c_m2)
        c_msk = ctx.op("dve", lambda E, lg=lg:
                       E.tensor_scalar(MSK1[:, :], lg, M2[:, :], None,
                                       ALU.is_ge), inc="dve")
        c_sig = ctx.op("act", lambda E, lg=lg:
                       E.activation(GTS[:, :], lg, AF.Sigmoid), inc="act")
        ctx.wait("dve", "act", c_sig)
        ctx.wait("dve", "dve", c_msk)
        c_wf = ctx.op("dve", lambda E, i=i:
                      E.tensor_mul(WF[:, 16 * i:16 * (i + 1)], MSK1[:, :],
                                   GTS[:, :]), inc="dve")
    c_wf_all = c_wf

    c_pre = ctx.cnt["dve"]
    c_dph = {0: c_pre, 1: c_pre}
    r_d_last = 0
    prev_mul7 = 0
    for ei in range(18):
        real = ei >= 2
        dwbuf = ei % 2
        ctx.wait("pe", "lgu", c_gu[ei])
        ctx.wait("pe", f"ldw{dwbuf}", c_dw[ei])
        if ei == 0:
            ctx.wait("pe", "dve", c_xp)
        dwv = DW[:, dwbuf * NCH * D:(dwbuf + 1) * NCH * D] \
            .rearrange("p (c h) -> p c h", h=D)
        if ei + 1 < 18:
            ctx.wait("pool", "pe", ctx.cnt["pe"])
            c_dw[ei + 1] = ctx.dma(
                "pool", f"ldw{(ei + 1) % 2}",
                DW[:, ((ei + 1) % 2) * NCH * D:((ei + 1) % 2 + 1) * NCH * D]
                .rearrange("p (c h) -> p c h", h=D),
                dw_src(ei + 1).rearrange("(c p) h -> p c h", p=128))
        c_silu = {}
        c_mul = {}
        for f in range(NCH):
            if f == 0:
                ctx.wait("pe", "dve", max(c_dph.values()))
            else:
                ctx.wait("pe", "act", c_silu[f - 1])
            for c in range(NCH):
                for tq in range(4):
                    ctx.op("pe", lambda E, c=c, f=f, tq=tq:
                           E.matmul(GA[:, 512 * tq:512 * (tq + 1)],
                                    GUv[:, c, 128 * f:128 * (f + 1)],
                                    XPv[:, c, 512 * tq:512 * (tq + 1)],
                                    start=(c == 0), stop=(c == NCH - 1)),
                           inc="pe" if (c == NCH - 1 and tq == 3) else None)
            r_g = ctx.cnt["pe"]
            if f >= 1:
                ctx.wait("pe", "dve", c_mul[f - 1])
            for c in range(NCH):
                for tq in range(4):
                    ctx.op("pe", lambda E, c=c, f=f, tq=tq:
                           E.matmul(GB[:, 512 * tq:512 * (tq + 1)],
                                    GUv[:, c, 1024 + 128 * f:1024 + 128 * (f + 1)],
                                    XPv[:, c, 512 * tq:512 * (tq + 1)],
                                    start=(c == 0), stop=(c == NCH - 1)),
                           inc="pe" if (c == NCH - 1 and tq == 3) else None)
            r_u = ctx.cnt["pe"]
            ctx.wait("act", "pe", r_g)
            if f == 0:
                ctx.wait("act", "dve", prev_mul7)
            c_silu[f] = ctx.op("act", lambda E:
                               E.activation(SG[:, :], GA[:, 0:2048], AF.Silu),
                               inc="act")
            ctx.wait("dve", "pe", r_u)
            if f == 0:
                ctx.wait("dve", "pe", r_d_last)
            ctx.wait("dve", "act", c_silu[f])
            c_mul[f] = ctx.op("dve", lambda E, f=f:
                              E.tensor_mul(Hb[:, 2048 * f:2048 * (f + 1)],
                                           GB[:, 0:2048], SG[:, :]), inc="dve")
        prev_mul7 = c_mul[NCH - 1]
        if ei + 1 < 18:
            ctx.wait("pool", "pe", r_u)
            c_gu[ei + 1] = ctx.dma(
                "pool", "lgu", GUv,
                gu_src(ei + 1).rearrange("(c p) h -> p c h", p=128))
        ctx.wait("pe", "act", c_silu[NCH - 1])
        ctx.wait("pe", "dve", c_mul[NCH - 1])
        for it in range(NT):
            par = it % 2
            base = par * 1024
            ctx.wait("pe", "dve", c_dph[par])
            for f in range(NCH):
                for dh in range(2):
                    ctx.op("pe", lambda E, f=f, dh=dh, it=it, base=base, dwv=dwv:
                           E.matmul(GA[:, base + 512 * dh:base + 512 * (dh + 1)],
                                    Hb[:, 2048 * f + 128 * it:2048 * f + 128 * (it + 1)],
                                    dwv[:, f, 512 * dh:512 * (dh + 1)],
                                    start=(f == 0), stop=(f == NCH - 1)),
                           inc="pe" if (f == NCH - 1 and dh == 1) else None)
            r_d_last = ctx.cnt["pe"]
            ctx.wait("dve", "pe", r_d_last)
            Xi = X[:, 1024 * it:1024 * (it + 1)]
            if real:
                wcol = WF[:, 16 * it + (ei - 2):16 * it + (ei - 2) + 1]
                c_dph[par] = ctx.op(
                    "dve", lambda E, Xi=Xi, base=base, wcol=wcol:
                    E.scalar_tensor_tensor(Xi, GA[:, base:base + 1024], wcol, Xi,
                                           ALU.mult, ALU.add), inc="dve")
            else:
                c_dph[par] = ctx.op(
                    "dve", lambda E, Xi=Xi, base=base:
                    E.tensor_add(Xi, Xi, GA[:, base:base + 1024]), inc="dve")




def emit_attn1(nc, ctx, P, C):
    w3_d, wo_d = P["a1_w3"], P["a1_wo"]
    X, IDN = P["X"], P["IDN"]

    XQ = C.t("XQ", [128, NCH * S], BF16)
    KF = C.t("KF", [128, NT * D], BF16)
    VT = C.t("VT", [128, NT * D], BF16)
    WO = C.t("WO", [128, NCH * D], BF16)
    W3 = C.t("W3", [128, 3 * 512], BF16)
    KVZ = C.t("KVZ", [128, 16 * 65], BF16)
    ONES = C.t("ONES", [128, 1], BF16)
    MN = C.t("MN", [128, D], F32)
    EXs = C.t("EXs", [128, D], BF16)
    RL = C.t("RL", [128, D], BF16)
    QT = C.t("QT", [128, D], BF16)
    XT = C.t("XT", [128, 2 * D], BF16)
    DN = C.t("DN", [128, 1], F32)
    RC = C.t("RC", [128, 1], F32)
    EPS6 = C.t("EPS6", [128, 1], F32)
    MV = C.t("MV", [128, 2 * NT], F32)
    RS = C.t("RS", [128, NT], F32)
    BN6 = C.t("BN6", [128, 192], F32)
    SC16 = C.t("SC16", [128, 16], F32)
    EPS = C.t("EPS", [128, 1], F32)

    pt_ps = P["ptv"]
    p_ps = P["bigB"]
    d_ps = P["bigB"]
    nd_ps = P["bigA"]
    KZ = P["bigA"]

    B = {"X": X, "XP": XQ, "XT": XT, "pt_ps": pt_ps, "BN6": BN6,
         "SC16": SC16, "MV": MV, "RS": RS, "EPS": EPS, "IDN": IDN,
         "Xv": X[:, :].rearrange("p (i d) -> p i d", d=D),
         "XPv": XQ[:, :].rearrange("p (c t) -> p c t", t=S)}
    XPv = B["XPv"]
    WOv = WO[:, :].rearrange("p (c d) -> p c d", d=D)

    c_w3 = ctx.dma("pool", "ld_w", W3[:, :], w3_d)
    c_wo = ctx.dma("pool", "ld_wo", WOv, wo_d.rearrange("(c p) d -> p c d", p=128))
    ctx.op("dve", lambda E: E.memset(ONES[:, :], 1.0), inc="dve")
    ctx.op("dve", lambda E: E.memset(EPS6[:, :], 1e-6), inc="dve")
    c_ones = ctx.cnt["dve"]

    c_xp = emit_load_ln_xp(ctx, nc, B, 1e-5, load_x=False)

    ctx.wait("pe", "ld_w", c_w3)
    c_evac = 0
    c_pt = ctx.cnt["dve"]
    for i in range(NT):
        for m, nm in ((0, "q"), (1, "k"), (2, "v")):
            ctx.wait("pe", "dve", c_evac)
            ctx.wait("pe", "act", ctx.cnt["act"])
            for par in range(2):
                for j in range(8):
                    h = 2 * j + par
                    pbp = 64 * par
                    pc = 512 * par + 64 * j
                    r = ctx.op("pe", lambda E, h=h, pbp=pbp, pc=pc, m=m, i=i:
                               E.matmul(p_ps[:, pc:pc + 64],
                                        XPv[pbp:pbp + 64, h // 2, 128 * i:128 * (i + 1)],
                                        W3[pbp:pbp + 64, 512 * m + 64 * (h // 2):
                                           512 * m + 64 * (h // 2) + 64],
                                        start=True, stop=True),
                               inc="pe" if (par == 1 and j == 7) else None)
            if m < 2:
                ctx.wait("dve", "pe", r)
                c_mn = ctx.op("dve", lambda E:
                              E.tensor_scalar_min(MN[:, :], p_ps[:, 0:1024], 0.0),
                              inc="dve")
                c_rl = ctx.op("dve", lambda E:
                              E.tensor_scalar_max(RL[:, :], p_ps[:, 0:1024], 0.0),
                              inc="dve")
                ctx.wait("act", "dve", c_mn)
                c_ex = ctx.op("act", lambda E:
                              E.activation(EXs[:, :], MN[:, :], AF.Exp),
                              inc="act")
                ctx.wait("dve", "act", c_ex)
                ctx.wait("dve", "dve", c_rl)
                dst = QT[:, :] if m == 0 else KF[:, D * i:D * (i + 1)]
                c_evac = ctx.op("dve", lambda E, dst=dst:
                                E.tensor_add(dst, EXs[:, :], RL[:, :]),
                                inc="dve")
            else:
                ctx.wait("dve", "pe", r)
                c_evac = ctx.op("dve", lambda E, i=i:
                                E.tensor_copy(VT[:, D * i:D * (i + 1)],
                                              p_ps[:, 0:1024]), inc="dve")
        for par in range(2):
            ctx.wait("pe", "dve", c_evac if par == 0 else c_pt)
            ctx.wait("pe", "dve", c_pt)
            for j in range(8):
                h = 2 * j + par
                pc = 512 * (h % 2) + 64 * (h // 2)
                r = ctx.op("pe", lambda E, j=j, pc=pc:
                           E.transpose(pt_ps[0:64, 128 * j:128 * (j + 1)],
                                       QT[:, pc:pc + 64], IDN[:, :]),
                           inc="pe" if j == 7 else None)
            ctx.wait("dve", "pe", r)
            c_pt = ctx.op("dve", lambda E, par=par, i=i: E.tensor_copy(
                XPv[64 * par:64 * par + 64, :, 128 * i:128 * (i + 1)],
                pt_ps[0:64, :].rearrange("p (c t) -> p c t", t=128)),
                inc="dve")
    c_proj = ctx.cnt["dve"]

    ctx.wait("pe", "dve", c_proj)
    ctx.wait("pe", "dve", c_ones)
    c_kvz_evac = 0
    for h in range(16):
        ko = (h % 2) * 65
        ctx.wait("pe", "dve", c_kvz_evac)
        pc = 512 * (h % 2) + 64 * (h // 2)
        for tt in range(NT):
            ctx.op("pe", lambda E, tt=tt, ko=ko, pc=pc:
                   E.matmul(KZ[0:64, ko:ko + 64],
                            KF[:, 1024 * tt + pc:1024 * tt + pc + 64],
                            VT[:, 1024 * tt + pc:1024 * tt + pc + 64],
                            start=(tt == 0), stop=(tt == NT - 1)))
        for tt in range(NT):
            r = ctx.op("pe", lambda E, tt=tt, ko=ko, pc=pc:
                       E.matmul(KZ[0:64, ko + 64:ko + 65],
                                KF[:, 1024 * tt + pc:1024 * tt + pc + 64],
                                ONES[:, :],
                                start=(tt == 0), stop=(tt == NT - 1)),
                       inc="pe" if tt == NT - 1 else None)
        ctx.wait("dve", "pe", r)
        ctx.op("dve", lambda E, h=h, ko=ko:
               E.tensor_copy(KVZ[0:64, 65 * h:65 * (h + 1)],
                             KZ[0:64, ko:ko + 65]), inc="dve")
        c_kvz_evac = ctx.op("dve", lambda E, h=h, ko=ko:
                            E.tensor_copy(KVZ[64:128, 65 * h:65 * (h + 1)],
                                          KZ[0:64, ko:ko + 65]), inc="dve")
    c_kvz = c_kvz_evac

    c_ct = 0
    for i in range(NT):
        for h in range(16):
            pbp = 64 * (h % 2)
            no = ((i * 16 + h) % 2) * 65
            ctx.wait("pe", "dve", c_ct)
            ctx.wait("pe", "dve", c_kvz)
            r = ctx.op("pe", lambda E, h=h, i=i, pbp=pbp, no=no:
                       E.matmul(nd_ps[:, 512 + no:512 + no + 65],
                                XPv[pbp:pbp + 64, h // 2, 128 * i:128 * (i + 1)],
                                KVZ[pbp:pbp + 64, 65 * h:65 * (h + 1)],
                                start=True, stop=True), inc="pe")
            ctx.wait("dve", "pe", r)
            ctx.op("dve", lambda E, no=no:
                   E.tensor_scalar(DN[:, :], nd_ps[:, 512 + no + 64:512 + no + 65],
                                   EPS6[:, :], None, ALU.add), inc="dve")
            ctx.wait("dve", "dve")
            ctx.op("dve", lambda E: E.reciprocal(RC[:, :], DN[:, :]), inc="dve")
            ctx.wait("dve", "dve")
            c_ct = ctx.op("dve", lambda E, i=i, h=h, no=no:
                          E.tensor_scalar(KF[:, 1024 * i + 64 * h:
                                             1024 * i + 64 * (h + 1)],
                                          nd_ps[:, 512 + no:512 + no + 64],
                                          RC[:, :], None, ALU.mult), inc="dve")

    VTv = VT[:, :].rearrange("p (c t) -> p c t", t=S)
    c_pt2 = c_pt
    for i in range(NT):
        ctx.wait("pe", "dve", c_ct)
        ctx.wait("pe", "dve", c_pt2)
        for c in range(NCH):
            r = ctx.op("pe", lambda E, c=c, i=i:
                       E.transpose(pt_ps[:, c * 128:(c + 1) * 128],
                                   KF[:, 1024 * i + 128 * c:1024 * i + 128 * (c + 1)],
                                   IDN[:, :]),
                       inc="pe" if c == NCH - 1 else None)
        ctx.wait("dve", "pe", r)
        c_pt2 = ctx.op("dve", lambda E, i=i: E.tensor_copy(
            VTv[:, :, 128 * i:128 * (i + 1)],
            pt_ps[:, :].rearrange("p (c t) -> p c t", t=128)), inc="dve")
    c_ctf = c_pt2

    ctx.wait("pe", "ld_wo", c_wo)
    ctx.wait("pe", "dve", c_ctf)
    c_y = 0
    for i in range(NT):
        ctx.wait("pe", "dve", c_y)
        for hh in range(2):
            for c in range(NCH):
                r = ctx.op("pe", lambda E, c=c, hh=hh, i=i:
                           E.matmul(d_ps[:, 512 * hh:512 * (hh + 1)],
                                    VTv[:, c, 128 * i:128 * (i + 1)],
                                    WOv[:, c, 512 * hh:512 * (hh + 1)],
                                    start=(c == 0), stop=(c == NCH - 1)),
                           inc="pe" if (hh == 1 and c == NCH - 1) else None)
        ctx.wait("dve", "pe", r)
        c_y = ctx.op("dve", lambda E, i=i:
                     E.tensor_add(X[:, 1024 * i:1024 * (i + 1)],
                                  X[:, 1024 * i:1024 * (i + 1)],
                                  d_ps[:, 0:1024]), inc="dve")




def emit_ffn1(nc, ctx, P, C):
    rw_d, br_d = P["f1_rw"], P["f1_br"]
    w1_d, b1_d, w2_d, b2_d = P["f1_w1"], P["f1_b1"], P["f1_w2"], P["f1_b2"]
    X, IDN = P["X"], P["IDN"]

    XP = C.t("XP", [128, NCH * S], BF16)
    W1 = C.t("W1", [128, NCH * 2048], BF16)
    W2 = C.t("W2", [128, 16 * D], BF16)
    Hb = C.t("Hb", [128, 2 * 16 * 512], BF16)
    XT = C.t("XT", [128, 2 * D], BF16)
    RW = C.t("RW", [128, NCH * 8], BF16)
    BR = C.t("BR", [128, 8], F32)
    B1 = C.t("B1", [128, 256], F32)
    B2 = C.t("B2", [8, D], BF16)
    SCO = C.t("SCO", [128, NT * 8], F32)
    OH = C.t("OH", [128, NT * 8], F32)
    OHB = C.t("OHB", [128, 8], BF16)
    OHT = C.t("OHT", [8, S], BF16)
    M1 = C.t("M1", [128, 1], F32)
    MV = C.t("MV", [128, 2 * NT], F32)
    RS = C.t("RS", [128, NT], F32)
    BN6 = C.t("BN6", [128, 192], F32)
    SC16 = C.t("SC16", [128, 16], F32)
    EPS = C.t("EPS", [128, 1], F32)

    pt_ps = P["ptv"]
    GA = P["bigA"]
    GB = P["bigB"]
    d_ps = P["bigB"]
    m_ps = P["bigA"]

    B = {"X": X, "XP": XP, "XT": XT, "pt_ps": pt_ps, "BN6": BN6,
         "SC16": SC16, "MV": MV, "RS": RS, "EPS": EPS, "IDN": IDN,
         "Xv": X[:, :].rearrange("p (i d) -> p i d", d=D),
         "XPv": XP[:, :].rearrange("p (c t) -> p c t", t=S)}
    XPv = B["XPv"]
    W1v = W1[:, :].rearrange("p (c h) -> p c h", h=2048)
    W2v = W2[:, :].rearrange("p (c h) -> p c h", h=D)

    ctx.dma("sp", "ld_c", BR[:, :], br_d)
    ctx.dma("sp", "ld_c", B1[:, :], b1_d)
    ctx.dma("pool", "ld_rw", RW[:, :].rearrange("p (c n) -> p c n", n=8),
            rw_d.rearrange("(c p) n -> p c n", p=128))
    ctx.dma("pool", "ld_rw", B2[:, :], b2_d)

    def w1_src(se):
        return w1_d[se // 2][:, 2048 * (se % 2):2048 * (se % 2 + 1)]

    def w2_src(se):
        return w2_d[se // 2][2048 * (se % 2):2048 * (se % 2 + 1), :]

    c_w1 = {0: ctx.dma("pool", "lw1", W1v,
                       w1_src(0).rearrange("(c p) h -> p c h", p=128))}
    c_w2 = {0: ctx.dma("pool", "lw2", W2v,
                       w2_src(0).rearrange("(c p) h -> p c h", p=128))}

    c_xp = emit_load_ln_xp(ctx, nc, B, 1e-5, load_x=False)

    ctx.wait("pe", "ld_c", ctx.cnt["ld_c"])
    ctx.wait("pe", "ld_rw", ctx.cnt["ld_rw"])
    for i in range(NT):
        ctx.wait("pe", "dve", ctx.cnt["dve"])
        for c in range(NCH):
            r = ctx.op("pe", lambda E, c=c, i=i:
                       E.matmul(m_ps[:, 0:8], XPv[:, c, 128 * i:128 * (i + 1)],
                                RW[:, 8 * c:8 * (c + 1)],
                                start=(c == 0), stop=(c == NCH - 1)),
                       inc="pe" if c == NCH - 1 else None)
        ctx.wait("dve", "pe", r)
        ctx.op("dve", lambda E, i=i:
               E.scalar_tensor_tensor(SCO[:, 8 * i:8 * (i + 1)], m_ps[:, 0:8],
                                      1.0, BR[:, :], ALU.mult, ALU.add),
               inc="dve")
    c_pt = ctx.cnt["dve"]
    for i in range(NT):
        sco = SCO[:, 8 * i:8 * (i + 1)]
        ctx.wait("dve", "dve")
        c_m1 = ctx.op("dve", lambda E, sco=sco:
                      E.reduce_max(M1[:, :], sco, axis=AX.X), inc="dve")
        ctx.wait("dve", "dve", c_m1)
        ctx.op("dve", lambda E, sco=sco, i=i:
               E.tensor_scalar(OH[:, 8 * i:8 * (i + 1)], sco, M1[:, :], None,
                               ALU.is_ge), inc="dve")
        ctx.wait("dve", "dve")
        c_ohb = ctx.op("dve", lambda E, i=i:
                       E.tensor_copy(OHB[:, :], OH[:, 8 * i:8 * (i + 1)]),
                       inc="dve")
        ctx.wait("pe", "dve", c_ohb)
        ctx.wait("pe", "dve", c_pt)
        r = ctx.op("pe", lambda E:
                   E.transpose(pt_ps[0:8, 0:128], OHB[:, :], IDN[:, :]),
                   inc="pe")
        ctx.wait("dve", "pe", r)
        c_pt = ctx.op("dve", lambda E, i=i:
                      E.tensor_copy(OHT[0:8, 128 * i:128 * (i + 1)],
                                    pt_ps[0:8, 0:128]), inc="dve")
    c_oh_all = ctx.cnt["dve"]

    c_pre = ctx.cnt["dve"]
    c_dph = {0: c_pre, 1: c_pre}
    gelu_hist = [0, 0]
    c_hb_done = {0: 0, 1: 0}
    state = {"r_d": 0}

    def emit_down(se, tq):
        ex = se // 2
        hbo = (tq % 2) * 8192
        ctx.wait("pe", "act", c_hb_done[tq % 2])
        for ts in range(4):
            it = 4 * tq + ts
            par = it % 2
            base = par * 1024
            ctx.wait("pe", "dve", c_dph[par])
            for f in range(16):
                for dh in range(2):
                    ctx.op("pe", lambda E, f=f, dh=dh, ts=ts, base=base, hbo=hbo:
                           E.matmul(GB[:, base + 512 * dh:base + 512 * (dh + 1)],
                                    Hb[:, hbo + 512 * f + 128 * ts:
                                       hbo + 512 * f + 128 * (ts + 1)],
                                    W2v[:, f, 512 * dh:512 * (dh + 1)],
                                    start=(f == 0), stop=(f == 15)),
                           inc="pe" if (f == 15 and dh == 1) else None)
            state["r_d"] = ctx.cnt["pe"]
            ctx.wait("dve", "pe", state["r_d"])
            Xi = X[:, 1024 * it:1024 * (it + 1)]
            ocol = OH[:, 8 * it + ex:8 * it + ex + 1]
            c_dph[par] = ctx.op(
                "dve", lambda E, Xi=Xi, base=base, ocol=ocol:
                E.scalar_tensor_tensor(Xi, GB[:, base:base + 1024], ocol, Xi,
                                       ALU.mult, ALU.add), inc="dve")

    for se in range(16):
        ctx.wait("pe", "lw1", c_w1[se])
        ctx.wait("pe", "lw2", c_w2[se])
        if se == 0:
            ctx.wait("pe", "dve", c_pre)
        for tq in range(4):
            hbo = (tq % 2) * 8192
            for f in range(16):
                hbase = (f % 2) * 512
                ctx.wait("pe", "act", gelu_hist[-2])
                for c in range(NCH):
                    ctx.op("pe", lambda E, c=c, f=f, tq=tq, hbase=hbase:
                           E.matmul(GA[:, hbase:hbase + 512],
                                    W1v[:, c, 128 * f:128 * (f + 1)],
                                    XPv[:, c, 512 * tq:512 * (tq + 1)],
                                    start=(c == 0), stop=(c == NCH - 1)),
                           inc="pe" if c == NCH - 1 else None)
                r_h = ctx.cnt["pe"]
                ctx.wait("act", "pe", r_h)
                if f == 0:
                    ctx.wait("act", "pe", state["r_d"])
                bcol = 32 * (se // 2) + 16 * (se % 2) + f
                cg = ctx.op("act", lambda E, f=f, hbase=hbase, bcol=bcol, hbo=hbo:
                            E.activation(Hb[:, hbo + 512 * f:hbo + 512 * (f + 1)],
                                         GA[:, hbase:hbase + 512], AF.Gelu,
                                         bias=B1[:, bcol:bcol + 1]), inc="act")
                gelu_hist.append(cg)
            c_hb_done[tq % 2] = gelu_hist[-1]
            if tq == 3 and se + 1 < 16:
                ctx.wait("pool", "pe", ctx.cnt["pe"])
                c_w1[se + 1] = ctx.dma(
                    "pool", "lw1", W1v,
                    w1_src(se + 1).rearrange("(c p) h -> p c h", p=128))
            if tq >= 1:
                emit_down(se, tq - 1)
        emit_down(se, 3)
        if se + 1 < 16:
            ctx.wait("pool", "pe", ctx.cnt["pe"])
            c_w2[se + 1] = ctx.dma(
                "pool", "lw2", W2v,
                w2_src(se + 1).rearrange("(c p) h -> p c h", p=128))

    ctx.wait("pe", "ld_rw", ctx.cnt["ld_rw"])
    c_y = max(c_dph.values())
    for i in range(NT):
        ctx.wait("pe", "dve", c_y)
        for dh in range(2):
            r = ctx.op("pe", lambda E, i=i, dh=dh:
                       E.matmul(d_ps[:, 512 * dh:512 * (dh + 1)],
                                OHT[0:8, 128 * i:128 * (i + 1)],
                                B2[0:8, 512 * dh:512 * (dh + 1)],
                                start=True, stop=True),
                       inc="pe" if dh == 1 else None)
        ctx.wait("dve", "pe", r)
        c_y = ctx.op("dve", lambda E, i=i:
                     E.tensor_add(X[:, 1024 * i:1024 * (i + 1)],
                                  X[:, 1024 * i:1024 * (i + 1)],
                                  d_ps[:, 0:1024]), inc="dve")




def build_full():
    nc = bass.Bass("TRN2", target_bir_lowering=False)
    P = {}
    P["x"] = nc.dram_tensor("x", [S, D], F32, kind="ExternalInput").ap()
    P["idn"] = nc.dram_tensor("idn", [128, 128], BF16, kind="ExternalInput").ap()
    P["a0_wq"] = nc.dram_tensor("a0_wq", [8, D, 128], F32, kind="ExternalInput").ap()
    P["a0_wk"] = nc.dram_tensor("a0_wk", [8, D, 128], F32, kind="ExternalInput").ap()
    P["a0_wv"] = nc.dram_tensor("a0_wv", [8, D, 128], F32, kind="ExternalInput").ap()
    P["a0_wo"] = nc.dram_tensor("a0_wo", [D, D], F32, kind="ExternalInput").ap()
    P["a0_gb"] = nc.dram_tensor("a0_gb", [128, D], F32, kind="ExternalInput").ap()
    P["a0_bb"] = nc.dram_tensor("a0_bb", [128, D], F32, kind="ExternalInput").ap()
    P["a0_mask"] = nc.dram_tensor("a0_mask", [128, 4 * 512], BF16,
                                  kind="ExternalInput").ap()
    P["f0_rw"] = nc.dram_tensor("f0_rw", [D, 16], F32, kind="ExternalInput").ap()
    P["f0_gup"] = nc.dram_tensor("f0_gup", [16, D, 2048], F32,
                                 kind="ExternalInput").ap()
    P["f0_dwn"] = nc.dram_tensor("f0_dwn", [16, D, D], F32,
                                 kind="ExternalInput").ap()
    P["f0_shgu"] = nc.dram_tensor("f0_shgu", [2, D, 2048], F32,
                                  kind="ExternalInput").ap()
    P["f0_shdw"] = nc.dram_tensor("f0_shdw", [2, D, D], F32,
                                  kind="ExternalInput").ap()
    P["a1_w3"] = nc.dram_tensor("a1_w3", [128, 3 * 512], F32,
                                kind="ExternalInput").ap()
    P["a1_wo"] = nc.dram_tensor("a1_wo", [D, D], F32, kind="ExternalInput").ap()
    P["f1_rw"] = nc.dram_tensor("f1_rw", [D, 8], F32, kind="ExternalInput").ap()
    P["f1_br"] = nc.dram_tensor("f1_br", [128, 8], F32, kind="ExternalInput").ap()
    P["f1_w1"] = nc.dram_tensor("f1_w1", [8, D, 4096], F32,
                                kind="ExternalInput").ap()
    P["f1_b1"] = nc.dram_tensor("f1_b1", [128, 256], F32, kind="ExternalInput").ap()
    P["f1_w2"] = nc.dram_tensor("f1_w2", [8, 4096, D], F32,
                                kind="ExternalInput").ap()
    P["f1_b2"] = nc.dram_tensor("f1_b2", [8, D], F32, kind="ExternalInput").ap()
    yout = nc.dram_tensor("y", [S, D], F32, kind="ExternalOutput").ap()

    with ExitStack() as stk:
        slab = nc.alloc_sbuf_tensor("slab", [128, SLAB], U8)
        base = nc.lookup_mloc(slab).addr
        X = nc.alloc_sbuf_tensor_at("X", [128, NT * D], F32, offset=base)
        IDN = nc.alloc_sbuf_tensor_at("IDN", [128, 128], BF16, offset=base + 65536)
        P["X"] = X
        P["IDN"] = IDN
        P["bigA"] = nc.alloc_psum_tensor("bigA", [128, 2048], F32)
        P["bigB"] = nc.alloc_psum_tensor("bigB", [128, 2048], F32)
        ptv = nc.virtual_psum_tensor("ptv", [128, 1024], BF16)
        nc._tensor(ptv.name, ptv.shape, ptv.dtype, type="PSUM")
        mloc = nc.lookup_mloc(ptv)
        mloc.allocated = True
        mloc.addr = nc.lookup_mloc(P["bigB"]).addr + 6144
        P["ptv"] = ptv
        Xv = X[:, :].rearrange("p (i d) -> p i d", d=D)

        ctx = Ctx(nc)
        ctx.make_sems(stk, ["ld_x", "ld_c", "ld_wo", "ld_rw", "ld_w",
                            "lw0", "lw1", "lw2", "lgu", "ldw0", "ldw1", "st",
                            "pe", "dve", "act", "pool"])

        ctx.dma("sp", "ld_c", IDN[:, :], P["idn"])

        emit_attn0(nc, ctx, P, Carver(nc, base, PERSIST, SLAB, "a0"))
        ctx.barrier()
        emit_ffn0(nc, ctx, P, Carver(nc, base, PERSIST, SLAB, "f0"))
        ctx.barrier()
        emit_attn1(nc, ctx, P, Carver(nc, base, PERSIST, SLAB, "a1"))
        ctx.barrier()
        emit_ffn1(nc, ctx, P, Carver(nc, base, PERSIST, SLAB, "f1"))

        ctx.wait("sp", "dve", ctx.cnt["dve"])
        ctx.store(yout.rearrange("(i p) d -> p i d", p=128), Xv)
        ctx.wait("sp", "st", ctx.cnt["st"])

        with nc.Block() as block:
            ctx.replay(block)
    return nc




def _pairify(w):
    w = np.asarray(w, np.float32)
    return np.ascontiguousarray(
        np.transpose(w, (1, 0, 2)).reshape(D, D).reshape(D, 8, 128)
        .transpose(1, 0, 2))


def _prep_all(inp):
    p = np.arange(128)[:, None]
    f = np.arange(512)[None, :]
    mask = np.stack([np.where(f <= p + 128 * k, 0.0, NEG) for k in range(4)],
                    axis=1).reshape(128, 4 * 512).astype(BF)
    d = {
        "idn": np.eye(128, dtype=BF),
        "a0_wq": _pairify(inp["L0_attn_Wq"]),
        "a0_wk": _pairify(inp["L0_attn_Wk"]),
        "a0_wv": _pairify(inp["L0_attn_Wv"]),
        "a0_wo": np.ascontiguousarray(np.asarray(inp["L0_attn_Wo"], np.float32)),
        "a0_gb": np.ascontiguousarray(np.broadcast_to(
            np.asarray(inp["L0_attn_gamma"], np.float32), (128, D))),
        "a0_bb": np.ascontiguousarray(np.broadcast_to(
            np.asarray(inp["L0_attn_beta"], np.float32), (128, D))),
        "a0_mask": mask,
    }
    sgw = np.asarray(inp["L0_ffn_shared_gate_w"], np.float32).T
    suw = np.asarray(inp["L0_ffn_shared_up_w"], np.float32).T
    sdw = np.asarray(inp["L0_ffn_shared_down_w"], np.float32).T
    shgu = np.stack([
        np.concatenate([sgw[:, 0:1024], suw[:, 0:1024]], axis=1),
        np.concatenate([sgw[:, 1024:2048], suw[:, 1024:2048]], axis=1)])
    d.update({
        "f0_rw": np.ascontiguousarray(
            np.asarray(inp["L0_ffn_router_w"], np.float32).T),
        "f0_gup": np.ascontiguousarray(np.asarray(inp["L0_ffn_gate_up_w"],
                                                  np.float32)),
        "f0_dwn": np.ascontiguousarray(np.asarray(inp["L0_ffn_down_w"],
                                                  np.float32)),
        "f0_shgu": np.ascontiguousarray(shgu),
        "f0_shdw": np.ascontiguousarray(np.stack([sdw[0:1024], sdw[1024:2048]])),
    })
    w3 = np.zeros((128, 3 * 512), np.float32)
    for m, nm in ((0, "L1_attn_Wq"), (1, "L1_attn_Wk"), (2, "L1_attn_Wv")):
        w = np.asarray(inp[nm], np.float32)
        for h in range(16):
            pbp = 64 * (h % 2)
            w3[pbp:pbp + 64, 512 * m + 64 * (h // 2):512 * m + 64 * (h // 2) + 64] = w[h]
    d.update({
        "a1_w3": w3,
        "a1_wo": np.ascontiguousarray(np.asarray(inp["L1_attn_Wo"], np.float32)),
    })
    b1 = np.asarray(inp["L1_ffn_b1"], np.float32)
    b1sb = np.zeros((128, 256), np.float32)
    for ex in range(8):
        for col in range(32):
            b1sb[:, 32 * ex + col] = b1[ex, 128 * col:128 * (col + 1)]
    d.update({
        "f1_rw": np.ascontiguousarray(np.asarray(inp["L1_ffn_wr"], np.float32)),
        "f1_br": np.ascontiguousarray(np.broadcast_to(
            np.asarray(inp["L1_ffn_br"], np.float32), (128, 8))),
        "f1_w1": np.ascontiguousarray(np.asarray(inp["L1_ffn_w1"], np.float32)),
        "f1_b1": b1sb,
        "f1_w2": np.ascontiguousarray(np.asarray(inp["L1_ffn_w2"], np.float32)),
        "f1_b2": np.ascontiguousarray(np.asarray(inp["L1_ffn_b2"], np.float32)),
    })
    return d


def kernel(**inp):
    x = np.asarray(inp["L0_attn_x"], dtype=np.float32)
    nc = build_full()
    shared = _prep_all(inp)
    in_maps = [dict(shared, x=np.ascontiguousarray(x[b])) for b in range(8)]
    res = run_bass_kernel_spmd(nc, in_maps, core_ids=list(range(8)))
    return np.ascontiguousarray(
        np.stack([res.results[b]["y"] for b in range(8)], axis=0))


# version 05
# speedup vs baseline: 1.3499x; 1.0391x over previous
import numpy as np
import ml_dtypes
from contextlib import ExitStack
from collections import defaultdict

import concourse.bass as bass
import concourse.mybir as mybir
from concourse.bass_utils import run_bass_kernel_spmd

F32 = mybir.dt.float32
BF16 = mybir.dt.bfloat16
U8 = mybir.dt.uint8
AF = mybir.ActivationFunctionType
ALU = mybir.AluOpType
AX = mybir.AxisListType
BF = ml_dtypes.bfloat16

D = 1024
S = 2048
NT = S // 128
NCH = D // 128
NEG = -1e30

SLAB = 212736
PERSIST = 65792



class Ctx:

    def __init__(self, nc):
        self.nc = nc
        self.plan = []
        self.sems = {}
        self.cnt = defaultdict(int)

    def make_sems(self, stack, names):
        for n in names:
            self.sems[n] = stack.enter_context(self.nc.semaphore(n))

    def op(self, eng, emit, inc=None, n=1):
        if inc is None:
            self.plan.append((eng, emit))
            return None
        self.cnt[inc] += n
        s = self.sems[inc]
        self.plan.append((eng, lambda E, emit=emit, s=s, n=n: emit(E).then_inc(s, n)))
        return self.cnt[inc]

    def wait(self, eng, sem, val=None):
        v = self.cnt[sem] if val is None else val
        if v <= 0:
            return
        s = self.sems[sem]
        self.plan.append((eng, lambda E, s=s, v=v: E.wait_ge(s, v)))

    def dma(self, eng, sem, out, in_, **kw):
        return self.op(eng, lambda E: E.dma_start(out=out, in_=in_, **kw), inc=sem, n=16)

    def store(self, out, in_):
        return self.op("sp", lambda E: E.dma_start(out=out, in_=in_), inc="st", n=16)

    def barrier(self):
        for eng in ("sp", "pool", "pe", "dve", "act"):
            for s in list(self.cnt):
                self.wait(eng, s, self.cnt[s])

    def replay(self, block):
        plan = self.plan

        @block.sync
        def _(E):
            for g, fn in plan:
                if g == "sp":
                    fn(E)

        @block.gpsimd
        def _(E):
            for g, fn in plan:
                if g == "pool":
                    fn(E)

        @block.tensor
        def _(E):
            for g, fn in plan:
                if g == "pe":
                    fn(E)

        @block.vector
        def _(E):
            for g, fn in plan:
                if g == "dve":
                    fn(E)

        @block.scalar
        def _(E):
            for g, fn in plan:
                if g == "act":
                    fn(E)


class Carver:

    def __init__(self, nc, base, lo, hi, prefix):
        self.nc = nc
        self.base = base
        self.p = lo
        self.hi = hi
        self.prefix = prefix

    def t(self, name, shape, dtype):
        per = mybir.dt.size(dtype)
        for s in shape[1:]:
            per *= s
        addr = (self.p + 31) & ~31
        self.p = addr + per
        assert self.p <= self.hi, (self.prefix, name, self.p, self.hi)
        return self.nc.alloc_sbuf_tensor_at(self.prefix + name, shape, dtype,
                                            offset=self.base + addr)


def emit_ln_stats(ctx, bn6, sc1, src_ap, mcol, vcol, rscol, eps_ap):
    ctx.op("dve", lambda E: E.bn_stats(bn6[:, 0:6], src_ap[:, 0:512]))
    ctx.op("dve", lambda E: E.bn_stats(bn6[:, 6:12], src_ap[:, 512:1024]), inc="dve")
    ctx.wait("dve", "dve")
    c_mv = ctx.op("dve", lambda E: E.bn_aggr(mcol, bn6[:, 0:12]), inc="dve")
    ctx.wait("act", "dve", c_mv)
    ctx.op("act", lambda E: E.activation(sc1, vcol, AF.Ln, bias=eps_ap), inc="act")
    ctx.wait("act", "act")
    c_rs = ctx.op("act", lambda E: E.activation(rscol, sc1, AF.Exp, scale=-0.5),
                  inc="act")
    return c_mv, c_rs


def emit_load_ln_xp(ctx, nc, B, eps_val, load_x, xin=None):
    Xv, XPv = B["Xv"], B["XPv"]
    BN6, MV, RS, SC16 = B["BN6"], B["MV"], B["RS"], B["SC16"]
    if load_x:
        c_x = ctx.dma("sp", "ld_x", Xv, xin.rearrange("(i p) d -> p i d", p=128))
    ctx.op("dve", lambda E: E.memset(B["EPS"][:, :], eps_val), inc="dve")
    c_eps = ctx.cnt["dve"]
    if load_x:
        ctx.wait("dve", "ld_x", c_x)
        ctx.wait("pool", "ld_x", c_x)
    for i in range(NT):
        Xi = B["X"][:, i * D:(i + 1) * D]
        ctx.op("dve", lambda E, i=i, Xi=Xi:
               E.bn_stats(BN6[:, 12 * i:12 * i + 6], Xi[:, 0:512]))
        ctx.op("dve", lambda E, i=i, Xi=Xi:
               E.bn_stats(BN6[:, 12 * i + 6:12 * i + 12], Xi[:, 512:1024]),
               inc="dve")
    ctx.wait("dve", "dve")
    for i in range(NT):
        ctx.op("dve", lambda E, i=i:
               E.bn_aggr(MV[:, 2 * i:2 * i + 2], BN6[:, 12 * i:12 * i + 12]),
               inc="dve" if i == NT - 1 else None)
    c_mv = ctx.cnt["dve"]
    vview = MV[:, :].rearrange("p (i two) -> p two i", two=2)[:, 1, :]
    ctx.wait("act", "dve", c_mv)
    ctx.wait("act", "dve", c_eps)
    ctx.op("act", lambda E: E.activation(SC16[:, :], vview, AF.Ln,
                                         bias=B["EPS"][:, :]), inc="act")
    ctx.wait("act", "act")
    c_rs = ctx.op("act", lambda E: E.activation(RS[:, 0:NT], SC16[:, :],
                                                AF.Exp, scale=-0.5), inc="act")
    ctx.wait("dve", "act", c_rs)
    pe_done = {}
    evac = {}
    for i in range(NT):
        Xi = B["X"][:, i * D:(i + 1) * D]
        xt = B["XT"][:, (i % 2) * D:(i % 2 + 1) * D]
        if i >= 2:
            ctx.wait("dve", "pe", pe_done[i - 2])
        c_ap = ctx.op("dve", lambda E, xt=xt, Xi=Xi, i=i:
                      E.tensor_scalar(xt, Xi, MV[:, 2 * i:2 * i + 1],
                                      RS[:, i:i + 1], ALU.subtract, ALU.mult),
                      inc="dve")
        ctx.wait("pe", "dve", c_ap)
        if i == 0:
            ctx.wait("pe", "ld_c", ctx.cnt["ld_c"])
        if i >= 1:
            ctx.wait("pe", "dve", evac[i - 1])
        for c in range(NCH):
            ctx.op("pe", lambda E, c=c, xt=xt:
                   E.transpose(B["pt_ps"][:, c * 128:(c + 1) * 128],
                               xt[:, c * 128:(c + 1) * 128], B["IDN"][:, :]),
                   inc="pe" if c == NCH - 1 else None)
        pe_done[i] = ctx.cnt["pe"]
        ctx.wait("dve", "pe", pe_done[i])
        evac[i] = ctx.op(
            "dve", lambda E, i=i: E.tensor_copy(
                XPv[:, :, 128 * i:128 * (i + 1)],
                B["pt_ps"][:, :].rearrange("p (c t) -> p c t", t=128)),
            inc="dve")
    return evac[NT - 1]




def emit_attn0(nc, ctx, P, C):
    xin = P["x"]
    wq_d, wk_d, wv_d = P["a0_wq"], P["a0_wk"], P["a0_wv"]
    wo_d, gb_d, bb_d, mask_d = P["a0_wo"], P["a0_gb"], P["a0_bb"], P["a0_mask"]
    X, IDN = P["X"], P["IDN"]

    XP = C.t("XP", [128, NCH * S], BF16)
    OF = C.t("OF", [128, NCH * S], BF16)
    WO = C.t("WO", [128, NCH * D], BF16)
    WQ = C.t("WQ", [128, 2 * NCH * 128], BF16)
    WK = C.t("WK", [128, 2 * NCH * 128], BF16)
    WV = C.t("WV", [128, 2 * NCH * 128], BF16)
    QP = C.t("QP", [128, S], BF16)
    KP = C.t("KP", [128, S], BF16)
    VP = C.t("VP", [128, S], BF16)
    PT = C.t("PT", [128, S], BF16)
    PTT = C.t("PTT", [128, S], BF16)
    XT = C.t("XT", [128, 2 * D], BF16)
    TMP = C.t("TMP", [128, D], F32)
    Y = C.t("Y", [128, D], F32)
    YN = C.t("YN", [128, D], F32)
    GB = C.t("GB", [128, D], F32)
    BBt = C.t("BBt", [128, D], F32)
    MASK = C.t("MASK", [128, 4 * 512], BF16)
    MV = C.t("MV", [128, 2 * NT], F32)
    RS = C.t("RS", [128, NT], F32)
    MV2 = C.t("MV2", [128, 2], F32)
    RS2 = C.t("RS2", [128, 1], F32)
    BN6 = C.t("BN6", [128, 192], F32)
    SC16 = C.t("SC16", [128, 16], F32)
    SC1 = C.t("SC1", [128, 1], F32)
    NGM = C.t("NGM", [128, 1], F32)
    SM = C.t("SM", [128, 1], F32)
    REC = C.t("REC", [128, 1], F32)
    EPS5 = C.t("EPS5", [128, 1], F32)
    EPS3 = C.t("EPS3", [128, 1], F32)

    proj_ps = P["bigB"]
    sc_ps = P["bigA"]
    pt_ps = P["ptv"]
    pv_ps = P["bigB"]

    Xv = X[:, :].rearrange("p (i d) -> p i d", d=D)
    XPv = XP[:, :].rearrange("p (c t) -> p c t", t=S)
    OFv = OF[:, :].rearrange("p (c t) -> p c t", t=S)
    WOv = WO[:, :].rearrange("p (c d) -> p c d", d=D)

    c_x = ctx.dma("sp", "ld_x", Xv, xin.rearrange("(i p) d -> p i d", p=128))
    ctx.dma("sp", "ld_c", MASK[:, :], mask_d)
    ctx.dma("sp", "ld_c", GB[:, :], gb_d)
    c_const = ctx.dma("sp", "ld_c", BBt[:, :], bb_d)
    c_wo = ctx.dma("pool", "ld_wo", WOv, wo_d.rearrange("(c p) d -> p c d", p=128))
    cw = {}
    for nmd, dst, src in (("q", WQ, wq_d), ("k", WK, wk_d), ("v", WV, wv_d)):
        cw[(0, nmd)] = ctx.dma(
            "pool", "lw0", dst[:, 0:1024].rearrange("p (c n) -> p c n", n=128),
            src[0].rearrange("(c p) n -> p c n", p=128))
    c_eps3 = ctx.op("dve", lambda E: E.memset(EPS3[:, :], 1e-3), inc="dve")

    B = {"X": X, "XP": XP, "XT": XT, "pt_ps": pt_ps, "BN6": BN6,
         "SC16": SC16, "MV": MV, "RS": RS, "EPS": EPS5, "IDN": IDN,
         "Xv": Xv, "XPv": XPv}
    c_pt_evac = emit_load_ln_xp(ctx, nc, B, 1e-5, load_x=True, xin=xin)
    ctx.wait("dve", "ld_c", c_const)
    ctx.wait("pool", "ld_c", c_const)
    ctx.wait("act", "dve", c_eps3)

    pair_pe_done = {}
    for p in range(8):
        if p + 1 < 8:
            if p >= 1:
                ctx.wait("pool", "pe", pair_pe_done[p - 1])
            b = (p + 1) % 2
            for nmd, dstT, srcT in (("q", WQ, wq_d), ("k", WK, wk_d),
                                    ("v", WV, wv_d)):
                cw[(p + 1, nmd)] = ctx.dma(
                    "pool", f"lw{b}",
                    dstT[:, 1024 * b:1024 * (b + 1)]
                    .rearrange("p (c n) -> p c n", n=128),
                    srcT[p + 1].rearrange("(c p) n -> p c n", p=128))
        ctx.wait("pe", f"lw{p % 2}", cw[(p, "v")])
        ctx.wait("pe", "dve", c_pt_evac)
        bofs = (p % 2) * 1024
        for which, wsb, scale in (("q", WQ, 0.125), ("k", WK, None)):
            eng = "act" if which == "q" else "dve"
            for tb in range(4):
                ps = proj_ps[:, (tb % 2) * 512:(tb % 2) * 512 + 512]
                ctx.wait("pe", "act", ctx.cnt["act"])
                ctx.wait("pe", "dve", ctx.cnt["dve"])
                for c in range(NCH):
                    cg = ctx.op("pe", lambda E, c=c, tb=tb, wsb=wsb, ps=ps, bofs=bofs:
                                E.matmul(ps, wsb[:, bofs + c * 128:bofs + (c + 1) * 128],
                                         XPv[:, c, 512 * tb:512 * (tb + 1)],
                                         start=(c == 0), stop=(c == NCH - 1)),
                                inc="pe" if c == NCH - 1 else None)
                ctx.wait(eng, "pe", cg)
                if which == "q":
                    ctx.op("act", lambda E, tb=tb, ps=ps:
                           E.activation(QP[:, 512 * tb:512 * (tb + 1)], ps,
                                        AF.Copy, scale=0.125), inc="act")
                else:
                    ctx.op("dve", lambda E, tb=tb, ps=ps:
                           E.tensor_copy(KP[:, 512 * tb:512 * (tb + 1)], ps),
                           inc="dve")
        for tt in range(NT):
            ps = proj_ps[:, (tt % 2) * 512:(tt % 2) * 512 + 128]
            ctx.wait("pe", "act", ctx.cnt["act"])
            ctx.wait("pe", "dve", ctx.cnt["dve"])
            for c in range(NCH):
                cg = ctx.op("pe", lambda E, c=c, tt=tt, ps=ps, bofs=bofs:
                            E.matmul(ps, XPv[:, c, 128 * tt:128 * (tt + 1)],
                                     WV[:, bofs + c * 128:bofs + (c + 1) * 128],
                                     start=(c == 0), stop=(c == NCH - 1)),
                            inc="pe" if c == NCH - 1 else None)
            ctx.wait("dve", "pe", cg)
            ctx.op("dve", lambda E, tt=tt, ps=ps:
                   E.tensor_copy(VP[:, 128 * tt:128 * (tt + 1)], ps), inc="dve")
        c_qkv_dve = ctx.cnt["dve"]
        c_qkv_act = ctx.cnt["act"]

        for par in range(2):
            q_h = QP[64 * par:64 * (par + 1), :]
            k_h = KP[64 * par:64 * (par + 1), :]
            for i in range(NT):
                jmax = i // 4
                span = 512 * (jmax + 1)
                ctx.wait("pe", "act", ctx.cnt["act"])
                if par == 0 and i == 0:
                    ctx.wait("pe", "dve", c_qkv_dve)
                    ctx.wait("pe", "act", c_qkv_act)
                for j in range(jmax + 1):
                    c_sc = ctx.op("pe", lambda E, i=i, j=j, q_h=q_h, k_h=k_h:
                                  E.matmul(sc_ps[:, 512 * j:512 * (j + 1)],
                                           q_h[:, 128 * i:128 * (i + 1)],
                                           k_h[:, 512 * j:512 * (j + 1)],
                                           start=True, stop=True),
                                  inc="pe" if j == jmax else None)
                ctx.wait("dve", "pe", c_sc)
                ctx.op("dve", lambda E, i=i, jmax=jmax:
                       E.tensor_tensor(sc_ps[:, 512 * jmax:512 * (jmax + 1)],
                                       sc_ps[:, 512 * jmax:512 * (jmax + 1)],
                                       MASK[:, 512 * (i % 4):512 * (i % 4 + 1)],
                                       ALU.add), inc="dve")
                ctx.wait("dve", "dve")
                c_ngm = ctx.op("dve", lambda E, span=span:
                               E.reduce_max(NGM[:, :], sc_ps[:, 0:span],
                                            axis=AX.X, negate=True), inc="dve")
                ctx.wait("act", "dve", c_ngm)
                c_exp = ctx.op("act", lambda E, span=span:
                               E.activation(PT[:, 0:span], sc_ps[:, 0:span],
                                            AF.Exp, bias=NGM[:, :],
                                            accum_out=SM[:, :]), inc="act")
                ctx.wait("dve", "act", c_exp)
                c_rec = ctx.op("dve", lambda E: E.reciprocal(REC[:, :], SM[:, :]),
                               inc="dve")
                ctx.wait("dve", "dve", c_rec)
                c_nrm = ctx.op("dve", lambda E, span=span:
                               E.tensor_scalar(PT[:, 0:span], PT[:, 0:span],
                                               REC[:, :], None, ALU.mult),
                               inc="dve")
                nblk = i + 1
                for w in range((nblk + 7) // 8):
                    lo, hi = 8 * w, min(8 * w + 8, nblk)
                    ctx.wait("pe", "dve", c_nrm if w == 0 else c_ptt)
                    for c in range(lo, hi):
                        c_tp = ctx.op("pe", lambda E, c=c, lo=lo:
                                      E.transpose(pt_ps[:, 128 * (c - lo):128 * (c - lo + 1)],
                                                  PT[:, 128 * c:128 * (c + 1)],
                                                  IDN[:, :]),
                                      inc="pe" if c == hi - 1 else None)
                    ctx.wait("dve", "pe", c_tp)
                    c_ptt = ctx.op("dve", lambda E, lo=lo, hi=hi: E.tensor_copy(
                        PTT[:, 128 * lo:128 * hi],
                        pt_ps[:, 0:128 * (hi - lo)]), inc="dve")
                eng = "act" if (par + i) % 2 else "dve"
                ctx.wait("pe", "dve", c_ptt)
                ctx.wait("pe", eng, ctx.cnt[eng])
                for c in range(nblk):
                    c_pv = ctx.op("pe", lambda E, c=c, par=par, i=i:
                                  E.matmul(pv_ps[0:64, 1024:1152],
                                           VP[:, 128 * c + 64 * par:128 * c + 64 * par + 64],
                                           PTT[:, 128 * c:128 * (c + 1)],
                                           start=(c == 0), stop=(c == i)),
                                  inc="pe" if c == i else None)
                ctx.wait(eng, "pe", c_pv)
                dst = OF[64 * par:64 * (par + 1),
                         S * p + 128 * i:S * p + 128 * (i + 1)]
                if eng == "act":
                    ctx.op("act", lambda E, dst=dst:
                           E.copy(dst, pv_ps[0:64, 1024:1152]), inc="act")
                else:
                    ctx.op("dve", lambda E, dst=dst:
                           E.tensor_copy(dst, pv_ps[0:64, 1024:1152]), inc="dve")
        pair_pe_done[p] = ctx.cnt["pe"]
    c_of_dve = ctx.cnt["dve"]
    c_of_act = ctx.cnt["act"]

    ctx.wait("pe", "ld_wo", c_wo)
    ctx.wait("pe", "dve", c_of_dve)
    ctx.wait("pe", "act", c_of_act)
    c_y = 0
    c_xu = 0
    for i in range(NT):
        Xi = X[:, i * D:(i + 1) * D]
        ctx.wait("pe", "dve", c_y)
        for hh in range(2):
            for c in range(NCH):
                c_d = ctx.op("pe", lambda E, c=c, hh=hh, i=i:
                             E.matmul(proj_ps[:, 512 * hh:512 * (hh + 1)],
                                      OFv[:, c, 128 * i:128 * (i + 1)],
                                      WOv[:, c, 512 * hh:512 * (hh + 1)],
                                      start=(c == 0), stop=(c == NCH - 1)),
                             inc="pe" if (hh == 1 and c == NCH - 1) else None)
        c_tmp = ctx.op("dve", lambda E, Xi=Xi, i=i:
                       E.tensor_scalar(TMP[:, :], Xi, MV[:, 2 * i:2 * i + 1],
                                       RS[:, i:i + 1], ALU.subtract, ALU.mult),
                       inc="dve")
        ctx.wait("dve", "pe", c_d)
        ctx.wait("dve", "dve")
        c_y = ctx.op("dve", lambda E: E.tensor_add(Y[:, :], TMP[:, :],
                                                   proj_ps[:, 0:1024]), inc="dve")
        ctx.wait("dve", "dve")
        ctx.op("dve", lambda E: E.bn_stats(BN6[:, 0:6], Y[:, 0:512]))
        ctx.op("dve", lambda E: E.bn_stats(BN6[:, 6:12], Y[:, 512:1024]),
               inc="dve")
        ctx.wait("dve", "dve")
        c_mv2 = ctx.op("dve", lambda E: E.bn_aggr(MV2[:, :], BN6[:, 0:12]),
                       inc="dve")
        ctx.wait("act", "dve", c_mv2)
        ctx.op("act", lambda E: E.activation(SC1[:, :], MV2[:, 1:2],
                                             AF.Ln, bias=EPS3[:, :]), inc="act")
        ctx.wait("act", "act")
        c_rs2 = ctx.op("act", lambda E: E.activation(RS2[:, :], SC1[:, :],
                                                     AF.Exp, scale=-0.5),
                       inc="act")
        ctx.wait("dve", "act", c_rs2)
        ctx.op("dve", lambda E: E.tensor_scalar(YN[:, :], Y[:, :], MV2[:, 0:1],
                                                RS2[:, :], ALU.subtract,
                                                ALU.mult), inc="dve")
        ctx.wait("dve", "dve")
        ctx.op("dve", lambda E: E.tensor_mul(YN[:, :], YN[:, :], GB[:, :]),
               inc="dve")
        ctx.wait("dve", "dve")
        ctx.op("dve", lambda E: E.tensor_add(YN[:, :], YN[:, :], BBt[:, :]),
               inc="dve")
        ctx.wait("dve", "dve")
        c_xu = ctx.op("dve", lambda E, Xi=Xi: E.tensor_add(Xi, Xi, YN[:, :]),
                      inc="dve")




def emit_ffn0(nc, ctx, P, C):
    rw_d, gup_d, dwn_d = P["f0_rw"], P["f0_gup"], P["f0_dwn"]
    shgu_d, shdw_d = P["f0_shgu"], P["f0_shdw"]
    X, IDN = P["X"], P["IDN"]

    XP = C.t("XP", [128, NCH * S], BF16)
    GU = C.t("GU", [128, NCH * 2048], BF16)
    DW = C.t("DW", [128, 2 * NCH * D], BF16)
    Hb = C.t("Hb", [128, 8 * 2048], BF16)
    SG = C.t("SG", [128, 2048], BF16)
    XT = C.t("XT", [128, 2 * D], BF16)
    RW = C.t("RW", [128, NCH * 16], BF16)
    LG = C.t("LG", [128, NT * 16], F32)
    WF = C.t("WF", [128, NT * 16], F32)
    M1 = C.t("M1", [128, 1], F32)
    M2 = C.t("M2", [128, 1], F32)
    MSK1 = C.t("MSK1", [128, 16], F32)
    LG2 = C.t("LG2", [128, 16], F32)
    GTS = C.t("GTS", [128, 16], F32)
    MV = C.t("MV", [128, 2 * NT], F32)
    RS = C.t("RS", [128, NT], F32)
    BN6 = C.t("BN6", [128, 192], F32)
    SC16 = C.t("SC16", [128, 16], F32)
    EPS = C.t("EPS", [128, 1], F32)

    pt_ps = P["ptv"]
    GA = P["bigA"]
    GB = P["bigB"]
    m_ps = P["bigA"]

    B = {"X": X, "XP": XP, "XT": XT, "pt_ps": pt_ps, "BN6": BN6,
         "SC16": SC16, "MV": MV, "RS": RS, "EPS": EPS, "IDN": IDN,
         "Xv": X[:, :].rearrange("p (i d) -> p i d", d=D),
         "XPv": XP[:, :].rearrange("p (c t) -> p c t", t=S)}
    XPv = B["XPv"]
    GUv = GU[:, :].rearrange("p (c h) -> p c h", h=2048)

    ctx.dma("pool", "ld_rw", RW[:, :].rearrange("p (c n) -> p c n", n=16),
            rw_d.rearrange("(c p) n -> p c n", p=128))

    def gu_src(ei):
        return shgu_d[ei] if ei < 2 else gup_d[ei - 2]

    def dw_src(ei):
        return shdw_d[ei] if ei < 2 else dwn_d[ei - 2]

    c_gu = {0: ctx.dma("pool", "lgu",
                       GUv, gu_src(0).rearrange("(c p) h -> p c h", p=128))}
    c_dw = {0: ctx.dma("pool", "ldw0",
                       DW[:, 0:NCH * D].rearrange("p (c h) -> p c h", h=D),
                       dw_src(0).rearrange("(c p) h -> p c h", h=D, p=128))}

    c_xp = emit_load_ln_xp(ctx, nc, B, 1e-5, load_x=False)

    ctx.wait("pe", "ld_c", ctx.cnt["ld_c"])
    ctx.wait("pe", "ld_rw", ctx.cnt["ld_rw"])
    for i in range(NT):
        ctx.wait("pe", "dve", ctx.cnt["dve"])
        for c in range(NCH):
            r = ctx.op("pe", lambda E, c=c, i=i:
                       E.matmul(m_ps[:, 0:16], XPv[:, c, 128 * i:128 * (i + 1)],
                                RW[:, 16 * c:16 * (c + 1)],
                                start=(c == 0), stop=(c == NCH - 1)),
                       inc="pe" if c == NCH - 1 else None)
        ctx.wait("dve", "pe", r)
        ctx.op("dve", lambda E, i=i:
               E.tensor_copy(LG[:, 16 * i:16 * (i + 1)], m_ps[:, 0:16]),
               inc="dve")
    ctx.wait("act", "dve", ctx.cnt["dve"])
    c_wf = 0
    for i in range(NT):
        lg = LG[:, 16 * i:16 * (i + 1)]
        ctx.wait("act", "dve", c_wf)
        ctx.wait("dve", "dve")
        c_m1 = ctx.op("dve", lambda E, lg=lg:
                      E.reduce_max(M1[:, :], lg, axis=AX.X), inc="dve")
        ctx.wait("dve", "dve", c_m1)
        ctx.op("dve", lambda E, lg=lg:
               E.tensor_scalar(MSK1[:, :], lg, M1[:, :], None, ALU.is_ge),
               inc="dve")
        ctx.wait("dve", "dve")
        ctx.op("dve", lambda E, lg=lg:
               E.scalar_tensor_tensor(LG2[:, :], MSK1[:, :], NEG, lg,
                                      ALU.mult, ALU.add), inc="dve")
        ctx.wait("dve", "dve")
        c_m2 = ctx.op("dve", lambda E:
                      E.reduce_max(M2[:, :], LG2[:, :], axis=AX.X), inc="dve")
        ctx.wait("dve", "dve", c_m2)
        c_msk = ctx.op("dve", lambda E, lg=lg:
                       E.tensor_scalar(MSK1[:, :], lg, M2[:, :], None,
                                       ALU.is_ge), inc="dve")
        c_sig = ctx.op("act", lambda E, lg=lg:
                       E.activation(GTS[:, :], lg, AF.Sigmoid), inc="act")
        ctx.wait("dve", "act", c_sig)
        ctx.wait("dve", "dve", c_msk)
        c_wf = ctx.op("dve", lambda E, i=i:
                      E.tensor_mul(WF[:, 16 * i:16 * (i + 1)], MSK1[:, :],
                                   GTS[:, :]), inc="dve")
    c_wf_all = c_wf

    c_pre = ctx.cnt["dve"]
    c_dph = {0: c_pre, 1: c_pre}
    r_d_last = 0
    prev_mul7 = 0
    for ei in range(18):
        real = ei >= 2
        dwbuf = ei % 2
        ctx.wait("pe", "lgu", c_gu[ei])
        ctx.wait("pe", f"ldw{dwbuf}", c_dw[ei])
        if ei == 0:
            ctx.wait("pe", "dve", c_xp)
        dwv = DW[:, dwbuf * NCH * D:(dwbuf + 1) * NCH * D] \
            .rearrange("p (c h) -> p c h", h=D)
        if ei + 1 < 18:
            ctx.wait("pool", "pe", ctx.cnt["pe"])
            c_dw[ei + 1] = ctx.dma(
                "pool", f"ldw{(ei + 1) % 2}",
                DW[:, ((ei + 1) % 2) * NCH * D:((ei + 1) % 2 + 1) * NCH * D]
                .rearrange("p (c h) -> p c h", h=D),
                dw_src(ei + 1).rearrange("(c p) h -> p c h", p=128))
        c_silu = {}
        c_mul = {}
        for f in range(NCH):
            if f == 0:
                ctx.wait("pe", "dve", max(c_dph.values()))
            else:
                ctx.wait("pe", "act", c_silu[f - 1])
            for c in range(NCH):
                for tq in range(4):
                    ctx.op("pe", lambda E, c=c, f=f, tq=tq:
                           E.matmul(GA[:, 512 * tq:512 * (tq + 1)],
                                    GUv[:, c, 128 * f:128 * (f + 1)],
                                    XPv[:, c, 512 * tq:512 * (tq + 1)],
                                    start=(c == 0), stop=(c == NCH - 1)),
                           inc="pe" if (c == NCH - 1 and tq == 3) else None)
            r_g = ctx.cnt["pe"]
            if f >= 1:
                ctx.wait("pe", "dve", c_mul[f - 1])
            for c in range(NCH):
                for tq in range(4):
                    ctx.op("pe", lambda E, c=c, f=f, tq=tq:
                           E.matmul(GB[:, 512 * tq:512 * (tq + 1)],
                                    GUv[:, c, 1024 + 128 * f:1024 + 128 * (f + 1)],
                                    XPv[:, c, 512 * tq:512 * (tq + 1)],
                                    start=(c == 0), stop=(c == NCH - 1)),
                           inc="pe" if (c == NCH - 1 and tq == 3) else None)
            r_u = ctx.cnt["pe"]
            ctx.wait("act", "pe", r_g)
            if f == 0:
                ctx.wait("act", "dve", prev_mul7)
            c_silu[f] = ctx.op("act", lambda E:
                               E.activation(SG[:, :], GA[:, 0:2048], AF.Silu),
                               inc="act")
            ctx.wait("dve", "pe", r_u)
            if f == 0:
                ctx.wait("dve", "pe", r_d_last)
            ctx.wait("dve", "act", c_silu[f])
            c_mul[f] = ctx.op("dve", lambda E, f=f:
                              E.tensor_mul(Hb[:, 2048 * f:2048 * (f + 1)],
                                           GB[:, 0:2048], SG[:, :]), inc="dve")
        prev_mul7 = c_mul[NCH - 1]
        if ei + 1 < 18:
            ctx.wait("pool", "pe", r_u)
            c_gu[ei + 1] = ctx.dma(
                "pool", "lgu", GUv,
                gu_src(ei + 1).rearrange("(c p) h -> p c h", p=128))
        ctx.wait("pe", "act", c_silu[NCH - 1])
        ctx.wait("pe", "dve", c_mul[NCH - 1])
        for it in range(NT):
            par = it % 2
            base = par * 1024
            ctx.wait("pe", "dve", c_dph[par])
            for f in range(NCH):
                for dh in range(2):
                    ctx.op("pe", lambda E, f=f, dh=dh, it=it, base=base, dwv=dwv:
                           E.matmul(GA[:, base + 512 * dh:base + 512 * (dh + 1)],
                                    Hb[:, 2048 * f + 128 * it:2048 * f + 128 * (it + 1)],
                                    dwv[:, f, 512 * dh:512 * (dh + 1)],
                                    start=(f == 0), stop=(f == NCH - 1)),
                           inc="pe" if (f == NCH - 1 and dh == 1) else None)
            r_d_last = ctx.cnt["pe"]
            ctx.wait("dve", "pe", r_d_last)
            Xi = X[:, 1024 * it:1024 * (it + 1)]
            if real:
                wcol = WF[:, 16 * it + (ei - 2):16 * it + (ei - 2) + 1]
                c_dph[par] = ctx.op(
                    "dve", lambda E, Xi=Xi, base=base, wcol=wcol:
                    E.scalar_tensor_tensor(Xi, GA[:, base:base + 1024], wcol, Xi,
                                           ALU.mult, ALU.add), inc="dve")
            else:
                c_dph[par] = ctx.op(
                    "dve", lambda E, Xi=Xi, base=base:
                    E.tensor_add(Xi, Xi, GA[:, base:base + 1024]), inc="dve")




def emit_attn1(nc, ctx, P, C):
    w3_d, wo_d = P["a1_w3"], P["a1_wo"]
    X, IDN = P["X"], P["IDN"]

    XQ = C.t("XQ", [128, NCH * S], BF16)
    KF = C.t("KF", [128, NT * D], BF16)
    VT = C.t("VT", [128, NT * D], BF16)
    WO = C.t("WO", [128, NCH * D], BF16)
    W3 = C.t("W3", [128, 3 * 512], BF16)
    KVZ = C.t("KVZ", [128, 16 * 65], BF16)
    ONES = C.t("ONES", [128, 1], BF16)
    MN = C.t("MN", [128, D], F32)
    EXs = C.t("EXs", [128, D], BF16)
    RL = C.t("RL", [128, D], BF16)
    QT = C.t("QT", [128, D], BF16)
    XT = C.t("XT", [128, 2 * D], BF16)
    DN = C.t("DN", [128, 1], F32)
    RC = C.t("RC", [128, 1], F32)
    EPS6 = C.t("EPS6", [128, 1], F32)
    MV = C.t("MV", [128, 2 * NT], F32)
    RS = C.t("RS", [128, NT], F32)
    BN6 = C.t("BN6", [128, 192], F32)
    SC16 = C.t("SC16", [128, 16], F32)
    EPS = C.t("EPS", [128, 1], F32)

    pt_ps = P["ptv"]
    p_ps = P["bigB"]
    d_ps = P["bigB"]
    nd_ps = P["bigA"]
    KZ = P["bigA"]

    B = {"X": X, "XP": XQ, "XT": XT, "pt_ps": pt_ps, "BN6": BN6,
         "SC16": SC16, "MV": MV, "RS": RS, "EPS": EPS, "IDN": IDN,
         "Xv": X[:, :].rearrange("p (i d) -> p i d", d=D),
         "XPv": XQ[:, :].rearrange("p (c t) -> p c t", t=S)}
    XPv = B["XPv"]
    WOv = WO[:, :].rearrange("p (c d) -> p c d", d=D)

    c_w3 = ctx.dma("pool", "ld_w", W3[:, :], w3_d)
    c_wo = ctx.dma("pool", "ld_wo", WOv, wo_d.rearrange("(c p) d -> p c d", p=128))
    ctx.op("dve", lambda E: E.memset(ONES[:, :], 1.0), inc="dve")
    ctx.op("dve", lambda E: E.memset(EPS6[:, :], 1e-6), inc="dve")
    c_ones = ctx.cnt["dve"]

    c_xp = emit_load_ln_xp(ctx, nc, B, 1e-5, load_x=False)

    ctx.wait("pe", "ld_w", c_w3)
    c_evac = 0
    c_pt = ctx.cnt["dve"]
    for i in range(NT):
        for m, nm in ((0, "q"), (1, "k"), (2, "v")):
            ctx.wait("pe", "dve", c_evac)
            ctx.wait("pe", "act", ctx.cnt["act"])
            for par in range(2):
                for j in range(8):
                    h = 2 * j + par
                    pbp = 64 * par
                    pc = 512 * par + 64 * j
                    r = ctx.op("pe", lambda E, h=h, pbp=pbp, pc=pc, m=m, i=i:
                               E.matmul(p_ps[:, pc:pc + 64],
                                        XPv[pbp:pbp + 64, h // 2, 128 * i:128 * (i + 1)],
                                        W3[pbp:pbp + 64, 512 * m + 64 * (h // 2):
                                           512 * m + 64 * (h // 2) + 64],
                                        start=True, stop=True),
                               inc="pe" if (par == 1 and j == 7) else None)
            if m < 2:
                ctx.wait("dve", "pe", r)
                c_mn = ctx.op("dve", lambda E:
                              E.tensor_scalar_min(MN[:, :], p_ps[:, 0:1024], 0.0),
                              inc="dve")
                c_rl = ctx.op("dve", lambda E:
                              E.tensor_scalar_max(RL[:, :], p_ps[:, 0:1024], 0.0),
                              inc="dve")
                ctx.wait("act", "dve", c_mn)
                c_ex = ctx.op("act", lambda E:
                              E.activation(EXs[:, :], MN[:, :], AF.Exp),
                              inc="act")
                ctx.wait("dve", "act", c_ex)
                ctx.wait("dve", "dve", c_rl)
                dst = QT[:, :] if m == 0 else KF[:, D * i:D * (i + 1)]
                c_evac = ctx.op("dve", lambda E, dst=dst:
                                E.tensor_add(dst, EXs[:, :], RL[:, :]),
                                inc="dve")
            else:
                ctx.wait("dve", "pe", r)
                c_evac = ctx.op("dve", lambda E, i=i:
                                E.tensor_copy(VT[:, D * i:D * (i + 1)],
                                              p_ps[:, 0:1024]), inc="dve")
        for par in range(2):
            ctx.wait("pe", "dve", c_evac if par == 0 else c_pt)
            ctx.wait("pe", "dve", c_pt)
            for j in range(8):
                h = 2 * j + par
                pc = 512 * (h % 2) + 64 * (h // 2)
                r = ctx.op("pe", lambda E, j=j, pc=pc:
                           E.transpose(pt_ps[0:64, 128 * j:128 * (j + 1)],
                                       QT[:, pc:pc + 64], IDN[:, :]),
                           inc="pe" if j == 7 else None)
            ctx.wait("dve", "pe", r)
            c_pt = ctx.op("dve", lambda E, par=par, i=i: E.tensor_copy(
                XPv[64 * par:64 * par + 64, :, 128 * i:128 * (i + 1)],
                pt_ps[0:64, :].rearrange("p (c t) -> p c t", t=128)),
                inc="dve")
    c_proj = ctx.cnt["dve"]

    ctx.wait("pe", "dve", c_proj)
    ctx.wait("pe", "dve", c_ones)
    c_kvz_evac = 0
    for h in range(16):
        ko = (h % 2) * 65
        ctx.wait("pe", "dve", c_kvz_evac)
        pc = 512 * (h % 2) + 64 * (h // 2)
        for tt in range(NT):
            ctx.op("pe", lambda E, tt=tt, ko=ko, pc=pc:
                   E.matmul(KZ[0:64, ko:ko + 64],
                            KF[:, 1024 * tt + pc:1024 * tt + pc + 64],
                            VT[:, 1024 * tt + pc:1024 * tt + pc + 64],
                            start=(tt == 0), stop=(tt == NT - 1)))
        for tt in range(NT):
            r = ctx.op("pe", lambda E, tt=tt, ko=ko, pc=pc:
                       E.matmul(KZ[0:64, ko + 64:ko + 65],
                                KF[:, 1024 * tt + pc:1024 * tt + pc + 64],
                                ONES[:, :],
                                start=(tt == 0), stop=(tt == NT - 1)),
                       inc="pe" if tt == NT - 1 else None)
        ctx.wait("dve", "pe", r)
        ctx.op("dve", lambda E, h=h, ko=ko:
               E.tensor_copy(KVZ[0:64, 65 * h:65 * (h + 1)],
                             KZ[0:64, ko:ko + 65]), inc="dve")
        c_kvz_evac = ctx.op("dve", lambda E, h=h, ko=ko:
                            E.tensor_copy(KVZ[64:128, 65 * h:65 * (h + 1)],
                                          KZ[0:64, ko:ko + 65]), inc="dve")
    c_kvz = c_kvz_evac

    c_ct = 0
    for i in range(NT):
        for h in range(16):
            pbp = 64 * (h % 2)
            no = ((i * 16 + h) % 2) * 65
            ctx.wait("pe", "dve", c_ct)
            ctx.wait("pe", "dve", c_kvz)
            r = ctx.op("pe", lambda E, h=h, i=i, pbp=pbp, no=no:
                       E.matmul(nd_ps[:, 512 + no:512 + no + 65],
                                XPv[pbp:pbp + 64, h // 2, 128 * i:128 * (i + 1)],
                                KVZ[pbp:pbp + 64, 65 * h:65 * (h + 1)],
                                start=True, stop=True), inc="pe")
            ctx.wait("dve", "pe", r)
            ctx.op("dve", lambda E, no=no:
                   E.tensor_scalar(DN[:, :], nd_ps[:, 512 + no + 64:512 + no + 65],
                                   EPS6[:, :], None, ALU.add), inc="dve")
            ctx.wait("dve", "dve")
            ctx.op("dve", lambda E: E.reciprocal(RC[:, :], DN[:, :]), inc="dve")
            ctx.wait("dve", "dve")
            c_ct = ctx.op("dve", lambda E, i=i, h=h, no=no:
                          E.tensor_scalar(KF[:, 1024 * i + 64 * h:
                                             1024 * i + 64 * (h + 1)],
                                          nd_ps[:, 512 + no:512 + no + 64],
                                          RC[:, :], None, ALU.mult), inc="dve")

    VTv = VT[:, :].rearrange("p (c t) -> p c t", t=S)
    c_pt2 = c_pt
    for i in range(NT):
        ctx.wait("pe", "dve", c_ct)
        ctx.wait("pe", "dve", c_pt2)
        for c in range(NCH):
            r = ctx.op("pe", lambda E, c=c, i=i:
                       E.transpose(pt_ps[:, c * 128:(c + 1) * 128],
                                   KF[:, 1024 * i + 128 * c:1024 * i + 128 * (c + 1)],
                                   IDN[:, :]),
                       inc="pe" if c == NCH - 1 else None)
        ctx.wait("dve", "pe", r)
        c_pt2 = ctx.op("dve", lambda E, i=i: E.tensor_copy(
            VTv[:, :, 128 * i:128 * (i + 1)],
            pt_ps[:, :].rearrange("p (c t) -> p c t", t=128)), inc="dve")
    c_ctf = c_pt2

    ctx.wait("pe", "ld_wo", c_wo)
    ctx.wait("pe", "dve", c_ctf)
    c_y = 0
    for i in range(NT):
        ctx.wait("pe", "dve", c_y)
        for hh in range(2):
            for c in range(NCH):
                r = ctx.op("pe", lambda E, c=c, hh=hh, i=i:
                           E.matmul(d_ps[:, 512 * hh:512 * (hh + 1)],
                                    VTv[:, c, 128 * i:128 * (i + 1)],
                                    WOv[:, c, 512 * hh:512 * (hh + 1)],
                                    start=(c == 0), stop=(c == NCH - 1)),
                           inc="pe" if (hh == 1 and c == NCH - 1) else None)
        ctx.wait("dve", "pe", r)
        c_y = ctx.op("dve", lambda E, i=i:
                     E.tensor_add(X[:, 1024 * i:1024 * (i + 1)],
                                  X[:, 1024 * i:1024 * (i + 1)],
                                  d_ps[:, 0:1024]), inc="dve")




def emit_ffn1(nc, ctx, P, C):
    rw_d, br_d = P["f1_rw"], P["f1_br"]
    w1_d, b1_d, w2_d, b2_d = P["f1_w1"], P["f1_b1"], P["f1_w2"], P["f1_b2"]
    X, IDN = P["X"], P["IDN"]

    XP = C.t("XP", [128, NCH * S], BF16)
    W1 = C.t("W1", [128, NCH * 2048], BF16)
    W2 = C.t("W2", [128, 16 * D], BF16)
    Hb = C.t("Hb", [128, 2 * 16 * 512], BF16)
    XT = C.t("XT", [128, 2 * D], BF16)
    RW = C.t("RW", [128, NCH * 8], BF16)
    BR = C.t("BR", [128, 8], F32)
    B1 = C.t("B1", [128, 256], F32)
    B2 = C.t("B2", [8, D], BF16)
    SCO = C.t("SCO", [128, NT * 8], F32)
    OH = C.t("OH", [128, NT * 8], F32)
    OHB = C.t("OHB", [128, 8], BF16)
    OHT = C.t("OHT", [8, S], BF16)
    M1 = C.t("M1", [128, 1], F32)
    MV = C.t("MV", [128, 2 * NT], F32)
    RS = C.t("RS", [128, NT], F32)
    BN6 = C.t("BN6", [128, 192], F32)
    SC16 = C.t("SC16", [128, 16], F32)
    EPS = C.t("EPS", [128, 1], F32)

    pt_ps = P["ptv"]
    GA = P["bigA"]
    GB = P["bigB"]
    d_ps = P["bigB"]
    m_ps = P["bigA"]

    B = {"X": X, "XP": XP, "XT": XT, "pt_ps": pt_ps, "BN6": BN6,
         "SC16": SC16, "MV": MV, "RS": RS, "EPS": EPS, "IDN": IDN,
         "Xv": X[:, :].rearrange("p (i d) -> p i d", d=D),
         "XPv": XP[:, :].rearrange("p (c t) -> p c t", t=S)}
    XPv = B["XPv"]
    W1v = W1[:, :].rearrange("p (c h) -> p c h", h=2048)
    W2v = W2[:, :].rearrange("p (c h) -> p c h", h=D)

    ctx.dma("sp", "ld_c", BR[:, :], br_d)
    ctx.dma("sp", "ld_c", B1[:, :], b1_d)
    ctx.dma("pool", "ld_rw", RW[:, :].rearrange("p (c n) -> p c n", n=8),
            rw_d.rearrange("(c p) n -> p c n", p=128))
    ctx.dma("pool", "ld_rw", B2[:, :], b2_d)

    def w1_src(se):
        return w1_d[se // 2][:, 2048 * (se % 2):2048 * (se % 2 + 1)]

    def w2_src(se):
        return w2_d[se // 2][2048 * (se % 2):2048 * (se % 2 + 1), :]

    c_w1 = {0: ctx.dma("pool", "lw1", W1v,
                       w1_src(0).rearrange("(c p) h -> p c h", p=128))}
    c_w2 = {0: ctx.dma("pool", "lw2", W2v,
                       w2_src(0).rearrange("(c p) h -> p c h", p=128))}

    c_xp = emit_load_ln_xp(ctx, nc, B, 1e-5, load_x=False)

    ctx.wait("pe", "ld_c", ctx.cnt["ld_c"])
    ctx.wait("pe", "ld_rw", ctx.cnt["ld_rw"])
    for i in range(NT):
        ctx.wait("pe", "dve", ctx.cnt["dve"])
        for c in range(NCH):
            r = ctx.op("pe", lambda E, c=c, i=i:
                       E.matmul(m_ps[:, 0:8], XPv[:, c, 128 * i:128 * (i + 1)],
                                RW[:, 8 * c:8 * (c + 1)],
                                start=(c == 0), stop=(c == NCH - 1)),
                       inc="pe" if c == NCH - 1 else None)
        ctx.wait("dve", "pe", r)
        ctx.op("dve", lambda E, i=i:
               E.scalar_tensor_tensor(SCO[:, 8 * i:8 * (i + 1)], m_ps[:, 0:8],
                                      1.0, BR[:, :], ALU.mult, ALU.add),
               inc="dve")
    c_pt = ctx.cnt["dve"]
    for i in range(NT):
        sco = SCO[:, 8 * i:8 * (i + 1)]
        ctx.wait("dve", "dve")
        c_m1 = ctx.op("dve", lambda E, sco=sco:
                      E.reduce_max(M1[:, :], sco, axis=AX.X), inc="dve")
        ctx.wait("dve", "dve", c_m1)
        ctx.op("dve", lambda E, sco=sco, i=i:
               E.tensor_scalar(OH[:, 8 * i:8 * (i + 1)], sco, M1[:, :], None,
                               ALU.is_ge), inc="dve")
        ctx.wait("dve", "dve")
        c_ohb = ctx.op("dve", lambda E, i=i:
                       E.tensor_copy(OHB[:, :], OH[:, 8 * i:8 * (i + 1)]),
                       inc="dve")
        ctx.wait("pe", "dve", c_ohb)
        ctx.wait("pe", "dve", c_pt)
        r = ctx.op("pe", lambda E:
                   E.transpose(pt_ps[0:8, 0:128], OHB[:, :], IDN[:, :]),
                   inc="pe")
        ctx.wait("dve", "pe", r)
        c_pt = ctx.op("dve", lambda E, i=i:
                      E.tensor_copy(OHT[0:8, 128 * i:128 * (i + 1)],
                                    pt_ps[0:8, 0:128]), inc="dve")
    c_oh_all = ctx.cnt["dve"]

    c_pre = ctx.cnt["dve"]
    c_dph = {0: c_pre, 1: c_pre}
    gelu_hist = [0, 0]
    c_hb_done = {0: 0, 1: 0}
    state = {"r_d": 0}

    def emit_down(se, tq):
        ex = se // 2
        hbo = (tq % 2) * 8192
        ctx.wait("pe", "act", c_hb_done[tq % 2])
        for ts in range(4):
            it = 4 * tq + ts
            par = it % 2
            base = par * 1024
            ctx.wait("pe", "dve", c_dph[par])
            for f in range(16):
                for dh in range(2):
                    ctx.op("pe", lambda E, f=f, dh=dh, ts=ts, base=base, hbo=hbo:
                           E.matmul(GB[:, base + 512 * dh:base + 512 * (dh + 1)],
                                    Hb[:, hbo + 512 * f + 128 * ts:
                                       hbo + 512 * f + 128 * (ts + 1)],
                                    W2v[:, f, 512 * dh:512 * (dh + 1)],
                                    start=(f == 0), stop=(f == 15)),
                           inc="pe" if (f == 15 and dh == 1) else None)
            state["r_d"] = ctx.cnt["pe"]
            ctx.wait("dve", "pe", state["r_d"])
            Xi = X[:, 1024 * it:1024 * (it + 1)]
            ocol = OH[:, 8 * it + ex:8 * it + ex + 1]
            c_dph[par] = ctx.op(
                "dve", lambda E, Xi=Xi, base=base, ocol=ocol:
                E.scalar_tensor_tensor(Xi, GB[:, base:base + 1024], ocol, Xi,
                                       ALU.mult, ALU.add), inc="dve")

    for se in range(16):
        ctx.wait("pe", "lw1", c_w1[se])
        ctx.wait("pe", "lw2", c_w2[se])
        if se == 0:
            ctx.wait("pe", "dve", c_pre)
        for tq in range(4):
            hbo = (tq % 2) * 8192
            for f in range(16):
                hbase = (f % 2) * 512
                ctx.wait("pe", "act", gelu_hist[-2])
                for c in range(NCH):
                    ctx.op("pe", lambda E, c=c, f=f, tq=tq, hbase=hbase:
                           E.matmul(GA[:, hbase:hbase + 512],
                                    W1v[:, c, 128 * f:128 * (f + 1)],
                                    XPv[:, c, 512 * tq:512 * (tq + 1)],
                                    start=(c == 0), stop=(c == NCH - 1)),
                           inc="pe" if c == NCH - 1 else None)
                r_h = ctx.cnt["pe"]
                ctx.wait("act", "pe", r_h)
                if f == 0:
                    ctx.wait("act", "pe", state["r_d"])
                bcol = 32 * (se // 2) + 16 * (se % 2) + f
                cg = ctx.op("act", lambda E, f=f, hbase=hbase, bcol=bcol, hbo=hbo:
                            E.activation(Hb[:, hbo + 512 * f:hbo + 512 * (f + 1)],
                                         GA[:, hbase:hbase + 512], AF.Gelu,
                                         bias=B1[:, bcol:bcol + 1]), inc="act")
                gelu_hist.append(cg)
            c_hb_done[tq % 2] = gelu_hist[-1]
            if tq == 3 and se + 1 < 16:
                ctx.wait("pool", "pe", ctx.cnt["pe"])
                c_w1[se + 1] = ctx.dma(
                    "pool", "lw1", W1v,
                    w1_src(se + 1).rearrange("(c p) h -> p c h", p=128))
            if tq >= 1:
                emit_down(se, tq - 1)
        emit_down(se, 3)
        if se + 1 < 16:
            ctx.wait("pool", "pe", ctx.cnt["pe"])
            c_w2[se + 1] = ctx.dma(
                "pool", "lw2", W2v,
                w2_src(se + 1).rearrange("(c p) h -> p c h", p=128))

    ctx.wait("pe", "ld_rw", ctx.cnt["ld_rw"])
    c_y = max(c_dph.values())
    for i in range(NT):
        ctx.wait("pe", "dve", c_y)
        for dh in range(2):
            r = ctx.op("pe", lambda E, i=i, dh=dh:
                       E.matmul(d_ps[:, 512 * dh:512 * (dh + 1)],
                                OHT[0:8, 128 * i:128 * (i + 1)],
                                B2[0:8, 512 * dh:512 * (dh + 1)],
                                start=True, stop=True),
                       inc="pe" if dh == 1 else None)
        ctx.wait("dve", "pe", r)
        c_y = ctx.op("dve", lambda E, i=i:
                     E.tensor_add(X[:, 1024 * i:1024 * (i + 1)],
                                  X[:, 1024 * i:1024 * (i + 1)],
                                  d_ps[:, 0:1024]), inc="dve")




def build_full():
    nc = bass.Bass("TRN2", target_bir_lowering=False)
    P = {}
    P["x"] = nc.dram_tensor("x", [S, D], F32, kind="ExternalInput").ap()
    P["idn"] = nc.dram_tensor("idn", [128, 128], BF16, kind="ExternalInput").ap()
    P["a0_wq"] = nc.dram_tensor("a0_wq", [8, D, 128], F32, kind="ExternalInput").ap()
    P["a0_wk"] = nc.dram_tensor("a0_wk", [8, D, 128], F32, kind="ExternalInput").ap()
    P["a0_wv"] = nc.dram_tensor("a0_wv", [8, D, 128], F32, kind="ExternalInput").ap()
    P["a0_wo"] = nc.dram_tensor("a0_wo", [D, D], F32, kind="ExternalInput").ap()
    P["a0_gb"] = nc.dram_tensor("a0_gb", [128, D], F32, kind="ExternalInput").ap()
    P["a0_bb"] = nc.dram_tensor("a0_bb", [128, D], F32, kind="ExternalInput").ap()
    P["a0_mask"] = nc.dram_tensor("a0_mask", [128, 4 * 512], BF16,
                                  kind="ExternalInput").ap()
    P["f0_rw"] = nc.dram_tensor("f0_rw", [D, 16], F32, kind="ExternalInput").ap()
    P["f0_gup"] = nc.dram_tensor("f0_gup", [16, D, 2048], F32,
                                 kind="ExternalInput").ap()
    P["f0_dwn"] = nc.dram_tensor("f0_dwn", [16, D, D], F32,
                                 kind="ExternalInput").ap()
    P["f0_shgu"] = nc.dram_tensor("f0_shgu", [2, D, 2048], F32,
                                  kind="ExternalInput").ap()
    P["f0_shdw"] = nc.dram_tensor("f0_shdw", [2, D, D], F32,
                                  kind="ExternalInput").ap()
    P["a1_w3"] = nc.dram_tensor("a1_w3", [128, 3 * 512], F32,
                                kind="ExternalInput").ap()
    P["a1_wo"] = nc.dram_tensor("a1_wo", [D, D], F32, kind="ExternalInput").ap()
    P["f1_rw"] = nc.dram_tensor("f1_rw", [D, 8], F32, kind="ExternalInput").ap()
    P["f1_br"] = nc.dram_tensor("f1_br", [128, 8], F32, kind="ExternalInput").ap()
    P["f1_w1"] = nc.dram_tensor("f1_w1", [8, D, 4096], F32,
                                kind="ExternalInput").ap()
    P["f1_b1"] = nc.dram_tensor("f1_b1", [128, 256], F32, kind="ExternalInput").ap()
    P["f1_w2"] = nc.dram_tensor("f1_w2", [8, 4096, D], F32,
                                kind="ExternalInput").ap()
    P["f1_b2"] = nc.dram_tensor("f1_b2", [8, D], F32, kind="ExternalInput").ap()
    yout = nc.dram_tensor("y", [S, D], F32, kind="ExternalOutput").ap()

    with ExitStack() as stk:
        slab = nc.alloc_sbuf_tensor("slab", [128, SLAB], U8)
        base = nc.lookup_mloc(slab).addr
        X = nc.alloc_sbuf_tensor_at("X", [128, NT * D], F32, offset=base)
        IDN = nc.alloc_sbuf_tensor_at("IDN", [128, 128], BF16, offset=base + 65536)
        P["X"] = X
        P["IDN"] = IDN
        P["bigA"] = nc.alloc_psum_tensor("bigA", [128, 2048], F32)
        P["bigB"] = nc.alloc_psum_tensor("bigB", [128, 2048], F32)
        ptv = nc.virtual_psum_tensor("ptv", [128, 1024], BF16)
        nc._tensor(ptv.name, ptv.shape, ptv.dtype, type="PSUM")
        mloc = nc.lookup_mloc(ptv)
        mloc.allocated = True
        mloc.addr = nc.lookup_mloc(P["bigB"]).addr + 6144
        P["ptv"] = ptv
        Xv = X[:, :].rearrange("p (i d) -> p i d", d=D)

        ctx = Ctx(nc)
        ctx.make_sems(stk, ["ld_x", "ld_c", "ld_wo", "ld_rw", "ld_w",
                            "lw0", "lw1", "lw2", "lgu", "ldw0", "ldw1", "st",
                            "pe", "dve", "act", "pool"])

        ctx.dma("sp", "ld_c", IDN[:, :], P["idn"])

        emit_attn0(nc, ctx, P, Carver(nc, base, PERSIST, SLAB, "a0"))
        ctx.barrier()
        emit_ffn0(nc, ctx, P, Carver(nc, base, PERSIST, SLAB, "f0"))
        ctx.barrier()
        emit_attn1(nc, ctx, P, Carver(nc, base, PERSIST, SLAB, "a1"))
        ctx.barrier()
        emit_ffn1(nc, ctx, P, Carver(nc, base, PERSIST, SLAB, "f1"))

        ctx.wait("sp", "dve", ctx.cnt["dve"])
        ctx.store(yout.rearrange("(i p) d -> p i d", p=128), Xv)
        ctx.wait("sp", "st", ctx.cnt["st"])

        with nc.Block() as block:
            ctx.replay(block)
    return nc




def _pairify(w):
    w = np.asarray(w, np.float32)
    return np.ascontiguousarray(
        np.transpose(w, (1, 0, 2)).reshape(D, D).reshape(D, 8, 128)
        .transpose(1, 0, 2))


def _prep_all(inp):
    p = np.arange(128)[:, None]
    f = np.arange(512)[None, :]
    mask = np.stack([np.where(f <= p + 128 * k, 0.0, NEG) for k in range(4)],
                    axis=1).reshape(128, 4 * 512).astype(BF)
    d = {
        "idn": np.eye(128, dtype=BF),
        "a0_wq": _pairify(inp["L0_attn_Wq"]),
        "a0_wk": _pairify(inp["L0_attn_Wk"]),
        "a0_wv": _pairify(inp["L0_attn_Wv"]),
        "a0_wo": np.ascontiguousarray(np.asarray(inp["L0_attn_Wo"], np.float32)),
        "a0_gb": np.ascontiguousarray(np.broadcast_to(
            np.asarray(inp["L0_attn_gamma"], np.float32), (128, D))),
        "a0_bb": np.ascontiguousarray(np.broadcast_to(
            np.asarray(inp["L0_attn_beta"], np.float32), (128, D))),
        "a0_mask": mask,
    }
    sgw = np.asarray(inp["L0_ffn_shared_gate_w"], np.float32).T
    suw = np.asarray(inp["L0_ffn_shared_up_w"], np.float32).T
    sdw = np.asarray(inp["L0_ffn_shared_down_w"], np.float32).T
    shgu = np.stack([
        np.concatenate([sgw[:, 0:1024], suw[:, 0:1024]], axis=1),
        np.concatenate([sgw[:, 1024:2048], suw[:, 1024:2048]], axis=1)])
    d.update({
        "f0_rw": np.ascontiguousarray(
            np.asarray(inp["L0_ffn_router_w"], np.float32).T),
        "f0_gup": np.ascontiguousarray(np.asarray(inp["L0_ffn_gate_up_w"],
                                                  np.float32)),
        "f0_dwn": np.ascontiguousarray(np.asarray(inp["L0_ffn_down_w"],
                                                  np.float32)),
        "f0_shgu": np.ascontiguousarray(shgu),
        "f0_shdw": np.ascontiguousarray(np.stack([sdw[0:1024], sdw[1024:2048]])),
    })
    w3 = np.zeros((128, 3 * 512), np.float32)
    for m, nm in ((0, "L1_attn_Wq"), (1, "L1_attn_Wk"), (2, "L1_attn_Wv")):
        w = np.asarray(inp[nm], np.float32)
        for h in range(16):
            pbp = 64 * (h % 2)
            w3[pbp:pbp + 64, 512 * m + 64 * (h // 2):512 * m + 64 * (h // 2) + 64] = w[h]
    d.update({
        "a1_w3": w3,
        "a1_wo": np.ascontiguousarray(np.asarray(inp["L1_attn_Wo"], np.float32)),
    })
    b1 = np.asarray(inp["L1_ffn_b1"], np.float32)
    b1sb = np.zeros((128, 256), np.float32)
    for ex in range(8):
        for col in range(32):
            b1sb[:, 32 * ex + col] = b1[ex, 128 * col:128 * (col + 1)]
    d.update({
        "f1_rw": np.ascontiguousarray(np.asarray(inp["L1_ffn_wr"], np.float32)),
        "f1_br": np.ascontiguousarray(np.broadcast_to(
            np.asarray(inp["L1_ffn_br"], np.float32), (128, 8))),
        "f1_w1": np.ascontiguousarray(np.asarray(inp["L1_ffn_w1"], np.float32)),
        "f1_b1": b1sb,
        "f1_w2": np.ascontiguousarray(np.asarray(inp["L1_ffn_w2"], np.float32)),
        "f1_b2": np.ascontiguousarray(np.asarray(inp["L1_ffn_b2"], np.float32)),
    })
    return d


def kernel(**inp):
    x = np.asarray(inp["L0_attn_x"], dtype=np.float32)
    nc = build_full()
    shared = _prep_all(inp)
    in_maps = [dict(shared, x=np.ascontiguousarray(x[b])) for b in range(8)]
    res = run_bass_kernel_spmd(nc, in_maps, core_ids=list(range(8)))
    return np.ascontiguousarray(
        np.stack([res.results[b]["y"] for b in range(8)], axis=0))


# version 08 (final)
# speedup vs baseline: 1.5267x; 1.1310x over previous
import numpy as np
import ml_dtypes
from contextlib import ExitStack
from collections import defaultdict

import concourse.bass as bass
import concourse.mybir as mybir
from concourse.bass_utils import run_bass_kernel_spmd

F32 = mybir.dt.float32
BF16 = mybir.dt.bfloat16
U8 = mybir.dt.uint8
U32 = mybir.dt.uint32
AF = mybir.ActivationFunctionType
ALU = mybir.AluOpType
AX = mybir.AxisListType
BF = ml_dtypes.bfloat16

D = 1024
S = 2048
NT = S // 128
NCH = D // 128
NEG = -1e30

SLAB = 212736
FFN1_TRUNC = None
FFN1_NBLOCKS = None
FFN1_PARTS = 'full'
FFN1_DEBUG = False
PERSIST = 65792



class Ctx:

    def __init__(self, nc):
        self.nc = nc
        self.plan = []
        self.sems = {}
        self.cnt = defaultdict(int)

    def make_sems(self, stack, names):
        for n in names:
            self.sems[n] = stack.enter_context(self.nc.semaphore(n))

    def op(self, eng, emit, inc=None, n=1):
        if inc is None:
            self.plan.append((eng, emit))
            return None
        self.cnt[inc] += n
        s = self.sems[inc]
        self.plan.append((eng, lambda E, emit=emit, s=s, n=n: emit(E).then_inc(s, n)))
        return self.cnt[inc]

    def wait(self, eng, sem, val=None):
        v = self.cnt[sem] if val is None else val
        if v <= 0:
            return
        s = self.sems[sem]
        self.plan.append((eng, lambda E, s=s, v=v: E.wait_ge(s, v)))

    def dma(self, eng, sem, out, in_, **kw):
        return self.op(eng, lambda E: E.dma_start(out=out, in_=in_, **kw), inc=sem, n=16)

    def store(self, out, in_):
        return self.op("sp", lambda E: E.dma_start(out=out, in_=in_), inc="st", n=16)

    def barrier(self):
        for eng in ("sp", "pool", "pe", "dve", "act"):
            for s in list(self.cnt):
                self.wait(eng, s, self.cnt[s])

    def replay(self, block):
        plan = self.plan

        @block.sync
        def _(E):
            for g, fn in plan:
                if g == "sp":
                    fn(E)

        @block.gpsimd
        def _(E):
            for g, fn in plan:
                if g == "pool":
                    fn(E)

        @block.tensor
        def _(E):
            for g, fn in plan:
                if g == "pe":
                    fn(E)

        @block.vector
        def _(E):
            for g, fn in plan:
                if g == "dve":
                    fn(E)

        @block.scalar
        def _(E):
            for g, fn in plan:
                if g == "act":
                    fn(E)


class Carver:

    def __init__(self, nc, base, lo, hi, prefix):
        self.nc = nc
        self.base = base
        self.p = lo
        self.hi = hi
        self.prefix = prefix

    def t(self, name, shape, dtype):
        per = mybir.dt.size(dtype)
        for s in shape[1:]:
            per *= s
        addr = (self.p + 31) & ~31
        self.p = addr + per
        assert self.p <= self.hi, (self.prefix, name, self.p, self.hi)
        return self.nc.alloc_sbuf_tensor_at(self.prefix + name, shape, dtype,
                                            offset=self.base + addr)


def emit_ln_stats(ctx, bn6, sc1, src_ap, mcol, vcol, rscol, eps_ap):
    ctx.op("dve", lambda E: E.bn_stats(bn6[:, 0:6], src_ap[:, 0:512]))
    ctx.op("dve", lambda E: E.bn_stats(bn6[:, 6:12], src_ap[:, 512:1024]), inc="dve")
    ctx.wait("dve", "dve")
    c_mv = ctx.op("dve", lambda E: E.bn_aggr(mcol, bn6[:, 0:12]), inc="dve")
    ctx.wait("act", "dve", c_mv)
    ctx.op("act", lambda E: E.activation(sc1, vcol, AF.Ln, bias=eps_ap), inc="act")
    ctx.wait("act", "act")
    c_rs = ctx.op("act", lambda E: E.activation(rscol, sc1, AF.Exp, scale=-0.5),
                  inc="act")
    return c_mv, c_rs


def emit_load_ln_xp(ctx, nc, B, eps_val, load_x, xin=None):
    Xv, XPv = B["Xv"], B["XPv"]
    BN6, MV, RS, SC16 = B["BN6"], B["MV"], B["RS"], B["SC16"]
    if load_x:
        c_x = ctx.dma("sp", "ld_x", Xv, xin.rearrange("(i p) d -> p i d", p=128))
    ctx.op("dve", lambda E: E.memset(B["EPS"][:, :], eps_val), inc="dve")
    c_eps = ctx.cnt["dve"]
    if load_x:
        ctx.wait("dve", "ld_x", c_x)
        ctx.wait("pool", "ld_x", c_x)
    for i in range(NT):
        Xi = B["X"][:, i * D:(i + 1) * D]
        ctx.op("dve", lambda E, i=i, Xi=Xi:
               E.bn_stats(BN6[:, 12 * i:12 * i + 6], Xi[:, 0:512]))
        ctx.op("dve", lambda E, i=i, Xi=Xi:
               E.bn_stats(BN6[:, 12 * i + 6:12 * i + 12], Xi[:, 512:1024]),
               inc="dve")
    ctx.wait("dve", "dve")
    for i in range(NT):
        ctx.op("dve", lambda E, i=i:
               E.bn_aggr(MV[:, 2 * i:2 * i + 2], BN6[:, 12 * i:12 * i + 12]),
               inc="dve" if i == NT - 1 else None)
    c_mv = ctx.cnt["dve"]
    vview = MV[:, :].rearrange("p (i two) -> p two i", two=2)[:, 1, :]
    ctx.wait("act", "dve", c_mv)
    ctx.wait("act", "dve", c_eps)
    ctx.op("act", lambda E: E.activation(SC16[:, :], vview, AF.Ln,
                                         bias=B["EPS"][:, :]), inc="act")
    ctx.wait("act", "act")
    c_rs = ctx.op("act", lambda E: E.activation(RS[:, 0:NT], SC16[:, :],
                                                AF.Exp, scale=-0.5), inc="act")
    ctx.wait("dve", "act", c_rs)
    pe_done = {}
    evac = {}
    for i in range(NT):
        Xi = B["X"][:, i * D:(i + 1) * D]
        xt = B["XT"][:, (i % 2) * D:(i % 2 + 1) * D]
        if i >= 2:
            ctx.wait("dve", "pe", pe_done[i - 2])
        c_ap = ctx.op("dve", lambda E, xt=xt, Xi=Xi, i=i:
                      E.tensor_scalar(xt, Xi, MV[:, 2 * i:2 * i + 1],
                                      RS[:, i:i + 1], ALU.subtract, ALU.mult),
                      inc="dve")
        ctx.wait("pe", "dve", c_ap)
        if i == 0:
            ctx.wait("pe", "ld_c", ctx.cnt["ld_c"])
        if i >= 1:
            ctx.wait("pe", "dve", evac[i - 1])
        for c in range(NCH):
            ctx.op("pe", lambda E, c=c, xt=xt:
                   E.transpose(B["pt_ps"][:, c * 128:(c + 1) * 128],
                               xt[:, c * 128:(c + 1) * 128], B["IDN"][:, :]),
                   inc="pe" if c == NCH - 1 else None)
        pe_done[i] = ctx.cnt["pe"]
        ctx.wait("dve", "pe", pe_done[i])
        evac[i] = ctx.op(
            "dve", lambda E, i=i: E.tensor_copy(
                XPv[:, :, 128 * i:128 * (i + 1)],
                B["pt_ps"][:, :].rearrange("p (c t) -> p c t", t=128)),
            inc="dve")
    return evac[NT - 1]




def emit_attn0(nc, ctx, P, C):
    xin = P["x"]
    wq_d, wk_d, wv_d = P["a0_wq"], P["a0_wk"], P["a0_wv"]
    wo_d, gb_d, bb_d, mask_d = P["a0_wo"], P["a0_gb"], P["a0_bb"], P["a0_mask"]
    X, IDN = P["X"], P["IDN"]

    XP = C.t("XP", [128, NCH * S], BF16)
    OF = C.t("OF", [128, NCH * S], BF16)
    WO = C.t("WO", [128, NCH * D], BF16)
    WQ = C.t("WQ", [128, 2 * NCH * 128], BF16)
    WK = C.t("WK", [128, 2 * NCH * 128], BF16)
    WV = C.t("WV", [128, 2 * NCH * 128], BF16)
    QP = C.t("QP", [128, S], BF16)
    KP = C.t("KP", [128, S], BF16)
    VP = C.t("VP", [128, S], BF16)
    PT = C.t("PT", [128, S], BF16)
    PTT = C.t("PTT", [128, S], BF16)
    XT = C.t("XT", [128, 2 * D], BF16)
    TMP = C.t("TMP", [128, D], F32)
    Y = C.t("Y", [128, D], F32)
    YN = C.t("YN", [128, D], F32)
    GB = C.t("GB", [128, D], F32)
    BBt = C.t("BBt", [128, D], F32)
    MASK = C.t("MASK", [128, 4 * 512], BF16)
    MV = C.t("MV", [128, 2 * NT], F32)
    RS = C.t("RS", [128, NT], F32)
    MV2 = C.t("MV2", [128, 2], F32)
    RS2 = C.t("RS2", [128, 1], F32)
    BN6 = C.t("BN6", [128, 192], F32)
    SC16 = C.t("SC16", [128, 16], F32)
    SC1 = C.t("SC1", [128, 1], F32)
    NGM = C.t("NGM", [128, 1], F32)
    SM = C.t("SM", [128, 1], F32)
    REC = C.t("REC", [128, 1], F32)
    EPS5 = C.t("EPS5", [128, 1], F32)
    EPS3 = C.t("EPS3", [128, 1], F32)

    proj_ps = P["bigB"]
    sc_ps = P["bigA"]
    pt_ps = P["ptv"]
    pv_ps = P["bigB"]

    Xv = X[:, :].rearrange("p (i d) -> p i d", d=D)
    XPv = XP[:, :].rearrange("p (c t) -> p c t", t=S)
    OFv = OF[:, :].rearrange("p (c t) -> p c t", t=S)
    WOv = WO[:, :].rearrange("p (c d) -> p c d", d=D)

    c_x = ctx.dma("sp", "ld_x", Xv, xin.rearrange("(i p) d -> p i d", p=128))
    ctx.dma("sp", "ld_c", MASK[:, :], mask_d)
    ctx.dma("sp", "ld_c", GB[:, :], gb_d)
    c_const = ctx.dma("sp", "ld_c", BBt[:, :], bb_d)
    c_wo = ctx.dma("pool", "ld_wo", WOv, wo_d.rearrange("(c p) d -> p c d", p=128))
    cw = {}
    for nmd, dst, src in (("q", WQ, wq_d), ("k", WK, wk_d), ("v", WV, wv_d)):
        cw[(0, nmd)] = ctx.dma(
            "pool", "lw0", dst[:, 0:1024].rearrange("p (c n) -> p c n", n=128),
            src[0].rearrange("(c p) n -> p c n", p=128))
    c_eps3 = ctx.op("dve", lambda E: E.memset(EPS3[:, :], 1e-3), inc="dve")

    B = {"X": X, "XP": XP, "XT": XT, "pt_ps": pt_ps, "BN6": BN6,
         "SC16": SC16, "MV": MV, "RS": RS, "EPS": EPS5, "IDN": IDN,
         "Xv": Xv, "XPv": XPv}
    c_pt_evac = emit_load_ln_xp(ctx, nc, B, 1e-5, load_x=True, xin=xin)
    ctx.wait("dve", "ld_c", c_const)
    ctx.wait("pool", "ld_c", c_const)
    ctx.wait("act", "dve", c_eps3)

    pair_pe_done = {}
    for p in range(8):
        if p + 1 < 8:
            if p >= 1:
                ctx.wait("pool", "pe", pair_pe_done[p - 1])
            b = (p + 1) % 2
            for nmd, dstT, srcT in (("q", WQ, wq_d), ("k", WK, wk_d),
                                    ("v", WV, wv_d)):
                cw[(p + 1, nmd)] = ctx.dma(
                    "pool", f"lw{b}",
                    dstT[:, 1024 * b:1024 * (b + 1)]
                    .rearrange("p (c n) -> p c n", n=128),
                    srcT[p + 1].rearrange("(c p) n -> p c n", p=128))
        ctx.wait("pe", f"lw{p % 2}", cw[(p, "v")])
        ctx.wait("pe", "dve", c_pt_evac)
        bofs = (p % 2) * 1024
        for which, wsb, scale in (("q", WQ, 0.125), ("k", WK, None)):
            eng = "act" if which == "q" else "dve"
            for tb in range(4):
                ps = proj_ps[:, (tb % 2) * 512:(tb % 2) * 512 + 512]
                ctx.wait("pe", "act", ctx.cnt["act"])
                ctx.wait("pe", "dve", ctx.cnt["dve"])
                for c in range(NCH):
                    cg = ctx.op("pe", lambda E, c=c, tb=tb, wsb=wsb, ps=ps, bofs=bofs:
                                E.matmul(ps, wsb[:, bofs + c * 128:bofs + (c + 1) * 128],
                                         XPv[:, c, 512 * tb:512 * (tb + 1)],
                                         start=(c == 0), stop=(c == NCH - 1)),
                                inc="pe" if c == NCH - 1 else None)
                ctx.wait(eng, "pe", cg)
                if which == "q":
                    ctx.op("act", lambda E, tb=tb, ps=ps:
                           E.activation(QP[:, 512 * tb:512 * (tb + 1)], ps,
                                        AF.Copy, scale=0.125), inc="act")
                else:
                    ctx.op("dve", lambda E, tb=tb, ps=ps:
                           E.tensor_copy(KP[:, 512 * tb:512 * (tb + 1)], ps),
                           inc="dve")
        for tt in range(NT):
            ps = proj_ps[:, (tt % 2) * 512:(tt % 2) * 512 + 128]
            ctx.wait("pe", "act", ctx.cnt["act"])
            ctx.wait("pe", "dve", ctx.cnt["dve"])
            for c in range(NCH):
                cg = ctx.op("pe", lambda E, c=c, tt=tt, ps=ps, bofs=bofs:
                            E.matmul(ps, XPv[:, c, 128 * tt:128 * (tt + 1)],
                                     WV[:, bofs + c * 128:bofs + (c + 1) * 128],
                                     start=(c == 0), stop=(c == NCH - 1)),
                            inc="pe" if c == NCH - 1 else None)
            ctx.wait("dve", "pe", cg)
            ctx.op("dve", lambda E, tt=tt, ps=ps:
                   E.tensor_copy(VP[:, 128 * tt:128 * (tt + 1)], ps), inc="dve")
        c_qkv_dve = ctx.cnt["dve"]
        c_qkv_act = ctx.cnt["act"]

        for par in range(2):
            q_h = QP[64 * par:64 * (par + 1), :]
            k_h = KP[64 * par:64 * (par + 1), :]
            for i in range(NT):
                jmax = i // 4
                span = 512 * (jmax + 1)
                ctx.wait("pe", "act", ctx.cnt["act"])
                if par == 0 and i == 0:
                    ctx.wait("pe", "dve", c_qkv_dve)
                    ctx.wait("pe", "act", c_qkv_act)
                for j in range(jmax + 1):
                    c_sc = ctx.op("pe", lambda E, i=i, j=j, q_h=q_h, k_h=k_h:
                                  E.matmul(sc_ps[:, 512 * j:512 * (j + 1)],
                                           q_h[:, 128 * i:128 * (i + 1)],
                                           k_h[:, 512 * j:512 * (j + 1)],
                                           start=True, stop=True),
                                  inc="pe" if j == jmax else None)
                ctx.wait("dve", "pe", c_sc)
                ctx.op("dve", lambda E, i=i, jmax=jmax:
                       E.tensor_tensor(sc_ps[:, 512 * jmax:512 * (jmax + 1)],
                                       sc_ps[:, 512 * jmax:512 * (jmax + 1)],
                                       MASK[:, 512 * (i % 4):512 * (i % 4 + 1)],
                                       ALU.add), inc="dve")
                ctx.wait("dve", "dve")
                c_ngm = ctx.op("dve", lambda E, span=span:
                               E.reduce_max(NGM[:, :], sc_ps[:, 0:span],
                                            axis=AX.X, negate=True), inc="dve")
                ctx.wait("act", "dve", c_ngm)
                c_exp = ctx.op("act", lambda E, span=span:
                               E.activation(PT[:, 0:span], sc_ps[:, 0:span],
                                            AF.Exp, bias=NGM[:, :],
                                            accum_out=SM[:, :]), inc="act")
                ctx.wait("dve", "act", c_exp)
                c_rec = ctx.op("dve", lambda E: E.reciprocal(REC[:, :], SM[:, :]),
                               inc="dve")
                ctx.wait("dve", "dve", c_rec)
                c_nrm = ctx.op("dve", lambda E, span=span:
                               E.tensor_scalar(PT[:, 0:span], PT[:, 0:span],
                                               REC[:, :], None, ALU.mult),
                               inc="dve")
                nblk = i + 1
                for w in range((nblk + 7) // 8):
                    lo, hi = 8 * w, min(8 * w + 8, nblk)
                    ctx.wait("pe", "dve", c_nrm if w == 0 else c_ptt)
                    for c in range(lo, hi):
                        c_tp = ctx.op("pe", lambda E, c=c, lo=lo:
                                      E.transpose(pt_ps[:, 128 * (c - lo):128 * (c - lo + 1)],
                                                  PT[:, 128 * c:128 * (c + 1)],
                                                  IDN[:, :]),
                                      inc="pe" if c == hi - 1 else None)
                    ctx.wait("dve", "pe", c_tp)
                    c_ptt = ctx.op("dve", lambda E, lo=lo, hi=hi: E.tensor_copy(
                        PTT[:, 128 * lo:128 * hi],
                        pt_ps[:, 0:128 * (hi - lo)]), inc="dve")
                eng = "act" if (par + i) % 2 else "dve"
                ctx.wait("pe", "dve", c_ptt)
                ctx.wait("pe", eng, ctx.cnt[eng])
                for c in range(nblk):
                    c_pv = ctx.op("pe", lambda E, c=c, par=par, i=i:
                                  E.matmul(pv_ps[0:64, 1024:1152],
                                           VP[:, 128 * c + 64 * par:128 * c + 64 * par + 64],
                                           PTT[:, 128 * c:128 * (c + 1)],
                                           start=(c == 0), stop=(c == i)),
                                  inc="pe" if c == i else None)
                ctx.wait(eng, "pe", c_pv)
                dst = OF[64 * par:64 * (par + 1),
                         S * p + 128 * i:S * p + 128 * (i + 1)]
                if eng == "act":
                    ctx.op("act", lambda E, dst=dst:
                           E.copy(dst, pv_ps[0:64, 1024:1152]), inc="act")
                else:
                    ctx.op("dve", lambda E, dst=dst:
                           E.tensor_copy(dst, pv_ps[0:64, 1024:1152]), inc="dve")
        pair_pe_done[p] = ctx.cnt["pe"]
    c_of_dve = ctx.cnt["dve"]
    c_of_act = ctx.cnt["act"]

    ctx.wait("pe", "ld_wo", c_wo)
    ctx.wait("pe", "dve", c_of_dve)
    ctx.wait("pe", "act", c_of_act)
    c_y = 0
    c_xu = 0
    for i in range(NT):
        Xi = X[:, i * D:(i + 1) * D]
        ctx.wait("pe", "dve", c_y)
        for hh in range(2):
            for c in range(NCH):
                c_d = ctx.op("pe", lambda E, c=c, hh=hh, i=i:
                             E.matmul(proj_ps[:, 512 * hh:512 * (hh + 1)],
                                      OFv[:, c, 128 * i:128 * (i + 1)],
                                      WOv[:, c, 512 * hh:512 * (hh + 1)],
                                      start=(c == 0), stop=(c == NCH - 1)),
                             inc="pe" if (hh == 1 and c == NCH - 1) else None)
        c_tmp = ctx.op("dve", lambda E, Xi=Xi, i=i:
                       E.tensor_scalar(TMP[:, :], Xi, MV[:, 2 * i:2 * i + 1],
                                       RS[:, i:i + 1], ALU.subtract, ALU.mult),
                       inc="dve")
        ctx.wait("dve", "pe", c_d)
        ctx.wait("dve", "dve")
        c_y = ctx.op("dve", lambda E: E.tensor_add(Y[:, :], TMP[:, :],
                                                   proj_ps[:, 0:1024]), inc="dve")
        ctx.wait("dve", "dve")
        ctx.op("dve", lambda E: E.bn_stats(BN6[:, 0:6], Y[:, 0:512]))
        ctx.op("dve", lambda E: E.bn_stats(BN6[:, 6:12], Y[:, 512:1024]),
               inc="dve")
        ctx.wait("dve", "dve")
        c_mv2 = ctx.op("dve", lambda E: E.bn_aggr(MV2[:, :], BN6[:, 0:12]),
                       inc="dve")
        ctx.wait("act", "dve", c_mv2)
        ctx.op("act", lambda E: E.activation(SC1[:, :], MV2[:, 1:2],
                                             AF.Ln, bias=EPS3[:, :]), inc="act")
        ctx.wait("act", "act")
        c_rs2 = ctx.op("act", lambda E: E.activation(RS2[:, :], SC1[:, :],
                                                     AF.Exp, scale=-0.5),
                       inc="act")
        ctx.wait("dve", "act", c_rs2)
        ctx.op("dve", lambda E: E.tensor_scalar(YN[:, :], Y[:, :], MV2[:, 0:1],
                                                RS2[:, :], ALU.subtract,
                                                ALU.mult), inc="dve")
        ctx.wait("dve", "dve")
        ctx.op("dve", lambda E: E.tensor_mul(YN[:, :], YN[:, :], GB[:, :]),
               inc="dve")
        ctx.wait("dve", "dve")
        ctx.op("dve", lambda E: E.tensor_add(YN[:, :], YN[:, :], BBt[:, :]),
               inc="dve")
        ctx.wait("dve", "dve")
        c_xu = ctx.op("dve", lambda E, Xi=Xi: E.tensor_add(Xi, Xi, YN[:, :]),
                      inc="dve")




def emit_ffn0(nc, ctx, P, C):
    rw_d, gup_d, dwn_d = P["f0_rw"], P["f0_gup"], P["f0_dwn"]
    shgu_d, shdw_d = P["f0_shgu"], P["f0_shdw"]
    X, IDN = P["X"], P["IDN"]

    XP = C.t("XP", [128, NCH * S], BF16)
    GU = C.t("GU", [128, NCH * 2048], BF16)
    DW = C.t("DW", [128, 2 * NCH * D], BF16)
    Hb = C.t("Hb", [128, 8 * 2048], BF16)
    SG = C.t("SG", [128, 2048], BF16)
    XT = C.t("XT", [128, 2 * D], BF16)
    RW = C.t("RW", [128, NCH * 16], BF16)
    LG = C.t("LG", [128, NT * 16], F32)
    WF = C.t("WF", [128, NT * 16], F32)
    M1 = C.t("M1", [128, 1], F32)
    M2 = C.t("M2", [128, 1], F32)
    MSK1 = C.t("MSK1", [128, 16], F32)
    LG2 = C.t("LG2", [128, 16], F32)
    GTS = C.t("GTS", [128, 16], F32)
    MV = C.t("MV", [128, 2 * NT], F32)
    RS = C.t("RS", [128, NT], F32)
    BN6 = C.t("BN6", [128, 192], F32)
    SC16 = C.t("SC16", [128, 16], F32)
    EPS = C.t("EPS", [128, 1], F32)

    pt_ps = P["ptv"]
    GA = P["bigA"]
    GB = P["bigB"]
    m_ps = P["bigA"]

    B = {"X": X, "XP": XP, "XT": XT, "pt_ps": pt_ps, "BN6": BN6,
         "SC16": SC16, "MV": MV, "RS": RS, "EPS": EPS, "IDN": IDN,
         "Xv": X[:, :].rearrange("p (i d) -> p i d", d=D),
         "XPv": XP[:, :].rearrange("p (c t) -> p c t", t=S)}
    XPv = B["XPv"]
    GUv = GU[:, :].rearrange("p (c h) -> p c h", h=2048)

    ctx.dma("pool", "ld_rw", RW[:, :].rearrange("p (c n) -> p c n", n=16),
            rw_d.rearrange("(c p) n -> p c n", p=128))

    def gu_src(ei):
        return shgu_d[ei] if ei < 2 else gup_d[ei - 2]

    def dw_src(ei):
        return shdw_d[ei] if ei < 2 else dwn_d[ei - 2]

    c_gu = {0: ctx.dma("pool", "lgu",
                       GUv, gu_src(0).rearrange("(c p) h -> p c h", p=128))}
    c_dw = {0: ctx.dma("pool", "ldw0",
                       DW[:, 0:NCH * D].rearrange("p (c h) -> p c h", h=D),
                       dw_src(0).rearrange("(c p) h -> p c h", h=D, p=128))}

    c_xp = emit_load_ln_xp(ctx, nc, B, 1e-5, load_x=False)

    ctx.wait("pe", "ld_c", ctx.cnt["ld_c"])
    ctx.wait("pe", "ld_rw", ctx.cnt["ld_rw"])
    for i in range(NT):
        ctx.wait("pe", "dve", ctx.cnt["dve"])
        for c in range(NCH):
            r = ctx.op("pe", lambda E, c=c, i=i:
                       E.matmul(m_ps[:, 0:16], XPv[:, c, 128 * i:128 * (i + 1)],
                                RW[:, 16 * c:16 * (c + 1)],
                                start=(c == 0), stop=(c == NCH - 1)),
                       inc="pe" if c == NCH - 1 else None)
        ctx.wait("dve", "pe", r)
        ctx.op("dve", lambda E, i=i:
               E.tensor_copy(LG[:, 16 * i:16 * (i + 1)], m_ps[:, 0:16]),
               inc="dve")
    ctx.wait("act", "dve", ctx.cnt["dve"])
    c_wf = 0
    for i in range(NT):
        lg = LG[:, 16 * i:16 * (i + 1)]
        ctx.wait("act", "dve", c_wf)
        ctx.wait("dve", "dve")
        c_m1 = ctx.op("dve", lambda E, lg=lg:
                      E.reduce_max(M1[:, :], lg, axis=AX.X), inc="dve")
        ctx.wait("dve", "dve", c_m1)
        ctx.op("dve", lambda E, lg=lg:
               E.tensor_scalar(MSK1[:, :], lg, M1[:, :], None, ALU.is_ge),
               inc="dve")
        ctx.wait("dve", "dve")
        ctx.op("dve", lambda E, lg=lg:
               E.scalar_tensor_tensor(LG2[:, :], MSK1[:, :], NEG, lg,
                                      ALU.mult, ALU.add), inc="dve")
        ctx.wait("dve", "dve")
        c_m2 = ctx.op("dve", lambda E:
                      E.reduce_max(M2[:, :], LG2[:, :], axis=AX.X), inc="dve")
        ctx.wait("dve", "dve", c_m2)
        c_msk = ctx.op("dve", lambda E, lg=lg:
                       E.tensor_scalar(MSK1[:, :], lg, M2[:, :], None,
                                       ALU.is_ge), inc="dve")
        c_sig = ctx.op("act", lambda E, lg=lg:
                       E.activation(GTS[:, :], lg, AF.Sigmoid), inc="act")
        ctx.wait("dve", "act", c_sig)
        ctx.wait("dve", "dve", c_msk)
        c_wf = ctx.op("dve", lambda E, i=i:
                      E.tensor_mul(WF[:, 16 * i:16 * (i + 1)], MSK1[:, :],
                                   GTS[:, :]), inc="dve")
    c_wf_all = c_wf

    c_pre = ctx.cnt["dve"]
    c_dph = {0: c_pre, 1: c_pre}
    r_d_last = 0
    prev_mul7 = 0
    for ei in range(18):
        real = ei >= 2
        dwbuf = ei % 2
        ctx.wait("pe", "lgu", c_gu[ei])
        ctx.wait("pe", f"ldw{dwbuf}", c_dw[ei])
        if ei == 0:
            ctx.wait("pe", "dve", c_xp)
        dwv = DW[:, dwbuf * NCH * D:(dwbuf + 1) * NCH * D] \
            .rearrange("p (c h) -> p c h", h=D)
        if ei + 1 < 18:
            ctx.wait("pool", "pe", ctx.cnt["pe"])
            c_dw[ei + 1] = ctx.dma(
                "pool", f"ldw{(ei + 1) % 2}",
                DW[:, ((ei + 1) % 2) * NCH * D:((ei + 1) % 2 + 1) * NCH * D]
                .rearrange("p (c h) -> p c h", h=D),
                dw_src(ei + 1).rearrange("(c p) h -> p c h", p=128))
        c_silu = {}
        c_mul = {}
        for f in range(NCH):
            if f == 0:
                ctx.wait("pe", "dve", max(c_dph.values()))
            else:
                ctx.wait("pe", "act", c_silu[f - 1])
            for c in range(NCH):
                for tq in range(4):
                    ctx.op("pe", lambda E, c=c, f=f, tq=tq:
                           E.matmul(GA[:, 512 * tq:512 * (tq + 1)],
                                    GUv[:, c, 128 * f:128 * (f + 1)],
                                    XPv[:, c, 512 * tq:512 * (tq + 1)],
                                    start=(c == 0), stop=(c == NCH - 1)),
                           inc="pe" if (c == NCH - 1 and tq == 3) else None)
            r_g = ctx.cnt["pe"]
            if f >= 1:
                ctx.wait("pe", "dve", c_mul[f - 1])
            for c in range(NCH):
                for tq in range(4):
                    ctx.op("pe", lambda E, c=c, f=f, tq=tq:
                           E.matmul(GB[:, 512 * tq:512 * (tq + 1)],
                                    GUv[:, c, 1024 + 128 * f:1024 + 128 * (f + 1)],
                                    XPv[:, c, 512 * tq:512 * (tq + 1)],
                                    start=(c == 0), stop=(c == NCH - 1)),
                           inc="pe" if (c == NCH - 1 and tq == 3) else None)
            r_u = ctx.cnt["pe"]
            ctx.wait("act", "pe", r_g)
            if f == 0:
                ctx.wait("act", "dve", prev_mul7)
            c_silu[f] = ctx.op("act", lambda E:
                               E.activation(SG[:, :], GA[:, 0:2048], AF.Silu),
                               inc="act")
            ctx.wait("dve", "pe", r_u)
            if f == 0:
                ctx.wait("dve", "pe", r_d_last)
            ctx.wait("dve", "act", c_silu[f])
            c_mul[f] = ctx.op("dve", lambda E, f=f:
                              E.tensor_mul(Hb[:, 2048 * f:2048 * (f + 1)],
                                           GB[:, 0:2048], SG[:, :]), inc="dve")
        prev_mul7 = c_mul[NCH - 1]
        if ei + 1 < 18:
            ctx.wait("pool", "pe", r_u)
            c_gu[ei + 1] = ctx.dma(
                "pool", "lgu", GUv,
                gu_src(ei + 1).rearrange("(c p) h -> p c h", p=128))
        ctx.wait("pe", "act", c_silu[NCH - 1])
        ctx.wait("pe", "dve", c_mul[NCH - 1])
        for it in range(NT):
            par = it % 2
            base = par * 1024
            ctx.wait("pe", "dve", c_dph[par])
            for f in range(NCH):
                for dh in range(2):
                    ctx.op("pe", lambda E, f=f, dh=dh, it=it, base=base, dwv=dwv:
                           E.matmul(GA[:, base + 512 * dh:base + 512 * (dh + 1)],
                                    Hb[:, 2048 * f + 128 * it:2048 * f + 128 * (it + 1)],
                                    dwv[:, f, 512 * dh:512 * (dh + 1)],
                                    start=(f == 0), stop=(f == NCH - 1)),
                           inc="pe" if (f == NCH - 1 and dh == 1) else None)
            r_d_last = ctx.cnt["pe"]
            ctx.wait("dve", "pe", r_d_last)
            Xi = X[:, 1024 * it:1024 * (it + 1)]
            if real:
                wcol = WF[:, 16 * it + (ei - 2):16 * it + (ei - 2) + 1]
                c_dph[par] = ctx.op(
                    "dve", lambda E, Xi=Xi, base=base, wcol=wcol:
                    E.scalar_tensor_tensor(Xi, GA[:, base:base + 1024], wcol, Xi,
                                           ALU.mult, ALU.add), inc="dve")
            else:
                c_dph[par] = ctx.op(
                    "dve", lambda E, Xi=Xi, base=base:
                    E.tensor_add(Xi, Xi, GA[:, base:base + 1024]), inc="dve")




def emit_attn1(nc, ctx, P, C):
    w3_d, wo_d = P["a1_w3"], P["a1_wo"]
    X, IDN = P["X"], P["IDN"]

    XQ = C.t("XQ", [128, NCH * S], BF16)
    KF = C.t("KF", [128, NT * D], BF16)
    VT = C.t("VT", [128, NT * D], BF16)
    WO = C.t("WO", [128, NCH * D], BF16)
    W3 = C.t("W3", [128, 3 * 512], BF16)
    KVZ = C.t("KVZ", [128, 16 * 65], BF16)
    ONES = C.t("ONES", [128, 1], BF16)
    MN = C.t("MN", [128, D], F32)
    EXs = C.t("EXs", [128, D], BF16)
    RL = C.t("RL", [128, D], BF16)
    QT = C.t("QT", [128, D], BF16)
    XT = C.t("XT", [128, 2 * D], BF16)
    DN = C.t("DN", [128, 1], F32)
    RC = C.t("RC", [128, 1], F32)
    EPS6 = C.t("EPS6", [128, 1], F32)
    MV = C.t("MV", [128, 2 * NT], F32)
    RS = C.t("RS", [128, NT], F32)
    BN6 = C.t("BN6", [128, 192], F32)
    SC16 = C.t("SC16", [128, 16], F32)
    EPS = C.t("EPS", [128, 1], F32)

    pt_ps = P["ptv"]
    p_ps = P["bigB"]
    d_ps = P["bigB"]
    nd_ps = P["bigA"]
    KZ = P["bigA"]

    B = {"X": X, "XP": XQ, "XT": XT, "pt_ps": pt_ps, "BN6": BN6,
         "SC16": SC16, "MV": MV, "RS": RS, "EPS": EPS, "IDN": IDN,
         "Xv": X[:, :].rearrange("p (i d) -> p i d", d=D),
         "XPv": XQ[:, :].rearrange("p (c t) -> p c t", t=S)}
    XPv = B["XPv"]
    WOv = WO[:, :].rearrange("p (c d) -> p c d", d=D)

    c_w3 = ctx.dma("pool", "ld_w", W3[:, :], w3_d)
    c_wo = ctx.dma("pool", "ld_wo", WOv, wo_d.rearrange("(c p) d -> p c d", p=128))
    ctx.op("dve", lambda E: E.memset(ONES[:, :], 1.0), inc="dve")
    ctx.op("dve", lambda E: E.memset(EPS6[:, :], 1e-6), inc="dve")
    c_ones = ctx.cnt["dve"]

    c_xp = emit_load_ln_xp(ctx, nc, B, 1e-5, load_x=False)

    ctx.wait("pe", "ld_w", c_w3)
    c_evac = 0
    c_pt = ctx.cnt["dve"]
    for i in range(NT):
        for m, nm in ((0, "q"), (1, "k"), (2, "v")):
            ctx.wait("pe", "dve", c_evac)
            ctx.wait("pe", "act", ctx.cnt["act"])
            for par in range(2):
                for j in range(8):
                    h = 2 * j + par
                    pbp = 64 * par
                    pc = 512 * par + 64 * j
                    r = ctx.op("pe", lambda E, h=h, pbp=pbp, pc=pc, m=m, i=i:
                               E.matmul(p_ps[:, pc:pc + 64],
                                        XPv[pbp:pbp + 64, h // 2, 128 * i:128 * (i + 1)],
                                        W3[pbp:pbp + 64, 512 * m + 64 * (h // 2):
                                           512 * m + 64 * (h // 2) + 64],
                                        start=True, stop=True),
                               inc="pe" if (par == 1 and j == 7) else None)
            if m < 2:
                ctx.wait("dve", "pe", r)
                c_mn = ctx.op("dve", lambda E:
                              E.tensor_scalar_min(MN[:, :], p_ps[:, 0:1024], 0.0),
                              inc="dve")
                c_rl = ctx.op("dve", lambda E:
                              E.tensor_scalar_max(RL[:, :], p_ps[:, 0:1024], 0.0),
                              inc="dve")
                ctx.wait("act", "dve", c_mn)
                c_ex = ctx.op("act", lambda E:
                              E.activation(EXs[:, :], MN[:, :], AF.Exp),
                              inc="act")
                ctx.wait("dve", "act", c_ex)
                ctx.wait("dve", "dve", c_rl)
                dst = QT[:, :] if m == 0 else KF[:, D * i:D * (i + 1)]
                c_evac = ctx.op("dve", lambda E, dst=dst:
                                E.tensor_add(dst, EXs[:, :], RL[:, :]),
                                inc="dve")
            else:
                ctx.wait("dve", "pe", r)
                c_evac = ctx.op("dve", lambda E, i=i:
                                E.tensor_copy(VT[:, D * i:D * (i + 1)],
                                              p_ps[:, 0:1024]), inc="dve")
        for par in range(2):
            ctx.wait("pe", "dve", c_evac if par == 0 else c_pt)
            ctx.wait("pe", "dve", c_pt)
            for j in range(8):
                h = 2 * j + par
                pc = 512 * (h % 2) + 64 * (h // 2)
                r = ctx.op("pe", lambda E, j=j, pc=pc:
                           E.transpose(pt_ps[0:64, 128 * j:128 * (j + 1)],
                                       QT[:, pc:pc + 64], IDN[:, :]),
                           inc="pe" if j == 7 else None)
            ctx.wait("dve", "pe", r)
            c_pt = ctx.op("dve", lambda E, par=par, i=i: E.tensor_copy(
                XPv[64 * par:64 * par + 64, :, 128 * i:128 * (i + 1)],
                pt_ps[0:64, :].rearrange("p (c t) -> p c t", t=128)),
                inc="dve")
    c_proj = ctx.cnt["dve"]

    ctx.wait("pe", "dve", c_proj)
    ctx.wait("pe", "dve", c_ones)
    c_kvz_evac = 0
    for h in range(16):
        ko = (h % 2) * 65
        ctx.wait("pe", "dve", c_kvz_evac)
        pc = 512 * (h % 2) + 64 * (h // 2)
        for tt in range(NT):
            ctx.op("pe", lambda E, tt=tt, ko=ko, pc=pc:
                   E.matmul(KZ[0:64, ko:ko + 64],
                            KF[:, 1024 * tt + pc:1024 * tt + pc + 64],
                            VT[:, 1024 * tt + pc:1024 * tt + pc + 64],
                            start=(tt == 0), stop=(tt == NT - 1)))
        for tt in range(NT):
            r = ctx.op("pe", lambda E, tt=tt, ko=ko, pc=pc:
                       E.matmul(KZ[0:64, ko + 64:ko + 65],
                                KF[:, 1024 * tt + pc:1024 * tt + pc + 64],
                                ONES[:, :],
                                start=(tt == 0), stop=(tt == NT - 1)),
                       inc="pe" if tt == NT - 1 else None)
        ctx.wait("dve", "pe", r)
        ctx.op("dve", lambda E, h=h, ko=ko:
               E.tensor_copy(KVZ[0:64, 65 * h:65 * (h + 1)],
                             KZ[0:64, ko:ko + 65]), inc="dve")
        c_kvz_evac = ctx.op("dve", lambda E, h=h, ko=ko:
                            E.tensor_copy(KVZ[64:128, 65 * h:65 * (h + 1)],
                                          KZ[0:64, ko:ko + 65]), inc="dve")
    c_kvz = c_kvz_evac

    c_ct = 0
    for i in range(NT):
        for h in range(16):
            pbp = 64 * (h % 2)
            no = ((i * 16 + h) % 2) * 65
            ctx.wait("pe", "dve", c_ct)
            ctx.wait("pe", "dve", c_kvz)
            r = ctx.op("pe", lambda E, h=h, i=i, pbp=pbp, no=no:
                       E.matmul(nd_ps[:, 512 + no:512 + no + 65],
                                XPv[pbp:pbp + 64, h // 2, 128 * i:128 * (i + 1)],
                                KVZ[pbp:pbp + 64, 65 * h:65 * (h + 1)],
                                start=True, stop=True), inc="pe")
            ctx.wait("dve", "pe", r)
            ctx.op("dve", lambda E, no=no:
                   E.tensor_scalar(DN[:, :], nd_ps[:, 512 + no + 64:512 + no + 65],
                                   EPS6[:, :], None, ALU.add), inc="dve")
            ctx.wait("dve", "dve")
            ctx.op("dve", lambda E: E.reciprocal(RC[:, :], DN[:, :]), inc="dve")
            ctx.wait("dve", "dve")
            c_ct = ctx.op("dve", lambda E, i=i, h=h, no=no:
                          E.tensor_scalar(KF[:, 1024 * i + 64 * h:
                                             1024 * i + 64 * (h + 1)],
                                          nd_ps[:, 512 + no:512 + no + 64],
                                          RC[:, :], None, ALU.mult), inc="dve")

    VTv = VT[:, :].rearrange("p (c t) -> p c t", t=S)
    c_pt2 = c_pt
    for i in range(NT):
        ctx.wait("pe", "dve", c_ct)
        ctx.wait("pe", "dve", c_pt2)
        for c in range(NCH):
            r = ctx.op("pe", lambda E, c=c, i=i:
                       E.transpose(pt_ps[:, c * 128:(c + 1) * 128],
                                   KF[:, 1024 * i + 128 * c:1024 * i + 128 * (c + 1)],
                                   IDN[:, :]),
                       inc="pe" if c == NCH - 1 else None)
        ctx.wait("dve", "pe", r)
        c_pt2 = ctx.op("dve", lambda E, i=i: E.tensor_copy(
            VTv[:, :, 128 * i:128 * (i + 1)],
            pt_ps[:, :].rearrange("p (c t) -> p c t", t=128)), inc="dve")
    c_ctf = c_pt2

    ctx.wait("pe", "ld_wo", c_wo)
    ctx.wait("pe", "dve", c_ctf)
    c_y = 0
    for i in range(NT):
        ctx.wait("pe", "dve", c_y)
        for hh in range(2):
            for c in range(NCH):
                r = ctx.op("pe", lambda E, c=c, hh=hh, i=i:
                           E.matmul(d_ps[:, 512 * hh:512 * (hh + 1)],
                                    VTv[:, c, 128 * i:128 * (i + 1)],
                                    WOv[:, c, 512 * hh:512 * (hh + 1)],
                                    start=(c == 0), stop=(c == NCH - 1)),
                           inc="pe" if (hh == 1 and c == NCH - 1) else None)
        ctx.wait("dve", "pe", r)
        c_y = ctx.op("dve", lambda E, i=i:
                     E.tensor_add(X[:, 1024 * i:1024 * (i + 1)],
                                  X[:, 1024 * i:1024 * (i + 1)],
                                  d_ps[:, 0:1024]), inc="dve")




def emit_ffn1_sparse(nc, ctx, P, C):
    rw_d, br_d, b2_d = P["f1_rw"], P["f1_br"], P["f1_b2"]
    w1b, w2b, b1g = P["f1_w1b"], P["f1_w2b"], P["f1_b1g"]
    xgscr, yscr = P["f1_xgscr"], P["f1_yscr"]
    tri_d, iq_d = P["f1_tri"], P["f1_iq"]
    X, IDN = P["X"], P["IDN"]
    NB = 15
    IOA = bass.IndirectOffsetOnAxis

    XP = C.t("XP", [128, NCH * S], BF16)
    XT = C.t("XT", [128, 2 * D], BF16)
    W1Q = C.t("W1Q", [128, 2 * 8192], BF16)
    W2Q = C.t("W2Q", [128, 2 * 8192], BF16)
    XGT = C.t("XGT", [128, 2 * 1024], BF16)
    XG = C.t("XG", [128, 2 * 8 * 256], BF16)
    HB = C.t("HB", [128, 2 * 8 * 256], BF16)
    Y = C.t("Y", [128, 2 * 1024], F32)
    B2B = C.t("B2B", [128, 2 * 1024], F32)
    B1B = C.t("B1B", [128, 64], F32)
    XT2 = XT
    TRI = C.t("TRI", [128, 256], BF16)
    IQ = C.t("IQ", [128, 8], F32)
    RW = C.t("RW", [128, NCH * 8], BF16)
    BR = C.t("BR", [128, 8], F32)
    SCO = C.t("SCO", [128, NT * 8], F32)
    OH = C.t("OH", [128, NT * 8], F32)
    OHB = C.t("OHB", [128, NT * 8], BF16)
    CSC = C.t("CSC", [128, NT * 8], F32)
    RK = C.t("RK", [128, NT * 8], F32)
    POS = C.t("POS", [128, NT], F32)
    POSU = C.t("POSU", [128, NT], U32)
    NBL = C.t("NBL", [128, 8], F32)
    PC = C.t("PC", [128, 8], F32)
    SA = C.t("SA", [128, 8], F32)
    SB = C.t("SB", [128, 8], F32)
    SST = C.t("SST", [128, 8], F32)
    BE = C.t("BE", [128, NB], F32)
    T8 = C.t("T8", [128, 8], F32)
    WIX = C.t("WIX", [128, 96], U32)
    WIXF = C.t("WIXF", [128, 1], F32)
    M1 = C.t("M1", [128, 1], F32)
    MV = C.t("MV", [128, 2 * NT], F32)
    RS = C.t("RS", [128, NT], F32)
    BN6 = C.t("BN6", [128, 192], F32)
    SC16 = C.t("SC16", [128, 16], F32)
    EPS = C.t("EPS", [128, 1], F32)

    pt_ps = P["ptv"]
    r_ps = P["bigA"]
    a_ps = P["bigA"]
    h_ps = P["bigA"]
    dn0 = P["bigB"]
    dn1 = P["bigA"]

    B = {"X": X, "XP": XP, "XT": XT, "pt_ps": pt_ps, "BN6": BN6,
         "SC16": SC16, "MV": MV, "RS": RS, "EPS": EPS, "IDN": IDN,
         "Xv": X[:, :].rearrange("p (i d) -> p i d", d=D),
         "XPv": XP[:, :].rearrange("p (c t) -> p c t", t=S)}
    XPv = B["XPv"]
    XGv = XG[:, :].rearrange("p (b c t) -> p b c t", c=NCH, t=256)
    HBv = HB[:, :].rearrange("p (b c t) -> p b c t", c=8, t=256)

    ctx.dma("sp", "ld_c", TRI[:, :], tri_d)
    ctx.dma("sp", "ld_c", IQ[:, :], iq_d)
    ctx.dma("sp", "ld_c", BR[:, :], br_d)
    ctx.dma("pool", "ld_rw", RW[:, :].rearrange("p (c n) -> p c n", n=8),
            rw_d.rearrange("(c p) n -> p c n", p=128))

    c_xp = emit_load_ln_xp(ctx, nc, B, 1e-5, load_x=False)
    if FFN1_TRUNC == 1:
        return

    ctx.wait("pe", "ld_c", ctx.cnt["ld_c"])
    ctx.wait("pe", "ld_rw", ctx.cnt["ld_rw"])
    for i in range(NT):
        ctx.wait("pe", "dve", ctx.cnt["dve"])
        for c in range(NCH):
            r = ctx.op("pe", lambda E, c=c, i=i:
                       E.matmul(r_ps[:, 0:8], XPv[:, c, 128 * i:128 * (i + 1)],
                                RW[:, 8 * c:8 * (c + 1)],
                                start=(c == 0), stop=(c == NCH - 1)),
                       inc="pe" if c == NCH - 1 else None)
        ctx.wait("dve", "pe", r)
        ctx.op("dve", lambda E, i=i:
               E.scalar_tensor_tensor(SCO[:, 8 * i:8 * (i + 1)], r_ps[:, 0:8],
                                      1.0, BR[:, :], ALU.mult, ALU.add),
               inc="dve")

    for i in range(NT):
        sco = SCO[:, 8 * i:8 * (i + 1)]
        ctx.wait("dve", "dve")
        c_m1 = ctx.op("dve", lambda E, sco=sco:
                      E.reduce_max(M1[:, :], sco, axis=AX.X), inc="dve")
        ctx.wait("dve", "dve", c_m1)
        ctx.op("dve", lambda E, sco=sco, i=i:
               E.tensor_scalar(OH[:, 8 * i:8 * (i + 1)], sco, M1[:, :], None,
                               ALU.is_ge), inc="dve")
        ctx.wait("dve", "dve")
        ctx.op("dve", lambda E, i=i:
               E.tensor_copy(OHB[:, 8 * i:8 * (i + 1)],
                             OH[:, 8 * i:8 * (i + 1)]), inc="dve")
    c_ohb = ctx.cnt["dve"]

    if FFN1_TRUNC == 2:
        return
    ctx.wait("pe", "dve", c_ohb)
    for i in range(NT):
        ctx.wait("pe", "dve", ctx.cnt["dve"])
        ctx.op("pe", lambda E, i=i:
               E.matmul(a_ps[:, 512:520], TRI[:, 128:256],
                        OHB[:, 8 * i:8 * (i + 1)], start=True, stop=True))
        r = ctx.op("pe", lambda E, i=i:
                   E.matmul(a_ps[:, 640:648], TRI[:, 0:128],
                            OHB[:, 8 * i:8 * (i + 1)], start=True, stop=True),
                   inc="pe")
        ctx.wait("dve", "pe", r)
        if i == 0:
            ctx.op("dve", lambda E: E.tensor_copy(CSC[:, 0:8], a_ps[:, 512:520]),
                   inc="dve")
            ctx.op("dve", lambda E: E.tensor_copy(RK[:, 0:8], a_ps[:, 640:648]),
                   inc="dve")
        else:
            ctx.wait("dve", "dve")
            ctx.op("dve", lambda E, i=i:
                   E.tensor_add(CSC[:, 8 * i:8 * (i + 1)], a_ps[:, 512:520],
                                CSC[:, 8 * (i - 1):8 * i]), inc="dve")
            ctx.op("dve", lambda E, i=i:
                   E.tensor_add(RK[:, 8 * i:8 * (i + 1)], a_ps[:, 640:648],
                                CSC[:, 8 * (i - 1):8 * i]), inc="dve")

    cnt_ap = CSC[:, 8 * (NT - 1):8 * NT]
    ctx.wait("dve", "dve")
    ctx.op("dve", lambda E: E.memset(NBL[:, :], 0.0), inc="dve")
    for k in range(1, 9):
        ctx.wait("dve", "dve")
        ctx.op("dve", lambda E, k=k:
               E.tensor_scalar(T8[:, :], cnt_ap, 256.0 * (k - 1) + 1.0, None,
                               ALU.is_ge), inc="dve")
        ctx.wait("dve", "dve")
        ctx.op("dve", lambda E: E.tensor_add(NBL[:, :], NBL[:, :], T8[:, :]),
               inc="dve")
    ctx.wait("dve", "dve")
    ctx.op("dve", lambda E: E.tensor_scalar(PC[:, :], NBL[:, :], 256.0, None,
                                            ALU.mult), inc="dve")
    ctx.wait("dve", "dve")
    ctx.op("dve", lambda E: E.memset(SA[:, :], 0.0), inc="dve")
    ctx.wait("dve", "dve")
    ctx.op("dve", lambda E: E.tensor_copy(SA[:, 1:8], PC[:, 0:7]), inc="dve")
    ctx.wait("dve", "dve")
    ctx.op("dve", lambda E: E.tensor_copy(SB[:, :], SA[:, :]), inc="dve")
    ctx.wait("dve", "dve")
    ctx.op("dve", lambda E: E.tensor_add(SB[:, 1:8], SA[:, 1:8], SA[:, 0:7]),
           inc="dve")
    ctx.wait("dve", "dve")
    ctx.op("dve", lambda E: E.tensor_copy(SST[:, :], SB[:, :]), inc="dve")
    ctx.wait("dve", "dve")
    ctx.op("dve", lambda E: E.tensor_add(SST[:, 2:8], SB[:, 2:8], SB[:, 0:6]),
           inc="dve")
    ctx.wait("dve", "dve")
    ctx.op("dve", lambda E: E.tensor_copy(SA[:, :], SST[:, :]), inc="dve")
    ctx.wait("dve", "dve")
    ctx.op("dve", lambda E: E.tensor_add(SA[:, 4:8], SST[:, 4:8], SST[:, 0:4]),
           inc="dve")
    ctx.wait("dve", "dve")

    if FFN1_TRUNC == 3:
        return
    for i in range(NT):
        ctx.op("dve", lambda E, i=i:
               E.tensor_add(T8[:, :], RK[:, 8 * i:8 * (i + 1)], SA[:, :]),
               inc="dve")
        ctx.wait("dve", "dve")
        ctx.op("dve", lambda E, i=i:
               E.tensor_mul(T8[:, :], T8[:, :], OH[:, 8 * i:8 * (i + 1)]),
               inc="dve")
        ctx.wait("dve", "dve")
        ctx.op("dve", lambda E, i=i:
               E.reduce_sum(POS[:, i:i + 1], T8[:, :], axis=AX.X), inc="dve")
        ctx.wait("dve", "dve")
    c_pos = ctx.op("dve", lambda E: E.tensor_copy(POSU[:, :], POS[:, :]),
                   inc="dve")

    for j in range(NB):
        ctx.wait("dve", "dve")
        ctx.op("dve", lambda E, j=j:
               E.tensor_scalar(T8[:, :], SA[:, :], 256.0 * j, None, ALU.is_le),
               inc="dve")
        ctx.wait("dve", "dve")
        ctx.op("dve", lambda E, j=j:
               E.reduce_sum(BE[:, j:j + 1], T8[:, :], axis=AX.X), inc="dve")
        ctx.wait("dve", "dve")
        ctx.op("dve", lambda E, j=j:
               E.tensor_scalar(BE[:, j:j + 1], BE[:, j:j + 1], 1.0, None,
                               ALU.subtract), inc="dve")
    for j in range(NB):
        for q in range(4):
            ctx.wait("dve", "dve")
            ctx.op("dve", lambda E, j=j, q=q:
                   E.scalar_tensor_tensor(WIXF[:, :], BE[:, j:j + 1], 512.0,
                                          IQ[:, q:q + 1], ALU.mult, ALU.add),
                   inc="dve")
            ctx.wait("dve", "dve")
            ctx.op("dve", lambda E, j=j, q=q:
                   E.tensor_copy(WIX[:, 4 * j + q:4 * j + q + 1], WIXF[:, :]),
                   inc="dve")
        ctx.wait("dve", "dve")
        ctx.op("dve", lambda E, j=j:
               E.scalar_tensor_tensor(WIXF[:, :], BE[:, j:j + 1], 128.0,
                                      IQ[:, 4:5], ALU.mult, ALU.add),
               inc="dve")
        ctx.wait("dve", "dve")
        ctx.op("dve", lambda E, j=j:
               E.tensor_copy(WIX[:, 60 + j:61 + j], WIXF[:, :]), inc="dve")
        ctx.wait("dve", "dve")
        ctx.op("dve", lambda E, j=j:
               E.tensor_copy(WIX[:, 75 + j:76 + j], BE[:, j:j + 1]), inc="dve")
    c_wix = ctx.cnt["dve"]

    if FFN1_TRUNC == 4:
        return
    ctx.wait("pool", "dve", c_pos)
    sct = {}
    for i in range(NT):
        buf = i % 2
        sgs = "sg0" if buf == 0 else "sg1"
        if i >= 2:
            ctx.wait("dve", sgs, sct[i - 2])
        c_x2 = ctx.op("dve", lambda E, i=i, buf=buf:
                      E.tensor_scalar(XT2[:, 1024 * buf:1024 * (buf + 1)],
                                      X[:, 1024 * i:1024 * (i + 1)],
                                      MV[:, 2 * i:2 * i + 1], RS[:, i:i + 1],
                                      ALU.subtract, ALU.mult), inc="dve")
        ctx.wait("pool", "dve", c_x2)
        sct[i] = ctx.op("pool", lambda E, i=i, buf=buf:
                        E.indirect_dma_start(
                            out=xgscr, out_offset=IOA(POSU[:, i:i + 1], 0),
                            in_=XT2[:, 1024 * buf:1024 * (buf + 1)],
                            in_offset=None, bounds_check=3839,
                            oob_is_err=False), inc=sgs, n=16)

    if FFN1_TRUNC == 5:
        return
    ctx.wait("pool", "dve", c_wix)
    ctx.wait("pool", "wpp", ctx.cnt["wpp"])
    ctx.wait("sp", "sg0", ctx.cnt["sg0"])
    ctx.wait("sp", "sg1", ctx.cnt["sg1"])

    c_lw1 = {}
    c_lw2 = {}
    c_lb = {}
    h_done = {}
    d_done = {}
    gelu_done = {}
    slot_hist = {0: 0, 1: 0}
    c_xgld = {}
    c_tp = {}
    c_xgcp = {}
    c_ptv = [0]
    c_yevac = {0: 0, 1: 0}
    c_yst = {}

    def emit_w1gather(G):
        buf = G % 2
        if G - 2 in h_done:
            ctx.wait("pool", "pe", h_done[G - 2])
        c_lw1[G] = ctx.op("pool", lambda E, buf=buf, G=G:
                          E.indirect_dma_start(
                              out=W1Q[:, 8192 * buf:8192 * (buf + 1)],
                              out_offset=None, in_=w1b,
                              in_offset=IOA(WIX[:, G:G + 1], 0)),
                          inc="lw1a" if buf == 0 else "lw1b", n=16)

    def emit_w2gather(G):
        buf = G % 2
        if G - 2 in d_done:
            ctx.wait("pool", "pe", d_done[G - 2])
        c_lw2[G] = ctx.op("pool", lambda E, buf=buf, G=G:
                          E.indirect_dma_start(
                              out=W2Q[:, 8192 * buf:8192 * (buf + 1)],
                              out_offset=None, in_=w2b,
                              in_offset=IOA(WIX[:, G:G + 1], 0)),
                          inc="lw2a" if buf == 0 else "lw2b", n=16)

    def emit_bgather(j):
        jb = j % 2
        if j - 2 >= 0:
            ctx.wait("pool", "act", gelu_done[(4 * (j - 2) + 3, 7)])
            ctx.wait("pool", "dve", max(c_yevac.values()))
        lbs = "lba" if jb == 0 else "lbb"
        ctx.op("pool", lambda E, j=j, jb=jb:
               E.indirect_dma_start(out=B1B[:, 32 * jb:32 * (jb + 1)],
                                    out_offset=None, in_=b1g,
                                    in_offset=IOA(WIX[:, 60 + j:61 + j], 0)),
               inc=lbs, n=16)
        c_lb[j] = ctx.op("pool", lambda E, j=j, jb=jb:
                         E.indirect_dma_start(
                             out=B2B[:, 1024 * jb:1024 * (jb + 1)],
                             out_offset=None, in_=b2_d,
                             in_offset=IOA(WIX[:, 75 + j:76 + j], 0)),
                         inc=lbs, n=16)

    def emit_xgload(j):
        for h in range(2):
            if (j - 1, h) in c_tp:
                ctx.wait("sp", "pe", c_tp[(j - 1, h)])
            c_xgld[(j, h)] = ctx.dma(
                "sp", "xg", XGT[:, 1024 * h:1024 * (h + 1)],
                xgscr[256 * j + 128 * h:256 * j + 128 * (h + 1), :])

    def emit_xgbuild(j):
        jb = j % 2
        for h in range(2):
            ctx.wait("pe", "xg", c_xgld[(j, 1)])
            ctx.wait("pe", "dve", c_ptv[0])
            for dc in range(NCH):
                r = ctx.op("pe", lambda E, h=h, dc=dc:
                           E.transpose(pt_ps[:, 128 * dc:128 * (dc + 1)],
                                       XGT[:, 1024 * h + 128 * dc:
                                           1024 * h + 128 * (dc + 1)],
                                       IDN[:, :]),
                           inc="pe" if dc == NCH - 1 else None)
            c_tp[(j, h)] = r
            ctx.wait("dve", "pe", r)
            c_ptv[0] = ctx.op(
                "dve", lambda E, jb=jb, h=h: E.tensor_copy(
                    XGv[:, jb, :, 128 * h:128 * (h + 1)],
                    pt_ps[:, :].rearrange("p (c t) -> p c t", t=128)),
                inc="dve")
            c_xgcp[j] = c_ptv[0]

    def emit_h(j, q):
        G = 4 * j + q
        buf = G % 2
        jb = j % 2
        ctx.wait("pe", "lw1a" if G % 2 == 0 else "lw1b", c_lw1[G])
        ctx.wait("pe", "dve", c_xgcp[j])
        if q == 0:
            ctx.wait("act", "lba" if jb == 0 else "lbb", c_lb[j])
        W1Qv = W1Q[:, 8192 * buf:8192 * (buf + 1)] \
            .rearrange("p (c f) -> p c f", f=1024)
        for fc in range(8):
            slot = fc % 2
            ctx.wait("pe", "act", slot_hist[slot])
            for dc in range(NCH):
                r = ctx.op("pe", lambda E, fc=fc, dc=dc, slot=slot, jb=jb,
                           W1Qv=W1Qv:
                           E.matmul(h_ps[:, 512 * slot:512 * slot + 256],
                                    W1Qv[:, dc, 128 * fc:128 * (fc + 1)],
                                    XGv[:, jb, dc, 0:256],
                                    start=(dc == 0), stop=(dc == NCH - 1)),
                           inc="pe" if dc == NCH - 1 else None)
            ctx.wait("act", "pe", r)
            if fc == 0 and G - 2 in d_done:
                ctx.wait("act", "pe", d_done[G - 2])
            if FFN1_PARTS == 'hnog':
                cg = ctx.op(
                    "act", lambda E, fc=fc, slot=slot, buf=buf:
                    E.copy(HB[:, 2048 * buf + 256 * fc:
                              2048 * buf + 256 * (fc + 1)],
                           h_ps[:, 512 * slot:512 * slot + 256]), inc="act")
            else:
                cg = ctx.op(
                    "act", lambda E, fc=fc, q=q, slot=slot, buf=buf, jb=jb:
                    E.activation(HB[:, 2048 * buf + 256 * fc:
                                    2048 * buf + 256 * (fc + 1)],
                                 h_ps[:, 512 * slot:512 * slot + 256], AF.Gelu,
                                 bias=B1B[:, 32 * jb + 8 * q + fc:
                                          32 * jb + 8 * q + fc + 1]), inc="act")
            gelu_done[(G, fc)] = cg
            slot_hist[slot] = cg
        h_done[G] = ctx.cnt["pe"]

    def emit_down(j, q):
        G = 4 * j + q
        buf = G % 2
        ctx.wait("pe", "act", gelu_done[(G, 7)])
        ctx.wait("pe", "lw2a" if G % 2 == 0 else "lw2b", c_lw2[G])
        if q == 0:
            ctx.wait("pe", "dve", max(c_yevac.values()))
        W2Qv = W2Q[:, 8192 * buf:8192 * (buf + 1)] \
            .rearrange("p (c d) -> p c d", d=D)
        for fc in range(8):
            for half in range(2):
                base = 0 if half == 0 else 1024
                dnt = dn0 if half == 0 else dn1
                for dh in range(2):
                    r = ctx.op("pe", lambda E, fc=fc, half=half, buf=buf, q=q,
                               dnt=dnt, base=base, dh=dh, W2Qv=W2Qv:
                               E.matmul(dnt[:, base + 512 * dh:
                                            base + 512 * (dh + 1)],
                                        HB[:, 2048 * buf + 256 * fc + 128 * half:
                                           2048 * buf + 256 * fc
                                           + 128 * (half + 1)],
                                        W2Qv[:, fc, 512 * dh:512 * (dh + 1)],
                                        start=(q == 0 and fc == 0),
                                        stop=(q == 3 and fc == 7)),
                               inc="pe" if (fc == 7 and half == 1 and dh == 1)
                               else None)
        d_done[G] = r

    PARTS = FFN1_PARTS
    emit_xgload(0)
    if PARTS != 'xg':
        emit_bgather(0)
        emit_w1gather(0)
        emit_w2gather(0)
        emit_w1gather(1)
        emit_w2gather(1)
        emit_bgather(1)

    for j in range(NB if FFN1_NBLOCKS is None else FFN1_NBLOCKS):
        jb = j % 2
        emit_xgbuild(j)
        if j + 1 < NB:
            emit_xgload(j + 1)
        if PARTS == 'xg':
            continue
        if PARTS not in ('gath',):
            emit_h(j, 0)
        emit_w1gather(4 * j + 2)
        if PARTS != 'gath':
            emit_h(j, 1)
        emit_w1gather(4 * j + 3)
        if PARTS == 'full':
            emit_down(j, 0)
        emit_w2gather(4 * j + 2)
        if PARTS != 'gath':
            emit_h(j, 2)
        if j + 1 < NB:
            emit_w1gather(4 * (j + 1))
        if PARTS == 'full':
            emit_down(j, 1)
        emit_w2gather(4 * j + 3)
        if PARTS != 'gath':
            emit_h(j, 3)
        if j + 1 < NB:
            emit_w1gather(4 * (j + 1) + 1)
        if PARTS == 'full':
            emit_down(j, 2)
        if j + 1 < NB:
            emit_w2gather(4 * (j + 1))
        if PARTS == 'full':
            emit_down(j, 3)
        if j + 1 < NB:
            emit_w2gather(4 * (j + 1) + 1)
        if PARTS != 'full':
            continue
        for half in range(2):
            ctx.wait("dve", "pe", d_done[4 * j + 3])
            ctx.wait("dve", "lba" if jb == 0 else "lbb", c_lb[j])
            if (j - 1, 1) in c_yst:
                ctx.wait("dve", "ysc", c_yst[(j - 1, 1)])
            dn = dn0[:, 0:1024] if half == 0 else dn1[:, 1024:2048]
            c_yevac[half] = ctx.op(
                "dve", lambda E, half=half, jb=jb, dn=dn:
                E.tensor_add(Y[:, 1024 * half:1024 * (half + 1)], dn,
                             B2B[:, 1024 * jb:1024 * (jb + 1)]), inc="dve")
            ctx.wait("sp", "dve", c_yevac[half])
            c_yst[(j, half)] = ctx.dma(
                "sp", "ysc",
                yscr[256 * j + 128 * half:256 * j + 128 * (half + 1), :],
                Y[:, 1024 * half:1024 * (half + 1)])
        if j + 2 < NB:
            emit_bgather(j + 2)

    if FFN1_TRUNC == 6:
        return
    ctx.wait("pool", "ysc", ctx.cnt["ysc"])
    c_add = {}
    for i in range(NT):
        buf = i % 2
        ygs = "ygta" if buf == 0 else "ygtb"
        if i - 2 in c_add:
            ctx.wait("pool", "dve", c_add[i - 2])
        cg = ctx.op("pool", lambda E, i=i, buf=buf:
                    E.indirect_dma_start(
                        out=Y[:, 1024 * buf:1024 * (buf + 1)], out_offset=None,
                        in_=yscr, in_offset=IOA(POSU[:, i:i + 1], 0),
                        bounds_check=3839, oob_is_err=False),
                    inc=ygs, n=16)
        ctx.wait("dve", ygs, cg)
        c_add[i] = ctx.op("dve", lambda E, i=i, buf=buf:
                          E.tensor_add(X[:, 1024 * i:1024 * (i + 1)],
                                       X[:, 1024 * i:1024 * (i + 1)],
                                       Y[:, 1024 * buf:1024 * (buf + 1)]),
                          inc="dve")
    if FFN1_DEBUG:
        ctx.wait("sp", "dve", ctx.cnt["dve"])
        ctx.store(P["dbg_pos"], POS[:, :])
        ctx.store(P["dbg_be"][:, 0:15], BE[:, 0:NB])
        ctx.store(P["dbg_sst"], SA[:, :])
        ctx.store(P["dbg_cnt"], CSC[:, 8 * (NT - 1):8 * NT])
        ctx.store(P["dbg_xg"], xgscr)
        ctx.store(P["dbg_ys"], yscr)


def emit_ffn1(nc, ctx, P, C):
    rw_d, br_d = P["f1_rw"], P["f1_br"]
    w1_d, b1_d, w2_d, b2_d = P["f1_w1"], P["f1_b1"], P["f1_w2"], P["f1_b2"]
    X, IDN = P["X"], P["IDN"]

    XP = C.t("XP", [128, NCH * S], BF16)
    W1 = C.t("W1", [128, NCH * 2048], BF16)
    W2 = C.t("W2", [128, 16 * D], BF16)
    Hb = C.t("Hb", [128, 2 * 16 * 512], BF16)
    XT = C.t("XT", [128, 2 * D], BF16)
    RW = C.t("RW", [128, NCH * 8], BF16)
    BR = C.t("BR", [128, 8], F32)
    B1 = C.t("B1", [128, 256], F32)
    B2 = C.t("B2", [8, D], BF16)
    SCO = C.t("SCO", [128, NT * 8], F32)
    OH = C.t("OH", [128, NT * 8], F32)
    OHB = C.t("OHB", [128, 8], BF16)
    OHT = C.t("OHT", [8, S], BF16)
    M1 = C.t("M1", [128, 1], F32)
    MV = C.t("MV", [128, 2 * NT], F32)
    RS = C.t("RS", [128, NT], F32)
    BN6 = C.t("BN6", [128, 192], F32)
    SC16 = C.t("SC16", [128, 16], F32)
    EPS = C.t("EPS", [128, 1], F32)

    pt_ps = P["ptv"]
    GA = P["bigA"]
    GB = P["bigB"]
    d_ps = P["bigB"]
    m_ps = P["bigA"]

    B = {"X": X, "XP": XP, "XT": XT, "pt_ps": pt_ps, "BN6": BN6,
         "SC16": SC16, "MV": MV, "RS": RS, "EPS": EPS, "IDN": IDN,
         "Xv": X[:, :].rearrange("p (i d) -> p i d", d=D),
         "XPv": XP[:, :].rearrange("p (c t) -> p c t", t=S)}
    XPv = B["XPv"]
    W1v = W1[:, :].rearrange("p (c h) -> p c h", h=2048)
    W2v = W2[:, :].rearrange("p (c h) -> p c h", h=D)

    ctx.dma("sp", "ld_c", BR[:, :], br_d)
    ctx.dma("sp", "ld_c", B1[:, :], b1_d)
    ctx.dma("pool", "ld_rw", RW[:, :].rearrange("p (c n) -> p c n", n=8),
            rw_d.rearrange("(c p) n -> p c n", p=128))
    ctx.dma("pool", "ld_rw", B2[:, :], b2_d)

    def w1_src(se):
        return w1_d[se // 2][:, 2048 * (se % 2):2048 * (se % 2 + 1)]

    def w2_src(se):
        return w2_d[se // 2][2048 * (se % 2):2048 * (se % 2 + 1), :]

    c_w1 = {0: ctx.dma("pool", "lw1", W1v,
                       w1_src(0).rearrange("(c p) h -> p c h", p=128))}
    c_w2 = {0: ctx.dma("pool", "lw2", W2v,
                       w2_src(0).rearrange("(c p) h -> p c h", p=128))}

    c_xp = emit_load_ln_xp(ctx, nc, B, 1e-5, load_x=False)

    ctx.wait("pe", "ld_c", ctx.cnt["ld_c"])
    ctx.wait("pe", "ld_rw", ctx.cnt["ld_rw"])
    for i in range(NT):
        ctx.wait("pe", "dve", ctx.cnt["dve"])
        for c in range(NCH):
            r = ctx.op("pe", lambda E, c=c, i=i:
                       E.matmul(m_ps[:, 0:8], XPv[:, c, 128 * i:128 * (i + 1)],
                                RW[:, 8 * c:8 * (c + 1)],
                                start=(c == 0), stop=(c == NCH - 1)),
                       inc="pe" if c == NCH - 1 else None)
        ctx.wait("dve", "pe", r)
        ctx.op("dve", lambda E, i=i:
               E.scalar_tensor_tensor(SCO[:, 8 * i:8 * (i + 1)], m_ps[:, 0:8],
                                      1.0, BR[:, :], ALU.mult, ALU.add),
               inc="dve")
    c_pt = ctx.cnt["dve"]
    for i in range(NT):
        sco = SCO[:, 8 * i:8 * (i + 1)]
        ctx.wait("dve", "dve")
        c_m1 = ctx.op("dve", lambda E, sco=sco:
                      E.reduce_max(M1[:, :], sco, axis=AX.X), inc="dve")
        ctx.wait("dve", "dve", c_m1)
        ctx.op("dve", lambda E, sco=sco, i=i:
               E.tensor_scalar(OH[:, 8 * i:8 * (i + 1)], sco, M1[:, :], None,
                               ALU.is_ge), inc="dve")
        ctx.wait("dve", "dve")
        c_ohb = ctx.op("dve", lambda E, i=i:
                       E.tensor_copy(OHB[:, :], OH[:, 8 * i:8 * (i + 1)]),
                       inc="dve")
        ctx.wait("pe", "dve", c_ohb)
        ctx.wait("pe", "dve", c_pt)
        r = ctx.op("pe", lambda E:
                   E.transpose(pt_ps[0:8, 0:128], OHB[:, :], IDN[:, :]),
                   inc="pe")
        ctx.wait("dve", "pe", r)
        c_pt = ctx.op("dve", lambda E, i=i:
                      E.tensor_copy(OHT[0:8, 128 * i:128 * (i + 1)],
                                    pt_ps[0:8, 0:128]), inc="dve")
    c_oh_all = ctx.cnt["dve"]

    c_pre = ctx.cnt["dve"]
    c_dph = {0: c_pre, 1: c_pre}
    gelu_hist = [0, 0]
    c_hb_done = {0: 0, 1: 0}
    state = {"r_d": 0}

    def emit_down(se, tq):
        ex = se // 2
        hbo = (tq % 2) * 8192
        ctx.wait("pe", "act", c_hb_done[tq % 2])
        for ts in range(4):
            it = 4 * tq + ts
            par = it % 2
            base = par * 1024
            ctx.wait("pe", "dve", c_dph[par])
            for f in range(16):
                for dh in range(2):
                    ctx.op("pe", lambda E, f=f, dh=dh, ts=ts, base=base, hbo=hbo:
                           E.matmul(GB[:, base + 512 * dh:base + 512 * (dh + 1)],
                                    Hb[:, hbo + 512 * f + 128 * ts:
                                       hbo + 512 * f + 128 * (ts + 1)],
                                    W2v[:, f, 512 * dh:512 * (dh + 1)],
                                    start=(f == 0), stop=(f == 15)),
                           inc="pe" if (f == 15 and dh == 1) else None)
            state["r_d"] = ctx.cnt["pe"]
            ctx.wait("dve", "pe", state["r_d"])
            Xi = X[:, 1024 * it:1024 * (it + 1)]
            ocol = OH[:, 8 * it + ex:8 * it + ex + 1]
            c_dph[par] = ctx.op(
                "dve", lambda E, Xi=Xi, base=base, ocol=ocol:
                E.scalar_tensor_tensor(Xi, GB[:, base:base + 1024], ocol, Xi,
                                       ALU.mult, ALU.add), inc="dve")

    for se in range(16):
        ctx.wait("pe", "lw1", c_w1[se])
        ctx.wait("pe", "lw2", c_w2[se])
        if se == 0:
            ctx.wait("pe", "dve", c_pre)
        for tq in range(4):
            hbo = (tq % 2) * 8192
            for f in range(16):
                hbase = (f % 2) * 512
                ctx.wait("pe", "act", gelu_hist[-2])
                for c in range(NCH):
                    ctx.op("pe", lambda E, c=c, f=f, tq=tq, hbase=hbase:
                           E.matmul(GA[:, hbase:hbase + 512],
                                    W1v[:, c, 128 * f:128 * (f + 1)],
                                    XPv[:, c, 512 * tq:512 * (tq + 1)],
                                    start=(c == 0), stop=(c == NCH - 1)),
                           inc="pe" if c == NCH - 1 else None)
                r_h = ctx.cnt["pe"]
                ctx.wait("act", "pe", r_h)
                if f == 0:
                    ctx.wait("act", "pe", state["r_d"])
                bcol = 32 * (se // 2) + 16 * (se % 2) + f
                cg = ctx.op("act", lambda E, f=f, hbase=hbase, bcol=bcol, hbo=hbo:
                            E.activation(Hb[:, hbo + 512 * f:hbo + 512 * (f + 1)],
                                         GA[:, hbase:hbase + 512], AF.Gelu,
                                         bias=B1[:, bcol:bcol + 1]), inc="act")
                gelu_hist.append(cg)
            c_hb_done[tq % 2] = gelu_hist[-1]
            if tq == 3 and se + 1 < 16:
                ctx.wait("pool", "pe", ctx.cnt["pe"])
                c_w1[se + 1] = ctx.dma(
                    "pool", "lw1", W1v,
                    w1_src(se + 1).rearrange("(c p) h -> p c h", p=128))
            if tq >= 1:
                emit_down(se, tq - 1)
        emit_down(se, 3)
        if se + 1 < 16:
            ctx.wait("pool", "pe", ctx.cnt["pe"])
            c_w2[se + 1] = ctx.dma(
                "pool", "lw2", W2v,
                w2_src(se + 1).rearrange("(c p) h -> p c h", p=128))

    ctx.wait("pe", "ld_rw", ctx.cnt["ld_rw"])
    c_y = max(c_dph.values())
    for i in range(NT):
        ctx.wait("pe", "dve", c_y)
        for dh in range(2):
            r = ctx.op("pe", lambda E, i=i, dh=dh:
                       E.matmul(d_ps[:, 512 * dh:512 * (dh + 1)],
                                OHT[0:8, 128 * i:128 * (i + 1)],
                                B2[0:8, 512 * dh:512 * (dh + 1)],
                                start=True, stop=True),
                       inc="pe" if dh == 1 else None)
        ctx.wait("dve", "pe", r)
        c_y = ctx.op("dve", lambda E, i=i:
                     E.tensor_add(X[:, 1024 * i:1024 * (i + 1)],
                                  X[:, 1024 * i:1024 * (i + 1)],
                                  d_ps[:, 0:1024]), inc="dve")




def build_full():
    nc = bass.Bass("TRN2", target_bir_lowering=False)
    P = {}
    P["x"] = nc.dram_tensor("x", [S, D], F32, kind="ExternalInput").ap()
    P["idn"] = nc.dram_tensor("idn", [128, 128], BF16, kind="ExternalInput").ap()
    P["a0_wq"] = nc.dram_tensor("a0_wq", [8, D, 128], F32, kind="ExternalInput").ap()
    P["a0_wk"] = nc.dram_tensor("a0_wk", [8, D, 128], F32, kind="ExternalInput").ap()
    P["a0_wv"] = nc.dram_tensor("a0_wv", [8, D, 128], F32, kind="ExternalInput").ap()
    P["a0_wo"] = nc.dram_tensor("a0_wo", [D, D], F32, kind="ExternalInput").ap()
    P["a0_gb"] = nc.dram_tensor("a0_gb", [128, D], F32, kind="ExternalInput").ap()
    P["a0_bb"] = nc.dram_tensor("a0_bb", [128, D], F32, kind="ExternalInput").ap()
    P["a0_mask"] = nc.dram_tensor("a0_mask", [128, 4 * 512], BF16,
                                  kind="ExternalInput").ap()
    P["f0_rw"] = nc.dram_tensor("f0_rw", [D, 16], F32, kind="ExternalInput").ap()
    P["f0_gup"] = nc.dram_tensor("f0_gup", [16, D, 2048], F32,
                                 kind="ExternalInput").ap()
    P["f0_dwn"] = nc.dram_tensor("f0_dwn", [16, D, D], F32,
                                 kind="ExternalInput").ap()
    P["f0_shgu"] = nc.dram_tensor("f0_shgu", [2, D, 2048], F32,
                                  kind="ExternalInput").ap()
    P["f0_shdw"] = nc.dram_tensor("f0_shdw", [2, D, D], F32,
                                  kind="ExternalInput").ap()
    P["a1_w3"] = nc.dram_tensor("a1_w3", [128, 3 * 512], F32,
                                kind="ExternalInput").ap()
    P["a1_wo"] = nc.dram_tensor("a1_wo", [D, D], F32, kind="ExternalInput").ap()
    P["f1_rw"] = nc.dram_tensor("f1_rw", [D, 8], F32, kind="ExternalInput").ap()
    P["f1_br"] = nc.dram_tensor("f1_br", [128, 8], F32, kind="ExternalInput").ap()
    P["f1_w1g"] = nc.dram_tensor("f1_w1g", [4096, 8192], F32,
                                 kind="ExternalInput").ap()
    P["f1_w2g"] = nc.dram_tensor("f1_w2g", [4096, 8192], F32,
                                 kind="ExternalInput").ap()
    P["f1_b1g"] = nc.dram_tensor("f1_b1g", [1024, 32], F32,
                                 kind="ExternalInput").ap()
    P["f1_b2"] = nc.dram_tensor("f1_b2", [8, D], F32, kind="ExternalInput").ap()
    P["f1_tri"] = nc.dram_tensor("f1_tri", [128, 256], BF16,
                                 kind="ExternalInput").ap()
    P["f1_iq"] = nc.dram_tensor("f1_iq", [128, 8], F32,
                                kind="ExternalInput").ap()
    P["f1_w1b"] = nc.dram_tensor("f1_w1b", [4096, 8192], BF16,
                                 kind="Internal").ap()
    P["f1_w2b"] = nc.dram_tensor("f1_w2b", [4096, 8192], BF16,
                                 kind="Internal").ap()
    P["f1_xgscr"] = nc.dram_tensor("f1_xgscr", [3840, 1024], BF16,
                                   kind="Internal").ap()
    P["f1_yscr"] = nc.dram_tensor("f1_yscr", [3840, 1024], F32,
                                  kind="Internal").ap()
    yout = nc.dram_tensor("y", [S, D], F32, kind="ExternalOutput").ap()
    if FFN1_DEBUG:
        P["dbg_pos"] = nc.dram_tensor("dbg_pos", [128, 16], F32,
                                      kind="ExternalOutput").ap()
        P["dbg_be"] = nc.dram_tensor("dbg_be", [128, 16], F32,
                                     kind="ExternalOutput").ap()
        P["dbg_sst"] = nc.dram_tensor("dbg_sst", [128, 8], F32,
                                      kind="ExternalOutput").ap()
        P["dbg_cnt"] = nc.dram_tensor("dbg_cnt", [128, 8], F32,
                                      kind="ExternalOutput").ap()
        P["dbg_xg"] = nc.dram_tensor("dbg_xg", [3840, 1024], BF16,
                                     kind="ExternalOutput").ap()
        P["dbg_ys"] = nc.dram_tensor("dbg_ys", [3840, 1024], F32,
                                     kind="ExternalOutput").ap()

    with ExitStack() as stk:
        slab = nc.alloc_sbuf_tensor("slab", [128, SLAB], U8)
        base = nc.lookup_mloc(slab).addr
        X = nc.alloc_sbuf_tensor_at("X", [128, NT * D], F32, offset=base)
        IDN = nc.alloc_sbuf_tensor_at("IDN", [128, 128], BF16, offset=base + 65536)
        P["X"] = X
        P["IDN"] = IDN
        P["bigA"] = nc.alloc_psum_tensor("bigA", [128, 2048], F32)
        P["bigB"] = nc.alloc_psum_tensor("bigB", [128, 2048], F32)
        ptv = nc.virtual_psum_tensor("ptv", [128, 1024], BF16)
        nc._tensor(ptv.name, ptv.shape, ptv.dtype, type="PSUM")
        mloc = nc.lookup_mloc(ptv)
        mloc.allocated = True
        mloc.bank = nc.lookup_mloc(P["bigB"]).bank + 3
        mloc.addr = nc.lookup_mloc(P["bigB"]).addr
        P["ptv"] = ptv
        Xv = X[:, :].rearrange("p (i d) -> p i d", d=D)

        ctx = Ctx(nc)
        ctx.make_sems(stk, ["ld_x", "ld_c", "ld_wo", "ld_rw", "ld_w",
                            "lw0", "lw1", "lw2", "lgu", "ldw0", "ldw1", "st",
                            "pe", "dve", "act", "pool",
                            "wpp", "sg0", "sg1", "xg", "ysc", "ygta", "ygtb",
                            "lw1a", "lw1b", "lw2a", "lw2b", "lba", "lbb"])

        ctx.dma("sp", "ld_c", IDN[:, :], P["idn"])
        for e in range(8):
            ctx.dma("pool", "wpp", P["f1_w1b"][512 * e:512 * (e + 1), :],
                    P["f1_w1g"][512 * e:512 * (e + 1), :])
            ctx.dma("pool", "wpp", P["f1_w2b"][512 * e:512 * (e + 1), :],
                    P["f1_w2g"][512 * e:512 * (e + 1), :])

        emit_attn0(nc, ctx, P, Carver(nc, base, PERSIST, SLAB, "a0"))
        ctx.barrier()
        emit_ffn0(nc, ctx, P, Carver(nc, base, PERSIST, SLAB, "f0"))
        ctx.barrier()
        emit_attn1(nc, ctx, P, Carver(nc, base, PERSIST, SLAB, "a1"))
        ctx.barrier()
        emit_ffn1_sparse(nc, ctx, P, Carver(nc, base, PERSIST, SLAB, "f1"))

        ctx.wait("sp", "dve", ctx.cnt["dve"])
        ctx.store(yout.rearrange("(i p) d -> p i d", p=128), Xv)
        ctx.wait("sp", "st", ctx.cnt["st"])

        with nc.Block() as block:
            ctx.replay(block)
    return nc




def _pairify(w):
    w = np.asarray(w, np.float32)
    return np.ascontiguousarray(
        np.transpose(w, (1, 0, 2)).reshape(D, D).reshape(D, 8, 128)
        .transpose(1, 0, 2))


def _prep_all(inp):
    p = np.arange(128)[:, None]
    f = np.arange(512)[None, :]
    mask = np.stack([np.where(f <= p + 128 * k, 0.0, NEG) for k in range(4)],
                    axis=1).reshape(128, 4 * 512).astype(BF)
    d = {
        "idn": np.eye(128, dtype=BF),
        "a0_wq": _pairify(inp["L0_attn_Wq"]),
        "a0_wk": _pairify(inp["L0_attn_Wk"]),
        "a0_wv": _pairify(inp["L0_attn_Wv"]),
        "a0_wo": np.ascontiguousarray(np.asarray(inp["L0_attn_Wo"], np.float32)),
        "a0_gb": np.ascontiguousarray(np.broadcast_to(
            np.asarray(inp["L0_attn_gamma"], np.float32), (128, D))),
        "a0_bb": np.ascontiguousarray(np.broadcast_to(
            np.asarray(inp["L0_attn_beta"], np.float32), (128, D))),
        "a0_mask": mask,
    }
    sgw = np.asarray(inp["L0_ffn_shared_gate_w"], np.float32).T
    suw = np.asarray(inp["L0_ffn_shared_up_w"], np.float32).T
    sdw = np.asarray(inp["L0_ffn_shared_down_w"], np.float32).T
    shgu = np.stack([
        np.concatenate([sgw[:, 0:1024], suw[:, 0:1024]], axis=1),
        np.concatenate([sgw[:, 1024:2048], suw[:, 1024:2048]], axis=1)])
    d.update({
        "f0_rw": np.ascontiguousarray(
            np.asarray(inp["L0_ffn_router_w"], np.float32).T),
        "f0_gup": np.ascontiguousarray(np.asarray(inp["L0_ffn_gate_up_w"],
                                                  np.float32)),
        "f0_dwn": np.ascontiguousarray(np.asarray(inp["L0_ffn_down_w"],
                                                  np.float32)),
        "f0_shgu": np.ascontiguousarray(shgu),
        "f0_shdw": np.ascontiguousarray(np.stack([sdw[0:1024], sdw[1024:2048]])),
    })
    w3 = np.zeros((128, 3 * 512), np.float32)
    for m, nm in ((0, "L1_attn_Wq"), (1, "L1_attn_Wk"), (2, "L1_attn_Wv")):
        w = np.asarray(inp[nm], np.float32)
        for h in range(16):
            pbp = 64 * (h % 2)
            w3[pbp:pbp + 64, 512 * m + 64 * (h // 2):512 * m + 64 * (h // 2) + 64] = w[h]
    d.update({
        "a1_w3": w3,
        "a1_wo": np.ascontiguousarray(np.asarray(inp["L1_attn_Wo"], np.float32)),
    })
    w1 = np.asarray(inp["L1_ffn_w1"], np.float32)
    w2 = np.asarray(inp["L1_ffn_w2"], np.float32)
    b1 = np.asarray(inp["L1_ffn_b1"], np.float32)
    w1g = w1.reshape(8, 8, 128, 4, 1024).transpose(0, 3, 2, 1, 4) \
        .reshape(4096, 8192)
    w2g = w2.reshape(8, 4, 8, 128, 1024).transpose(0, 1, 3, 2, 4) \
        .reshape(4096, 8192)
    b1g = b1.reshape(8, 32, 128).transpose(0, 2, 1).reshape(1024, 32)
    ar = np.arange(128, dtype=np.float32)
    iq = np.zeros((128, 8), np.float32)
    for q in range(4):
        iq[:, q] = 128.0 * q + ar
    iq[:, 4] = ar
    tri = np.zeros((128, 256), BF)
    tri[:, 0:128] = (np.arange(128)[:, None] < np.arange(128)[None, :]).astype(BF)
    tri[:, 128:256] = np.ones((128, 128), BF)
    d.update({
        "f1_rw": np.ascontiguousarray(np.asarray(inp["L1_ffn_wr"], np.float32)),
        "f1_br": np.ascontiguousarray(np.broadcast_to(
            np.asarray(inp["L1_ffn_br"], np.float32), (128, 8))),
        "f1_w1g": np.ascontiguousarray(w1g),
        "f1_w2g": np.ascontiguousarray(w2g),
        "f1_b1g": np.ascontiguousarray(b1g),
        "f1_b2": np.ascontiguousarray(np.asarray(inp["L1_ffn_b2"], np.float32)),
        "f1_tri": tri,
        "f1_iq": iq,
    })
    return d


def kernel(**inp):
    x = np.asarray(inp["L0_attn_x"], dtype=np.float32)
    nc = build_full()
    shared = _prep_all(inp)
    in_maps = [dict(shared, x=np.ascontiguousarray(x[b])) for b in range(8)]
    res = run_bass_kernel_spmd(nc, in_maps, core_ids=list(range(8)))
    return np.ascontiguousarray(
        np.stack([res.results[b]["y"] for b in range(8)], axis=0))
